# version 14
# baseline (speedup 1.0000x reference)
import numpy as np
import ml_dtypes
from contextlib import ExitStack

import concourse.bass as bass
import concourse.mybir as mybir
from concourse.bass_utils import run_bass_kernel_spmd
from concourse.alu_op_type import AluOpType

F32 = mybir.dt.float32
BF16 = mybir.dt.bfloat16
AF = mybir.ActivationFunctionType
AX = mybir.AxisListType
OP = AluOpType

ENGINES = ("pe", "dve", "act", "pool", "sp")
NLANES = 8
QLANES = {q: tuple(f"q{q}{i}" for i in range(NLANES)) for q in ("sp", "act", "pool")}
SEMS = ("pe", "dve", "act", "pool", "cc") + QLANES["sp"] + QLANES["act"] + QLANES["pool"]


class Buf:
    __slots__ = ("name", "w", "r")

    def __init__(self, name, prior=()):
        self.name = name
        self.w = {}
        self.r = {}
        for b in prior:
            for k, v in b.dep_write().items():
                if self.w.get(k, 0) < v:
                    self.w[k] = v

    def dep_read(self):
        return dict(self.w)

    def dep_write(self):
        d = dict(self.w)
        for k, v in self.r.items():
            if d.get(k, 0) < v:
                d[k] = v
        return d

    def note_w(self, key, val):
        if self.w.get(key, 0) < val:
            self.w[key] = val

    def note_r(self, key, val):
        if self.r.get(key, 0) < val:
            self.r[key] = val


def _merge(dst, src):
    for k, v in src.items():
        if dst.get(k, 0) < v:
            dst[k] = v


class Prog:
    def __init__(self, nc):
        self.nc = nc
        self.ops = {e: [] for e in ENGINES}
        self.cnt = {k: 0 for k in SEMS}
        self.waited = {e: {} for e in ENGINES}
        self.lane_rr = {q: 0 for q in QLANES}

    def _deps(self, reads, writes, extra):
        d = {}
        for b in reads:
            _merge(d, b.dep_read())
        for b in writes:
            _merge(d, b.dep_write())
        for ev in extra:
            if ev is not None:
                _merge(d, {ev[0]: ev[1]})
        return d

    def _prune(self, eng, deps, self_sync=True):
        out = {}
        wt = self.waited[eng]
        for k, v in deps.items():
            if not self_sync and k == eng:
                continue
            if k == eng and eng == "pe":
                continue
            if wt.get(k, 0) >= v:
                continue
            out[k] = v
        for k, v in out.items():
            wt[k] = v
        return out

    def _rec(self, eng, waits, fn, inc_key, inc_amt):
        self.ops[eng].append((waits, fn, inc_key, inc_amt))

    def comp(self, eng, fn, reads=(), writes=(), extra=()):
        deps = self._deps(reads, writes, extra)
        waits = self._prune(eng, deps, True)
        self.cnt[eng] += 1
        ev = (eng, self.cnt[eng])
        self._rec(eng, waits, fn, eng, 1)
        for b in writes:
            b.note_w(eng, ev[1])
        for b in reads:
            b.note_r(eng, ev[1])
        return ev

    def dma(self, queue, out_ap, in_ap, reads=(), writes=(), extra=(), **kw):
        r = self.lane_rr[queue]
        self.lane_rr[queue] = (r + 1) % NLANES
        qk = QLANES[queue][r]
        deps = self._deps(reads, writes, extra)
        _merge(deps, {qk: self.cnt[qk]})
        waits = self._prune(queue, deps, True)
        self.cnt[qk] += 16
        val = self.cnt[qk]

        def fn(eng, _o=out_ap, _i=in_ap, _kw=dict(kw)):
            return eng.dma_start(out=_o, in_=_i, **_kw)

        self._rec(queue, waits, fn, qk, 16)
        for b in writes:
            b.note_w(qk, val)
        for b in reads:
            b.note_r(qk, val)
        return (qk, val)

    def mm_group(self, psum_buf, mms):
        n = len(mms)
        ev = None
        for i, (fn, reads) in enumerate(mms):
            deps = self._deps(reads, (psum_buf,) if i == 0 else (), ())
            waits = self._prune("pe", deps, True)
            if i == n - 1:
                self.cnt["pe"] += 1
                ev = ("pe", self.cnt["pe"])
                self._rec("pe", waits, fn, "pe", 1)
                psum_buf.note_w("pe", ev[1])
                for _f2, reads2 in mms:
                    for b in reads2:
                        b.note_r("pe", ev[1])
            else:
                self._rec("pe", waits, fn, None, 0)
        return ev

    def wait_only(self, eng, extra):
        deps = {}
        for e in extra:
            _merge(deps, {e[0]: e[1]})
        waits = self._prune(eng, deps, True)
        if waits:
            self._rec(eng, waits, None, None, 0)

    def drain_queues(self, eng="sp"):
        evs = [(lk, self.cnt[lk]) for lanes in QLANES.values() for lk in lanes
               if self.cnt[lk] > 0]
        self.wait_only(eng, evs)

    def verify(self):
        idx = {e: 0 for e in ENGINES}
        sem = {k: 0 for k in SEMS}
        total = sum(len(v) for v in self.ops.values())
        done = 0
        progressed = True
        while progressed:
            progressed = False
            for e in ENGINES:
                while idx[e] < len(self.ops[e]):
                    waits, fn, ik, ia = self.ops[e][idx[e]]
                    if any(sem[k] < v for k, v in waits.items()):
                        break
                    if ik is not None:
                        sem[ik] += ia
                    idx[e] += 1
                    done += 1
                    progressed = True
        if done != total:
            lines = []
            for e in ENGINES:
                if idx[e] < len(self.ops[e]):
                    waits = self.ops[e][idx[e]][0]
                    missing = {k: (v, sem[k]) for k, v in waits.items() if sem[k] < v}
                    lines.append(f"  {e} stuck at {idx[e]}/{len(self.ops[e])} on {missing}")
            raise RuntimeError("DEADLOCK in schedule:\n" + "\n".join(lines))

    def replay(self):
        nc = self.nc
        with ExitStack() as st:
            sems = {k: st.enter_context(nc.semaphore("sem_" + k)) for k in SEMS}
            block = st.enter_context(nc.Block())

            def mk(engname):
                def run(eng):
                    for waits, fn, ik, ia in self.ops[engname]:
                        for k in sorted(waits):
                            eng.wait_ge(sems[k], waits[k])
                        if fn is not None:
                            ins = fn(eng)
                            if ik is not None:
                                ins.then_inc(sems[ik], ia)
                return run

            block.tensor(mk("pe"))
            block.vector(mk("dve"))
            block.scalar(mk("act"))
            block.gpsimd(mk("pool"))
            block.sync(mk("sp"))


class Flex:

    def __init__(self):
        self.reg = {}

    def view(self, arena_handle, arena_name, byte0, nbytes, dtype, tag):
        es = 4 if dtype == F32 else 2
        assert byte0 % es == 0 and nbytes % es == 0
        ap = arena_handle[:, :]
        if dtype != ap.dtype:
            ap = ap.bitcast(dtype)
        ap = ap[:, byte0 // es:(byte0 + nbytes) // es]
        lst = self.reg.setdefault(arena_name, [])
        b1 = byte0 + nbytes
        priors = [bu for (a0, a1, bu) in lst if a0 < b1 and byte0 < a1]
        buf = Buf(tag, priors)
        self.reg[arena_name] = [(a0, a1, bu) for (a0, a1, bu) in lst
                                if not (a0 < b1 and byte0 < a1)]
        self.reg[arena_name].append((byte0, b1, buf))
        return ap, buf


T = 2048
D = 2048
NT = 16
NH = 16
NB = 8
NE = 32
LAT = 1024
SH = 4096
EPS_LN = np.float32(1e-5)
EPS_RMS = np.float32(1e-6)
BIG = 1.0e6
BIGR = 1.0e9


def build_program(stage_limit=99, taps=()):
    nc = bass.Bass("TRN2", target_bir_lowering=False, debug=False)
    p = Prog(nc)
    taps = set(taps)

    def di(name, shape, dt=F32):
        return nc.dram_tensor(name, shape, dt, kind="ExternalInput").ap()

    x_in = di("x", [T, D])
    wqkvT = di("wqkvT", [D, 3 * D])
    wgT = di("wgT", [D, D])
    woT = di("woT", [D, D])
    fc1T = di("fc1T", [D, LAT])
    fc2T = di("fc2T", [LAT, D])
    supT = di("supT", [D, SH])
    sdnT = di("sdnT", [SH, D])
    w1T = di("w1T", [NE, LAT, LAT])
    w2T = di("w2T", [NE, LAT, LAT])
    gwT = di("gwT", [D, NE])
    sl_row = di("sl_row", [1, NH])
    gb_row = di("gb_row", [1, NE])
    nw_pc = di("nw_pc", [128, 16])
    dist_in = di("dist", [1, 8])
    arb_in = di("arb", [128, 256])
    kdm_in = di("kdm", [128, 2])
    mtt_in = di("mtt", [128, 512])
    ident_in = di("ident", [128, 128], BF16)
    ones_bf_in = di("ones_bf", [128, 128], BF16)
    onesf_in = di("onesf", [1, 128])

    y_out = nc.dram_tensor("y", [T, D], F32, kind="ExternalOutput").ap()

    def ds(name, shape, dt):
        return nc.dram_tensor(name, shape, dt, kind="Internal").ap()

    q_fm_s = ds("q_fm_s", [NB, 128, NH * 256], BF16)
    k_fm_s = ds("k_fm_s", [NB, 128, NH * 256], BF16)
    k_tm_s = ds("k_tm_s", [NT, 128, D], BF16)
    v_tm_s = ds("v_tm_s", [NT, 128, D], BF16)
    g_s = ds("g_s", [NH, 128, T], BF16)
    attn_s = ds("attn_s", [NT, 128, NH * 128], BF16)
    S_loc = ds("S_loc", [NH * 128, 128], F32)
    S_gth = ds("S_gth", [8 * NH * 128, 128], F32)
    x1_s = ds("x1_s", [T, D], F32)
    mid_s = ds("mid_s", [32, 128, T], BF16)
    rout_s = ds("rout_s", [NT, 128, D], BF16)

    def sb(name, shape, dt):
        return nc.alloc_sbuf_tensor(name, shape, dt)

    ident = sb("identS", [128, 128], BF16)
    ones_bf = sb("ones_bfS", [128, 128], BF16)
    onesf = sb("onesfS", [1, 128], F32)
    arb = sb("arbS", [128, 256], F32)
    kdm = sb("kdmS", [128, 2], F32)
    mtt = sb("mttS", [128, 512], F32)
    slr = sb("slrS", [1, NH], F32)
    gbr = sb("gbrS", [1, NE], F32)
    dsr = sb("dsrS", [1, 8], F32)
    nwc = sb("nwcS", [128, 16], F32)
    s_bc = sb("s_bc", [128, NH], F32)
    negs = sb("negs", [128, NH], F32)
    lam = sb("lam", [128, NH], F32)
    distb = sb("distb", [128, 8], F32)
    wkv = sb("wkv", [128, 8 * NH], F32)
    gb_bc = sb("gb_bc", [128, NE], F32)
    gb_rep = sb("gb_rep", [128, 512], F32)
    kdc = sb("kdc", [128, NH * 2], F32)
    cw = sb("cw", [128, 512], F32)
    rstdc = sb("rstdc", [128, 16], F32)
    gw_sb = sb("gw_sbS", [128, 512], BF16)
    colA = sb("colA", [128, 8], F32)
    colB = sb("colB", [128, 8], F32)
    colC = sb("colC", [128, 8], F32)
    colD = sb("colD", [128, 8], F32)
    colE = sb("colE", [128, 8], F32)

    BIGA = sb("BIGA", [128, 32768], BF16)
    BIGW = sb("BIGW", [128, 32768], BF16)
    FLXT = sb("FLX", [128, 30720], BF16)

    PB = [nc.alloc_psum_tensor(f"pb{i}", [128, 512], F32) for i in range(7)]
    PTR = nc.alloc_psum_tensor("ptrT", [128, 1024], BF16)

    flex = Flex()

    def fA(byte0, nbytes, dtype, tag):
        return flex.view(BIGA, "A", byte0, nbytes, dtype, tag)

    def fW(byte0, nbytes, dtype, tag):
        return flex.view(BIGW, "W", byte0, nbytes, dtype, tag)

    def fX(byte0, nbytes, dtype, tag):
        return flex.view(FLXT, "X", byte0, nbytes, dtype, tag)

    bufs = {}

    def B(name):
        if name not in bufs:
            bufs[name] = Buf(name)
        return bufs[name]

    pbb = [B(f"pb{i}") for i in range(7)]
    ptrb = B("ptrT")

    dtt, dttb = flex.view(FLXT, "X", 36864, NH * 1024, BF16, "dtt")
    qdb, qdbb = flex.view(FLXT, "X", 53248, NH * 512, BF16, "qdb")

    def tap(name, src_ap, shape, dt=F32, reads=()):
        if name in taps:
            o = nc.dram_tensor("tap_" + name, shape, dt, kind="ExternalOutput").ap()
            p.dma("sp", o, src_ap, reads=reads)

    _statrot = [0]

    def stat_cols():
        i = _statrot[0] % 8
        _statrot[0] += 1
        return (colA[:, i:i + 1], colB[:, i:i + 1], colC[:, i:i + 1], colD[:, i:i + 1],
                (B(f"cA{i}"), B(f"cB{i}"), B(f"cC{i}"), B(f"cD{i}")))

    def ln_stats(x_ap, xbuf_b, sq_scr, sq_b, eps, eps_extra=None):
        n = x_ap.shape[1]
        c_ssq, c_sum, c_mean, c_r, (b_ssq, b_sum, b_mean, b_r) = stat_cols()
        p.comp("act", lambda e: e.activation(sq_scr, x_ap, AF.Square, accum_out=c_ssq),
               reads=[xbuf_b], writes=[sq_b, b_ssq])
        p.comp("dve", lambda e: e.tensor_reduce(
            c_sum.rearrange("p (a o) -> p a o", a=1, o=1),
            x_ap.rearrange("p (a b) -> p a b", a=1, b=n), AX.X, OP.add),
            reads=[xbuf_b], writes=[b_sum])
        p.comp("dve", lambda e: e.tensor_scalar_mul(c_mean, c_sum, 1.0 / n),
               reads=[b_sum], writes=[b_mean])
        p.comp("dve", lambda e: e.scalar_tensor_tensor(c_sum, c_mean, -1.0, c_mean,
                                                       OP.mult, OP.mult),
               reads=[b_mean], writes=[b_sum])
        p.comp("dve", lambda e: e.scalar_tensor_tensor(c_ssq, c_ssq, 1.0 / n, c_sum,
                                                       OP.mult, OP.add),
               reads=[b_ssq, b_sum], writes=[b_ssq])
        if eps_extra is not None:
            ea, eb = eps_extra
            p.comp("dve", lambda e: e.tensor_tensor(c_ssq, c_ssq, ea, OP.add),
                   reads=[b_ssq, eb], writes=[b_ssq])
            eps = np.float32(0.0)
        p.comp("act", lambda e: e.activation(c_r, c_ssq, AF.Ln, bias=eps),
               reads=[b_ssq], writes=[b_r])
        p.comp("act", lambda e: e.activation(c_r, c_r, AF.Exp, scale=-0.5),
               reads=[b_r], writes=[b_r])
        return c_mean, c_r, b_mean, b_r

    for t_, src, bn in [(ident, ident_in, "ident"), (ones_bf, ones_bf_in, "ones"),
                        (onesf, onesf_in, "onesf"), (arb, arb_in, "arb"),
                        (kdm, kdm_in, "kdm"), (mtt, mtt_in, "mtt"),
                        (slr, sl_row, "slr"), (gbr, gb_row, "gbr"),
                        (dsr, dist_in, "dsr"), (nwc, nw_pc, "nwc")]:
        p.dma("sp", t_[:, :], src, writes=[B(bn)])
    p.dma("pool", gw_sb[:, :].rearrange("p (f e) -> p f e", f=16),
          gwT.rearrange("(f p) e -> p f e", p=128), writes=[B("gw_sb")])

    p.mm_group(pbb[0], [
        (lambda e: e.matmul(PB[0][:, 0:NH], onesf[0:1, :], slr[0:1, :],
                            start=True, stop=True), [B("onesf"), B("slr")]),
        (lambda e: e.matmul(PB[0][:, 32:32 + NE], onesf[0:1, :], gbr[0:1, :],
                            start=True, stop=True), [B("onesf"), B("gbr")]),
        (lambda e: e.matmul(PB[0][:, 64:72], onesf[0:1, :], dsr[0:1, :],
                            start=True, stop=True), [B("onesf"), B("dsr")]),
    ])
    p.comp("dve", lambda e: e.tensor_copy(s_bc[:, :], PB[0][:, 0:NH]),
           reads=[pbb[0]], writes=[B("s_bc")])
    p.comp("dve", lambda e: e.tensor_copy(gb_bc[:, :], PB[0][:, 32:32 + NE]),
           reads=[pbb[0]], writes=[B("gb_bc")])
    p.comp("dve", lambda e: e.tensor_scalar_mul(distb[:, :], PB[0][:, 64:72], -2048.0),
           reads=[pbb[0]], writes=[B("distb")])
    p.comp("dve", lambda e: e.tensor_scalar_mul(negs[:, :], s_bc[:, :], -1.0),
           reads=[B("s_bc")], writes=[B("negs")])
    p.comp("act", lambda e: e.activation(lam[:, :], s_bc[:, :], AF.Exp, scale=-256.0),
           reads=[B("s_bc")], writes=[B("lam")])
    for i in range(8):
        p.comp("act", (lambda ii: lambda e: e.activation(
            wkv[:, ii * NH:(ii + 1) * NH], s_bc[:, :], AF.Exp,
            scale=distb[:, ii:ii + 1]))(i),
            reads=[B("s_bc"), B("distb")], writes=[B("wkv")])
    for j in range(16):
        p.comp("dve", (lambda jj: lambda e: e.tensor_copy(
            gb_rep[:, jj * NE:(jj + 1) * NE], gb_bc[:, :]))(j),
            reads=[B("gb_bc")], writes=[B("gb_rep")])
    for h in range(NH):
        p.comp("act", (lambda hh: lambda e: e.activation(
            dtt[:, hh * 512:(hh + 1) * 512], mtt[:, :], AF.Exp,
            scale=negs[:, hh:hh + 1]))(h),
            reads=[B("mtt"), B("negs")], writes=[dttb])
        p.comp("act", (lambda hh: lambda e: e.activation(
            qdb[:, hh * 256:(hh + 1) * 256], arb[:, :], AF.Exp,
            scale=negs[:, hh:hh + 1]))(h),
            reads=[B("arb"), B("negs")], writes=[qdbb])
        p.comp("act", (lambda hh: lambda e: e.activation(
            kdc[:, hh * 2:(hh + 1) * 2], kdm[:, :], AF.Exp,
            scale=negs[:, hh:hh + 1]))(h),
            reads=[B("kdm"), B("negs")], writes=[B("kdc")])
    tap("wkv", wkv[:, :], [128, 128], F32, reads=[B("wkv")])

    def finish():
        if stage_limit < 10:
            sink = sb("sinkS", [1, 64], F32)
            skb = B("sink")
            for ap in [x_in[0:1, 0:64], wqkvT[0:1, 0:64], wgT[0:1, 0:64],
                       woT[0:1, 0:64], fc1T[0:1, 0:64], fc2T[0:1, 0:64],
                       supT[0:1, 0:64], sdnT[0:1, 0:64], w1T[0, 0:1, 0:64],
                       w2T[0, 0:1, 0:64]]:
                p.dma("sp", sink[:, :], ap, reads=[skb], writes=[skb])
        p.drain_queues("sp")
        p.verify()
        p.replay()
        return nc

    if stage_limit < 1:
        return finish()

    lnA, lnAb = fA(0, 65536, BF16, "ln1fm")
    lnfm = lnA.rearrange("p (f t) -> p f t", f=16)
    XT0, XT0b = fX(0, 8192, F32, "xt0")
    XT1, XT1b = fX(8192, 8192, F32, "xt1")
    SQS, SQSb = fX(16384, 8192, F32, "sqs")
    LNT, LNTb = fX(24576, 4096, BF16, "lnt")
    XTl = [(XT0, XT0b), (XT1, XT1b)]

    def transpose_tile(src_ap, src_b, dst_view, dst_b, m, also_w=None):
        for half in range(2):
            mms = []
            for c in range(8):
                cc = half * 8 + c
                mms.append((
                    (lambda _c=cc, _o=c: lambda e: e.transpose(
                        PTR[:, _o * 128:(_o + 1) * 128],
                        src_ap[:, _c * 128:(_c + 1) * 128], ident[:, :]))(),
                    [src_b, B("ident")]))
            p.mm_group(ptrb, mms)
            ev = p.comp("dve", (lambda _m=m, _h=half: lambda e: e.tensor_copy(
                dst_view[:, _h * 8:(_h + 1) * 8, _m * 128:(_m + 1) * 128],
                PTR[:, :].rearrange("p (f t) -> p f t", f=8)))(),
                reads=[ptrb], writes=[dst_b])
            if also_w is not None:
                also_w.note_w(*ev)

    lnCh = [Buf(f"lnch{m}") for m in range(NT)]
    for m in range(NT):
        xb, xbb = XTl[m % 2]
        p.dma("sp", xb, x_in[m * 128:(m + 1) * 128, :], writes=[xbb])
        cm, cr, bm, br = ln_stats(xb, xbb, SQS, SQSb, EPS_LN)
        p.comp("dve", (lambda _x=xb, _m=cm, _r=cr: lambda e: e.tensor_scalar(
            LNT, _x, _m, _r, OP.subtract, OP.mult))(),
            reads=[xbb, bm, br], writes=[LNTb])
        transpose_tile(LNT, LNTb, lnfm, lnCh[m], m, also_w=lnAb)
    tap("ln1fm", lnA[:, 0:2048], [128, 2048], BF16, reads=[lnAb])

    if stage_limit < 2:
        return finish()

    wslice = [None, None, None, None]

    def load_wslice(src_ap, tag, s):
        wap, _junk = fW(0, 65536, BF16, tag + f"_s{s}") if False else (BIGW, None)
        prior = [wslice[s]] if wslice[s] is not None else []
        gb_ = Buf(tag + f"_s{s}", prior)
        wv_ = BIGW[:, :].rearrange("p (f n) -> p f n", f=16)
        p.dma("pool", wv_[:, :, 512 * s:512 * (s + 1)],
              src_ap[:, 512 * s:512 * (s + 1)].rearrange("(f p) n -> p f n", p=128),
              writes=[gb_])
        wslice[s] = gb_
        return gb_

    def load_wfull(src_ap, tag):
        wv_ = BIGW[:, :].rearrange("p (f n) -> p f n", f=16)
        sbufs = [load_wslice(src_ap, tag, s) for s in range(4)]
        return wv_, (None, sbufs)

    def wslice_into_flex(owner=None):
        if owner is None:
            owner = Buf("wslices_owner", [b for b in wslice if b is not None])
        flex.reg.setdefault("W", []).append((0, 65536, owner))

    def tm_pass(wv_, wbt_, dst_s, dst_db, chbufs, next_load=None):
        _junk, wgb_ = wbt_
        for s in range(4):
            for m in range(NT):
                vc, vcb = chbufs[m % 2]
                bank = (2 * (m % 2) + (s % 2)) % 4
                mms = []
                for f in range(16):
                    mms.append((
                        (lambda _f=f, _m=m, _s=s, _bk=bank: lambda e: e.matmul(
                            PB[_bk][:, 0:512],
                            lnfm[:, _f, _m * 128:(_m + 1) * 128],
                            wv_[:, _f, _s * 512:(_s + 1) * 512],
                            start=(_f == 0), stop=(_f == 15)))(),
                        [lnCh[m], wgb_[s]]))
                p.mm_group(pbb[bank], mms)
                p.comp("act", (lambda _s=s, _bk=bank, _v=vc: lambda e: e.activation(
                    _v[:, 0:512], PB[_bk][:, 0:512], AF.Silu))(),
                    reads=[pbb[bank]], writes=[vcb])
                p.dma("sp", dst_s[m, :, 512 * s:512 * (s + 1)], vc[:, 0:512],
                      reads=[vcb], writes=[dst_db])
            if next_load is not None:
                next_load(s)

    VCH0, VCH0b = fX(0, 2048, BF16, "vch0")
    VCH1, VCH1b = fX(2048, 2048, BF16, "vch1")
    wv, wvb = load_wfull(wqkvT[:, 2 * D:3 * D], "w_v")
    tm_pass(wv, wvb, v_tm_s, B("v_tm_s"), [(VCH0, VCH0b), (VCH1, VCH1b)],
            next_load=lambda s: load_wslice(wqkvT[:, D:2 * D], "w_k", s))

    wk = BIGW[:, :].rearrange("p (f n) -> p f n", f=16)
    wkb = (None, [wslice[s] for s in range(4)])
    KCH0, KCH0b = fX(0, 2048, BF16, "kch0")
    KCH1, KCH1b = fX(2048, 2048, BF16, "kch1")
    tm_pass(wk, wkb, k_tm_s, B("k_tm_s"), [(KCH0, KCH0b), (KCH1, KCH1b)],
            next_load=lambda s: load_wslice(wqkvT[:, D:2 * D], "w_k2", s))

    KRA = [fX(8192, 8192, BF16, "kra0")]
    VRA = [fX(16384, 8192, BF16, "vra0")]
    SALL, SALLb = fX(24576, 8192, F32, "sall")
    p.comp("dve", lambda e: e.memset(SALL, 0.0), writes=[SALLb])

    def passA_load(i):
        kra, krab = KRA[0]
        vra, vrab = VRA[0]
        p.dma("pool", kra.rearrange("p (c n) -> p c n", c=2),
              k_tm_s.rearrange("m p n -> p m n")[:, 2 * i:2 * i + 2, :],
              reads=[B("k_tm_s")], writes=[krab])
        p.dma("pool", vra.rearrange("p (c n) -> p c n", c=2),
              v_tm_s.rearrange("m p n -> p m n")[:, 2 * i:2 * i + 2, :],
              reads=[B("v_tm_s")], writes=[vrab])

    def passA_block(i):
        kra, krab = KRA[0]
        vra, vrab = VRA[0]
        for cc_ in range(2):
            for h in range(NH):
                p.comp("dve", (lambda _h=h, _c=cc_, _v=vra:
                               lambda e: e.tensor_scalar_mul(
                    _v[:, _c * 2048 + _h * 128:_c * 2048 + (_h + 1) * 128],
                    _v[:, _c * 2048 + _h * 128:_c * 2048 + (_h + 1) * 128],
                    kdc[:, 2 * _h + _c:2 * _h + _c + 1]))(),
                    reads=[vrab, B("kdc")], writes=[vrab])
        for h in range(NH):
            bank = 4 + (h % 2)
            mms = []
            for cc_ in range(2):
                mms.append((
                    (lambda _h=h, _c=cc_, _k=kra, _V=vra, _bk=bank:
                     lambda e: e.matmul(
                        PB[_bk][:, 0:128],
                        _k[:, _c * 2048 + _h * 128:_c * 2048 + (_h + 1) * 128],
                        _V[:, _c * 2048 + _h * 128:_c * 2048 + (_h + 1) * 128],
                        start=(_c == 0), stop=(_c == 1)))(),
                    [krab, vrab]))
            p.mm_group(pbb[bank], mms)
            p.comp("dve", (lambda _h=h, _bk=bank: lambda e: e.scalar_tensor_tensor(
                SALL[:, _h * 128:(_h + 1) * 128], SALL[:, _h * 128:(_h + 1) * 128],
                lam[:, _h:_h + 1], PB[_bk][:, 0:128], OP.mult, OP.add))(),
                reads=[SALLb, B("lam"), pbb[bank]], writes=[SALLb])

    STG0, STG0b = fX(0, 4096, BF16, "stg0")
    STG1, STG1b = fX(4096, 4096, BF16, "stg1")
    STG = [(STG0, STG0b), (STG1, STG1b)]

    def fm_pass(wv_, wbt_, dst_s, dst_db, actf, next_load=None, on_head_pre=None):
        _junk, wgb_ = wbt_
        for h in range(NH):
            if on_head_pre is not None:
                on_head_pre(h)
            st, stb = STG[h % 2]
            for s in range(4):
                bank = (2 * (h % 2) + (s % 2)) % 4
                mms = []
                for f in range(16):
                    mms.append((
                        (lambda _f=f, _h=h, _s=s, _bk=bank: lambda e: e.matmul(
                            PB[_bk][:, 0:512],
                            wv_[:, _f, _h * 128:(_h + 1) * 128],
                            lnfm[:, _f, _s * 512:(_s + 1) * 512],
                            start=(_f == 0), stop=(_f == 15)))(),
                        [lnCh[4 * s], lnCh[4 * s + 1], lnCh[4 * s + 2],
                         lnCh[4 * s + 3], wgb_[h // 4]]))
                p.mm_group(pbb[bank], mms)
                p.comp("act", (lambda _s=s, _bk=bank, _t=st, _a=actf: lambda e: e.activation(
                    _t[:, _s * 512:(_s + 1) * 512], PB[_bk][:, 0:512], _a))(),
                    reads=[pbb[bank]], writes=[stb])
            if dst_s.shape[0] == NB:
                p.dma("sp", dst_s[:, :, h * 256:(h + 1) * 256].rearrange("b p t -> p b t"),
                      st.rearrange("p (b t) -> p b t", b=NB),
                      reads=[stb], writes=[dst_db])
            else:
                p.dma("sp", dst_s[h, :, :], st, reads=[stb], writes=[dst_db])
            if next_load is not None and h % 4 == 3:
                next_load(h // 4)


    wk2 = BIGW[:, :].rearrange("p (f n) -> p f n", f=16)
    wk2b = (None, [wslice[s] for s in range(4)])
    def kfm_head_hook(h):
        if h % 2 == 0:
            i = h // 2
            passA_load(i)
            passA_block(i)

    fm_pass(wk2, wk2b, k_fm_s, B("k_fm_s"), AF.Silu,
            next_load=lambda s: load_wslice(wqkvT[:, 0:D], "w_q", s),
            on_head_pre=kfm_head_hook)

    p.dma("sp", S_loc.rearrange("(h d) e -> d h e", d=128),
          SALL.rearrange("p (h e) -> p h e", h=NH), reads=[SALLb], writes=[B("S_loc")])
    p.comp("pool", lambda e: e.collective_compute(
        "AllGather", OP.bypass, replica_groups=[list(range(8))],
        ins=[S_loc.opt()], outs=[S_gth.opt()]),
        reads=[B("S_loc")], writes=[B("S_gth")])
    tap("sall", SALL, [128, 2048], F32, reads=[SALLb])

    if stage_limit < 3:
        return finish()

    wq = BIGW[:, :].rearrange("p (f n) -> p f n", f=16)
    wqb = (None, [wslice[s] for s in range(4)])
    fm_pass(wq, wqb, q_fm_s, B("q_fm_s"), AF.Silu,
            next_load=lambda s: load_wslice(wgT, "w_g", s))
    wg = BIGW[:, :].rearrange("p (f n) -> p f n", f=16)
    wgb = (None, [wslice[s] for s in range(4)])
    fm_pass(wg, wgb, g_s, B("g_s"), AF.Sigmoid)

    wo = BIGW[:, :].rearrange("p (f n) -> p f n", f=16)
    for s in range(4):
        load_wslice(woT, "w_o", s)
    wob = Buf("w_o_all", [wslice[s] for s in range(4)])
    for f in range(16):
        p.comp("dve", (lambda _f=f: lambda e: e.tensor_scalar_mul(
            wo[:, _f, :], wo[:, _f, :], nwc[:, _f:_f + 1]))(),
            reads=[wob, B("nwc")], writes=[wob])
    wslice_into_flex(owner=wob)
    tap("gfm", g_s[0, :, :], [128, T], BF16, reads=[B("g_s")])

    if stage_limit < 4:
        return finish()

    KVST, KVSTb = fX(8192, 8192, F32, "kvst")
    KVG, KVGb = fX(16384, 4096, F32, "kvg")
    sg4 = S_gth.rearrange("(r c p) e -> p r c e", r=8, c=16)
    for h in range(NH):
        p.dma("sp", KVG.rearrange("p (r e) -> p r e", r=8), sg4[:, :, h, :],
              reads=[B("S_gth")], writes=[KVGb])
        p.comp("dve", (lambda _h=h: lambda e: e.tensor_scalar_mul(
            KVST[:, _h * 128:(_h + 1) * 128], KVG[:, 0:128],
            wkv[:, _h:_h + 1]))(),
            reads=[KVGb, B("wkv")], writes=[KVSTb])
        for i in range(1, 8):
            p.comp("dve", (lambda _h=h, _i=i: lambda e: e.scalar_tensor_tensor(
                KVST[:, _h * 128:(_h + 1) * 128], KVG[:, _i * 128:(_i + 1) * 128],
                wkv[:, _i * NH + _h:_i * NH + _h + 1],
                KVST[:, _h * 128:(_h + 1) * 128], OP.mult, OP.add))(),
                reads=[KVGb, B("wkv"), KVSTb], writes=[KVSTb])
    tap("kv0", KVST, [128, 2048], F32, reads=[KVSTb])

    ROWS, ROWSb = fX(24576, 8192, F32, "rows")
    ssrow = ROWS[0:1, :]
    rsrow = ROWS[32:33, :]
    RSBF, RSBFb = fX(32768, 4096, BF16, "rsbf")
    rsbf = RSBF[0:1, :]
    QBs = [fA(0, 8192, BF16, "qb0"), fA(8192, 8192, BF16, "qb1")]
    KBs = [fA(16384, 8192, BF16, "kb0"), fA(24576, 8192, BF16, "kb1")]
    KTBs = [fA(32768, 8192, BF16, "ktb0"), fA(40960, 8192, BF16, "ktb1")]
    VBs = [fA(49152, 8192, BF16, "vb0"), fA(57344, 8192, BF16, "vb1")]
    ATTS, ATTSb = fX(0, 8192, BF16, "atts")
    STBs = [fX(20480, 1024, BF16, "stb0"), fX(21504, 1024, BF16, "stb1")]
    QQs = [fX(22528, 512, BF16, "qq0"), fX(23040, 512, BF16, "qq1")]
    SQG, SQGb = fX(23552, 512, BF16, "sqg")
    KVBFs = [fX(16384, 256, BF16, "kvbf0"), fX(16640, 256, BF16, "kvbf1")]
    VKDPd = [[fX(16896, 256, BF16, "vkdp00"), fX(17152, 256, BF16, "vkdp01")],
             [fX(17408, 256, BF16, "vkdp10"), fX(17664, 256, BF16, "vkdp11")]]
    GBs = [fX(17920, 512, BF16, "gb0"), fX(18432, 512, BF16, "gb1")]

    def loadblk(i, par):
        p.dma("sp", QBs[par][0], q_fm_s[i, :, :],
              reads=[B("q_fm_s")], writes=[QBs[par][1]])
        p.dma("sp", KBs[par][0], k_fm_s[i, :, :],
              reads=[B("k_fm_s")], writes=[KBs[par][1]])
        p.dma("sp", KTBs[par][0].rearrange("p (c n) -> p c n", c=2),
              k_tm_s.rearrange("m p n -> p m n")[:, 2 * i:2 * i + 2, :],
              reads=[B("k_tm_s")], writes=[KTBs[par][1]])
        p.dma("sp", VBs[par][0].rearrange("p (c n) -> p c n", c=2),
              v_tm_s.rearrange("m p n -> p m n")[:, 2 * i:2 * i + 2, :],
              reads=[B("v_tm_s")], writes=[VBs[par][1]])

    loadblk(0, 0)
    for i in range(NB):
        par = i % 2
        if i + 1 < NB:
            loadblk(i + 1, (i + 1) % 2)
        qb, qbb = QBs[par]
        kb, kbb = KBs[par]
        ktb, ktbb = KTBs[par]
        vb, vbb = VBs[par]
        ssbank = 2
        for h in range(NH):
            pbP = 5 if h % 2 == 0 else 3
            pbO = 6 if h % 2 == 0 else 0
            pbC = 4 if h % 2 == 0 else 1
            KVBF, KVBFb = KVBFs[h % 2]
            QQ, QQb = QQs[h % 2]
            STB, STBb = STBs[h % 2]
            VKDP = VKDPd[h % 2]
            p.comp("act", (lambda _h=h, _K=KVBF: lambda e: e.activation(
                _K, KVST[:, _h * 128:(_h + 1) * 128], AF.Copy, bias=0.0))(),
                reads=[KVSTb], writes=[KVBFb])
            p.comp("dve", (lambda _h=h, _q=qb, _Q=QQ: lambda e: e.tensor_mul(
                _Q, _q[:, _h * 256:(_h + 1) * 256],
                qdb[:, _h * 256:(_h + 1) * 256]))(),
                reads=[qbb, qdbb], writes=[QQb])
            p.mm_group(pbb[pbP], [
                ((lambda _h=h, _c=c, _k=kb, _q=qb, _p=pbP: lambda e: e.matmul(
                    PB[_p][:, _c * 256:(_c + 1) * 256],
                    _k[:, _h * 256 + _c * 128:_h * 256 + (_c + 1) * 128],
                    _q[:, _h * 256:(_h + 1) * 256],
                    start=True, stop=True))(), [kbb, qbb])
                for c in range(2)])
            p.comp("dve", (lambda _h=h, _S=STB, _p=pbP: lambda e: e.tensor_tensor(
                _S, PB[_p][:, 0:512], dtt[:, _h * 512:(_h + 1) * 512], OP.mult))(),
                reads=[pbb[pbP], dttb], writes=[STBb])
            p.mm_group(pbb[pbO], [
                ((lambda _h=h, _v=vb, _S=STB, _p=pbO: lambda e: e.matmul(
                    PB[_p][:, 0:256], _v[:, _h * 128:(_h + 1) * 128],
                    _S[:, 0:256], start=True, stop=False))(), [vbb, STBb]),
                ((lambda _h=h, _v=vb, _S=STB, _p=pbO: lambda e: e.matmul(
                    PB[_p][:, 0:256], _v[:, 2048 + _h * 128:2048 + (_h + 1) * 128],
                    _S[:, 256:512], start=False, stop=False))(), [vbb, STBb]),
                ((lambda _K=KVBF, _Q=QQ, _p=pbO: lambda e: e.matmul(
                    PB[_p][:, 0:256], _K, _Q, start=False, stop=True))(),
                 [KVBFb, QQb]),
            ])
            p.comp("act", (lambda _h=h, _p=pbO: lambda e: e.activation(
                ATTS[:, _h * 256:(_h + 1) * 256], PB[_p][:, 0:256], AF.Copy,
                bias=0.0))(),
                reads=[pbb[pbO]], writes=[ATTSb])
            p.comp("act", (lambda _h=h: lambda e: e.activation(
                SQG[:, 0:256], ATTS[:, _h * 256:(_h + 1) * 256], AF.Square))(),
                reads=[ATTSb], writes=[SQGb])
            gbuf, gbufb = GBs[h % 2]
            p.dma("act", gbuf, g_s[h, :, 256 * i:256 * (i + 1)],
                  reads=[B("g_s")], writes=[gbufb])
            p.comp("pool", (lambda _h=h, _g=gbuf: lambda e: e.tensor_mul(
                ATTS[:, _h * 256:(_h + 1) * 256],
                ATTS[:, _h * 256:(_h + 1) * 256], _g))(),
                reads=[ATTSb, gbufb], writes=[ATTSb])
            p.mm_group(pbb[ssbank], [
                ((lambda _h=h, _b=ssbank: lambda e: e.matmul(
                    PB[_b][:, 0:256], ones_bf[:, :], SQG[:, 0:256],
                    start=(_h == 0), stop=(_h == NH - 1)))(), [B("ones"), SQGb])])
            for cc_ in range(2):
                p.comp("dve", (lambda _h=h, _c=cc_, _v=vb, _V=VKDP[cc_][0]:
                               lambda e: e.tensor_scalar_mul(
                    _V, _v[:, _c * 2048 + _h * 128:_c * 2048 + (_h + 1) * 128],
                    kdc[:, 2 * _h + _c:2 * _h + _c + 1]))(),
                    reads=[vbb, B("kdc")], writes=[VKDP[cc_][1]])
            p.mm_group(pbb[pbC], [
                ((lambda _h=h, _c=c, _k=ktb, _V=VKDP[c][0], _p=pbC: lambda e: e.matmul(
                    PB[_p][:, 0:128],
                    _k[:, _c * 2048 + _h * 128:_c * 2048 + (_h + 1) * 128],
                    _V, start=(_c == 0), stop=(_c == 1)))(),
                 [ktbb, VKDP[c][1]])
                for c in range(2)])
            p.comp("dve", (lambda _h=h, _p=pbC: lambda e: e.scalar_tensor_tensor(
                KVST[:, _h * 128:(_h + 1) * 128], KVST[:, _h * 128:(_h + 1) * 128],
                lam[:, _h:_h + 1], PB[_p][:, 0:128], OP.mult, OP.add))(),
                reads=[KVSTb, B("lam"), pbb[pbC]], writes=[KVSTb])
        p.comp("dve", (lambda _i=i, _b=ssbank: lambda e: e.tensor_copy(
            ssrow[:, 256 * _i:256 * (_i + 1)], PB[_b][0:1, 0:256]))(),
            reads=[pbb[ssbank]], writes=[ROWSb])
        for cc_ in range(2):
            p.dma("sp", attn_s[2 * i + cc_, :, :].rearrange("p (h t) -> p h t", h=NH),
                  ATTS.rearrange("p (h c t) -> p c h t", h=NH, c=2)[:, cc_, :, :],
                  reads=[ATTSb], writes=[B("attn_s")])
    tap("attnc0", attn_s[0, :, :], [128, NH * 128], BF16, reads=[B("attn_s")])

    p.comp("dve", lambda e: e.tensor_scalar(rsrow[:, :], ssrow[:, :],
                                            float(EPS_LN / D),
                                            float(EPS_RMS * EPS_LN),
                                            OP.mult, OP.add),
           reads=[ROWSb], writes=[ROWSb])
    p.comp("dve", lambda e: e.tensor_copy(rsbf[:, :], rsrow[:, :]),
           reads=[ROWSb], writes=[RSBFb])
    p.mm_group(pbb[4], [
        ((lambda _m=m: lambda e: e.matmul(
            PB[4][:, _m:_m + 1], rsbf[:, _m * 128:(_m + 1) * 128],
            ones_bf[0:1, 0:1], start=True, stop=True))(), [RSBFb, B("ones")])
        for m in range(NT)])
    p.comp("dve", lambda e: e.tensor_copy(rstdc[:, :], PB[4][:, 0:16]),
           reads=[pbb[4]], writes=[B("rstdc")])
    tap("rstdc", rstdc[:, :], [128, 16], F32, reads=[B("rstdc")])

    if stage_limit < 5:
        return finish()

    OSC, OSCb = fX(0, 8192, F32, "osc")
    XADJ, XADJb = fX(8192, 8192, F32, "xadj")
    XP, XPb = fX(16384, 8192, F32, "xp")
    XT4, XT4b = fX(24576, 8192, F32, "xt4")
    ATC0, ATC0b = fX(40960, 4096, BF16, "atc0")
    ATC1, ATC1b = fX(45056, 4096, BF16, "atc1")
    ATC = [(ATC0, ATC0b), (ATC1, ATC1b)]
    GTC, GTCb = fX(49152, 4096, BF16, "gtc")
    LN2T, LN2Tb = fX(57344, 4096, BF16, "ln2t")

    ln2A, ln2Ab = fA(0, 65536, BF16, "ln2fm")
    lnfm2 = ln2A.rearrange("p (f t) -> p f t", f=16)

    ATC2, ATC2b = fX(49152, 4096, BF16, "atc2")
    ATC3, ATC3b = fX(53248, 4096, BF16, "atc3")
    ATC = [ATC[0], ATC[1], (ATC2, ATC2b), (ATC3, ATC3b)]
    LN2T1, LN2T1b = fX(32768, 4096, BF16, "ln2t1")
    LN2Ts = [(LN2T, LN2Tb), (LN2T1, LN2T1b)]

    def s4_loads(m):
        atc, atcb = ATC[m % 4]
        p.dma("sp", atc, attn_s[m, :, :], reads=[B("attn_s")], writes=[atcb])

    def s4_main(m):
        atc, atcb = ATC[m % 4]
        ln2t, ln2tb = LN2Ts[m % 2]
        p.dma("sp", XT4, x_in[m * 128:(m + 1) * 128, :], writes=[XT4b])
        for s in range(4):
            mms = []
            for f in range(16):
                mms.append((
                    (lambda _f=f, _s=s, _a=atc: lambda e: e.matmul(
                        PB[_s][:, 0:512], _a[:, _f * 128:(_f + 1) * 128],
                        wo[:, _f, _s * 512:(_s + 1) * 512],
                        start=(_f == 0), stop=(_f == 15)))(),
                    [atcb, wob]))
            p.mm_group(pbb[s], mms)
        for s in range(4):
            p.comp("dve", (lambda _s=s: lambda e: e.tensor_copy(
                OSC[:, _s * 512:(_s + 1) * 512], PB[_s][:, 0:512]))(),
                reads=[pbb[s]], writes=[OSCb])
        cm, cr, bm, br = ln_stats(OSC, OSCb, XP, XPb, EPS_LN,
                                  eps_extra=(rstdc[:, m:m + 1], B("rstdc")))
        p.comp("dve", (lambda _m=cm, _r=cr: lambda e: e.scalar_tensor_tensor(
            colE[:, 0:1], _m, -1.0, _r, OP.mult, OP.mult))(),
            reads=[bm, br], writes=[B("cE0")])
        p.comp("dve", lambda e: e.tensor_scalar_add(XADJ, XT4, colE[:, 0:1]),
               reads=[XT4b, B("cE0")], writes=[XADJb])
        p.comp("dve", (lambda _r=cr: lambda e: e.scalar_tensor_tensor(
            XP, OSC, _r, XADJ, OP.mult, OP.add))(),
            reads=[OSCb, br, XADJb], writes=[XPb])
        p.dma("sp", x1_s[m * 128:(m + 1) * 128, :], XP, reads=[XPb],
              writes=[B("x1_s")])
        cm, cr, bm, br = ln_stats(XP, XPb, XADJ, XADJb, EPS_LN)
        p.comp("dve", (lambda _m=cm, _r=cr, _t=ln2t: lambda e: e.tensor_scalar(
            _t, XP, _m, _r, OP.subtract, OP.mult))(),
            reads=[XPb, bm, br], writes=[ln2tb])

    def s4_tail(m):
        ln2t, ln2tb = LN2Ts[m % 2]
        transpose_tile(ln2t, ln2tb, lnfm2, ln2Ab, m)
        mms = []
        for f in range(16):
            mms.append((
                (lambda _f=f, _m=m: lambda e: e.matmul(
                    PB[4][:, _m * 32:(_m + 1) * 32],
                    lnfm2[:, _f, _m * 128:(_m + 1) * 128],
                    gw_sb[:, _f * 32:(_f + 1) * 32],
                    start=(_f == 0), stop=(_f == 15)))(),
                [ln2Ab, B("gw_sb")]))
        p.mm_group(pbb[4], mms)

    s4_loads(0)
    s4_loads(1)
    s4_loads(2)
    for m in range(NT):
        if m + 3 < NT:
            s4_loads(m + 3)
        s4_main(m)
        if m >= 1:
            s4_tail(m - 1)
    s4_tail(NT - 1)
    tap("ln2fm", ln2A[:, 0:2048], [128, 2048], BF16, reads=[ln2Ab])
    tap("x1", x1_s[0:128, :], [128, D], F32, reads=[B("x1_s")])

    if stage_limit < 6:
        return finish()

    SC, SCb = fX(0, 2048, F32, "sc5")
    SB5, SB5b = fX(2048, 2048, F32, "sb5")
    WK1, WK1b = fX(4096, 2048, F32, "wk1")
    MK, MKb = fX(6144, 2048, F32, "mk5")
    CUR, CURb = fX(8192, 2048, F32, "cur5")
    T15, T15b = fX(10240, 2048, F32, "t15")
    M1, M1b = fX(12288, 256, F32, "m15")
    M2, M2b = fX(12544, 256, F32, "m25")
    GS, GSb = fX(12800, 256, F32, "gs5")
    G1, G1b = fX(13056, 64, F32, "g15")
    G2, G2b = fX(13120, 64, F32, "g25")
    TH, THb = fX(13184, 64, F32, "th5")
    SSUM, SSUMb = fX(13248, 64, F32, "ssum5")

    def v3(ap, a, b_):
        return ap.rearrange("p (a b) -> p a b", a=a, b=b_)

    def vcol(ap, a):
        return ap.rearrange("p (a o) -> p a o", a=a, o=1)

    p.comp("act", lambda e: e.activation(SC, PB[4][:, 0:512], AF.Sigmoid),
           reads=[pbb[4]], writes=[SCb])
    p.comp("dve", lambda e: e.tensor_tensor(SB5, SC, gb_rep[:, :], OP.add),
           reads=[SCb, B("gb_rep")], writes=[SB5b])
    p.comp("dve", lambda e: e.tensor_reduce(vcol(M1, 64), v3(SB5, 64, 8), AX.X, OP.max),
           reads=[SB5b], writes=[M1b])
    p.comp("dve", lambda e: e.tensor_tensor(
        v3(WK1, 64, 8), v3(SB5, 64, 8), vcol(M1, 64).broadcast_to((128, 64, 8)),
        OP.is_ge), reads=[SB5b, M1b], writes=[WK1b])
    p.comp("dve", lambda e: e.scalar_tensor_tensor(WK1, WK1, -BIGR, SB5, OP.mult, OP.add),
           reads=[WK1b, SB5b], writes=[WK1b])
    p.comp("dve", lambda e: e.tensor_reduce(vcol(M2, 64), v3(WK1, 64, 8), AX.X, OP.max),
           reads=[WK1b], writes=[M2b])
    p.comp("dve", lambda e: e.tensor_tensor(GS[:, 0:64], M1[:, 0:64], M2[:, 0:64], OP.add),
           reads=[M1b, M2b], writes=[GSb])
    p.comp("dve", lambda e: e.tensor_reduce(vcol(G1, 16), v3(GS[:, 0:64], 16, 4), AX.X, OP.max),
           reads=[GSb], writes=[G1b])
    p.comp("dve", lambda e: e.tensor_tensor(
        v3(M1[:, 0:64], 16, 4), v3(GS[:, 0:64], 16, 4),
        vcol(G1, 16).broadcast_to((128, 16, 4)), OP.is_ge),
        reads=[GSb, G1b], writes=[M1b])
    p.comp("dve", lambda e: e.scalar_tensor_tensor(
        M1[:, 0:64], M1[:, 0:64], -BIGR, GS[:, 0:64], OP.mult, OP.add),
        reads=[M1b, GSb], writes=[M1b])
    p.comp("dve", lambda e: e.tensor_reduce(vcol(G2, 16), v3(M1[:, 0:64], 16, 4), AX.X, OP.max),
           reads=[M1b], writes=[G2b])
    p.comp("dve", lambda e: e.tensor_tensor(
        v3(M2[:, 0:64], 16, 4), v3(GS[:, 0:64], 16, 4),
        vcol(G2, 16).broadcast_to((128, 16, 4)), OP.is_ge),
        reads=[GSb, G2b], writes=[M2b])
    p.comp("dve", lambda e: e.tensor_scalar(M2[:, 0:64], M2[:, 0:64], BIGR, -BIGR,
                                            OP.mult, OP.add),
           reads=[M2b], writes=[M2b])
    p.comp("dve", lambda e: e.tensor_tensor(
        v3(MK, 64, 8), v3(SB5, 64, 8), vcol(M2[:, 0:64], 64).broadcast_to((128, 64, 8)),
        OP.add), reads=[SB5b, M2b], writes=[MKb])
    p.comp("dve", lambda e: e.tensor_copy(CUR, MK), reads=[MKb], writes=[CURb])
    for t_ in range(8):
        p.comp("dve", lambda e: e.tensor_reduce(vcol(TH, 16), v3(CUR, 16, 32), AX.X, OP.max),
               reads=[CURb], writes=[THb])
        if t_ < 7:
            p.comp("dve", lambda e: e.tensor_tensor(
                v3(T15, 16, 32), v3(CUR, 16, 32), vcol(TH, 16).broadcast_to((128, 16, 32)),
                OP.is_ge), reads=[CURb, THb], writes=[T15b])
            p.comp("dve", lambda e: e.scalar_tensor_tensor(CUR, T15, -BIGR, CUR,
                                                           OP.mult, OP.add),
                   reads=[T15b, CURb], writes=[CURb])
    p.comp("dve", lambda e: e.tensor_tensor(
        v3(T15, 16, 32), v3(MK, 16, 32), vcol(TH, 16).broadcast_to((128, 16, 32)),
        OP.is_ge), reads=[MKb, THb], writes=[T15b])
    p.comp("dve", lambda e: e.tensor_tensor(T15, T15, SC, OP.mult),
           reads=[T15b, SCb], writes=[T15b])
    p.comp("dve", lambda e: e.tensor_reduce(vcol(SSUM, 16), v3(T15, 16, 32), AX.X, OP.add),
           reads=[T15b], writes=[SSUMb])
    p.comp("dve", lambda e: e.reciprocal(SSUM[:, 0:16], SSUM[:, 0:16]),
           reads=[SSUMb], writes=[SSUMb])
    p.comp("dve", lambda e: e.tensor_scalar_mul(SSUM[:, 0:16], SSUM[:, 0:16], 2.5),
           reads=[SSUMb], writes=[SSUMb])
    p.comp("dve", lambda e: e.tensor_tensor(
        v3(cw[:, :], 16, 32), v3(T15, 16, 32), vcol(SSUM[:, 0:16], 16).broadcast_to((128, 16, 32)),
        OP.mult), reads=[T15b, SSUMb], writes=[B("cw")])
    tap("cw", cw[:, :], [128, 512], F32, reads=[B("cw")])

    if stage_limit < 7:
        return finish()

    FC1ap, FC1b = fW(0, 32768, BF16, "fc1")
    FC1 = FC1ap.rearrange("p (f l) -> p f l", f=16)
    for g in range(2):
        p.dma("pool", FC1[:, 8 * g:8 * (g + 1), :],
              fc1T.rearrange("(f p) l -> p f l", p=128)[:, 8 * g:8 * (g + 1), :],
              writes=[FC1b])
    XLTap, XLTb = fX(0, 32768, BF16, "xlt")
    XLT = XLTap.rearrange("p (l t) -> p l t", l=8)
    for lo in range(8):
        for s in range(4):
            bank = (2 * lo + s) % 4
            mms = []
            for f in range(16):
                mms.append((
                    (lambda _f=f, _lo=lo, _s=s, _bk=bank: lambda e: e.matmul(
                        PB[_bk][:, 0:512], FC1[:, _f, _lo * 128:(_lo + 1) * 128],
                        lnfm2[:, _f, _s * 512:(_s + 1) * 512],
                        start=(_f == 0), stop=(_f == 15)))(),
                    [FC1b, ln2Ab]))
            p.mm_group(pbb[bank], mms)
            p.comp("dve", (lambda _lo=lo, _s=s, _bk=bank: lambda e: e.tensor_copy(
                XLT[:, _lo, _s * 512:(_s + 1) * 512], PB[_bk][:, 0:512]))(),
                reads=[pbb[bank]], writes=[XLTb])
    tap("xlat", XLTap[:, 0:2048], [128, 2048], BF16, reads=[XLTb])

    SUPs = [fW(32768, 16384, BF16, "supb0"), fW(49152, 16384, BF16, "supb1")]
    RSXs = [fX(57344, 2048, F32, "rsx0"), fX(59392, 2048, F32, "rsx1")]
    MSTs = [fX(32768, 4096, BF16, "mst0"), fX(36864, 4096, BF16, "mst1")]
    for j in range(8):
        sj, sjb = SUPs[j % 2]
        sjv = sj.rearrange("p (f n) -> p f n", f=16)
        p.dma("pool", sjv[:, :, :],
              supT[:, 512 * j:512 * (j + 1)].rearrange("(f p) n -> p f n", p=128),
              writes=[sjb])
        for sub in range(4):
            mo = j * 4 + sub
            mstp, mstb = MSTs[mo % 2]
            for s in range(4):
                bank = (2 * sub + s) % 4
                mms = []
                for f in range(16):
                    mms.append((
                        (lambda _f=f, _sub=sub, _s=s, _bk=bank, _sj=sjv: lambda e: e.matmul(
                            PB[_bk][:, 0:512], _sj[:, _f, _sub * 128:(_sub + 1) * 128],
                            lnfm2[:, _f, _s * 512:(_s + 1) * 512],
                            start=(_f == 0), stop=(_f == 15)))(),
                        [sjb, ln2Ab]))
                p.mm_group(pbb[bank], mms)
                rs, rsb = RSXs[s % 2]
                p.comp("act", (lambda _bk=bank, _rs=rs: lambda e: e.activation(
                    _rs[:, 0:512], PB[_bk][:, 0:512], AF.Relu))(),
                    reads=[pbb[bank]], writes=[rsb])
                p.comp("act", (lambda _rs=rs, _mt=mstp, _s=s: lambda e: e.activation(
                    _mt[:, _s * 512:(_s + 1) * 512], _rs[:, 0:512], AF.Square))(),
                    reads=[rsb], writes=[mstb])
            p.dma("sp", mid_s[mo, :, :], mstp, reads=[mstb], writes=[B("mid_s")])
    tap("mid", mid_s[0, :, :], [128, T], BF16, reads=[B("mid_s")])

    if stage_limit < 8:
        return finish()

    W1ap, W1b_ = fW(0, 16384, BF16, "w1b")
    W1B = W1ap.rearrange("p (c i) -> p c i", c=8)
    W2ap, W2b_ = fW(16384, 16384, BF16, "w2b")
    W2B = W2ap.rearrange("p (c l) -> p c l", c=8)
    HSBap, HSBb = fW(32768, 32768, BF16, "hsb")
    HSB = HSBap.rearrange("p (c t) -> p c t", c=8)
    RELs = [fX(57344, 1024, BF16, "rels0"), fX(58368, 1024, BF16, "rels1")]
    YACCap, YACCb = fA(0, 65536, F32, "yacc")
    for e_ in range(NE):
        p.dma("pool", W1B[:, :, :], w1T[e_].rearrange("(c p) i -> p c i", p=128),
              writes=[W1b_])
        p.dma("pool", W2B[:, :, :], w2T[e_].rearrange("(c p) l -> p c l", p=128),
              writes=[W2b_])
        for ic in range(8):
            for s in range(4):
                bank = (2 * ic + s) % 4
                mms = []
                for lc in range(8):
                    mms.append((
                        (lambda _l=lc, _i=ic, _s=s, _bk=bank: lambda e: e.matmul(
                            PB[_bk][:, 0:512], W1B[:, _l, _i * 128:(_i + 1) * 128],
                            XLT[:, _l, _s * 512:(_s + 1) * 512],
                            start=(_l == 0), stop=(_l == 7)))(),
                        [W1b_, XLTb]))
                p.mm_group(pbb[bank], mms)
                rs, rsb = RELs[s % 2]
                p.comp("act", (lambda _bk=bank, _rs=rs: lambda e: e.activation(
                    _rs[:, 0:512], PB[_bk][:, 0:512], AF.Relu))(),
                    reads=[pbb[bank]], writes=[rsb])
                p.comp("pool", (lambda _rs=rs, _i=ic, _s=s: lambda e: e.tensor_mul(
                    HSB[:, _i, _s * 512:(_s + 1) * 512], _rs[:, 0:512], _rs[:, 0:512]))(),
                    reads=[rsb], writes=[HSBb])
        for m in range(NT):
            for ls in range(2):
                bank = 4 + ((2 * m + ls) % 2)
                mms = []
                for ic in range(8):
                    mms.append((
                        (lambda _i=ic, _m=m, _ls=ls, _bk=bank: lambda e: e.matmul(
                            PB[_bk][:, 0:512], HSB[:, _i, _m * 128:(_m + 1) * 128],
                            W2B[:, _i, _ls * 512:(_ls + 1) * 512],
                            start=(_i == 0), stop=(_i == 7)))(),
                        [HSBb, W2b_]))
                p.mm_group(pbb[bank], mms)
                if e_ == 0:
                    p.comp("dve", (lambda _m=m, _ls=ls, _e=e_, _bk=bank:
                                   lambda e: e.tensor_scalar_mul(
                        YACCap[:, _m * 1024 + _ls * 512:_m * 1024 + (_ls + 1) * 512],
                        PB[_bk][:, 0:512], cw[:, _m * 32 + _e:_m * 32 + _e + 1]))(),
                        reads=[pbb[bank], B("cw")], writes=[YACCb])
                else:
                    p.comp("dve", (lambda _m=m, _ls=ls, _e=e_, _bk=bank:
                                   lambda e: e.scalar_tensor_tensor(
                        YACCap[:, _m * 1024 + _ls * 512:_m * 1024 + (_ls + 1) * 512],
                        PB[_bk][:, 0:512], cw[:, _m * 32 + _e:_m * 32 + _e + 1],
                        YACCap[:, _m * 1024 + _ls * 512:_m * 1024 + (_ls + 1) * 512],
                        OP.mult, OP.add))(),
                        reads=[pbb[bank], B("cw"), YACCb], writes=[YACCb])
    tap("yacc", YACCap[:, 0:2048], [128, 2048], F32, reads=[YACCb])

    if stage_limit < 9:
        return finish()

    YBFap, YBFb = fX(0, 32768, BF16, "ybf")
    for m in range(NT):
        p.comp("act", (lambda _m=m: lambda e: e.activation(
            YBFap[:, _m * 1024:(_m + 1) * 1024],
            YACCap[:, _m * 1024:(_m + 1) * 1024], AF.Copy, bias=0.0))(),
            reads=[YACCb], writes=[YBFb])
    YFMap, YFMb = fW(32768, 32768, BF16, "yfm")
    YFM = YFMap.rearrange("p (c t) -> p c t", c=8)
    for m in range(NT):
        mms = []
        for lc in range(8):
            mms.append((
                (lambda _m=m, _l=lc: lambda e: e.transpose(
                    PTR[:, _l * 128:(_l + 1) * 128],
                    YBFap[:, _m * 1024 + _l * 128:_m * 1024 + (_l + 1) * 128],
                    ident[:, :]))(),
                [YBFb, B("ident")]))
        p.mm_group(ptrb, mms)
        p.comp("dve", (lambda _m=m: lambda e: e.tensor_copy(
            YFM[:, :, _m * 128:(_m + 1) * 128],
            PTR[:, :].rearrange("p (l t) -> p l t", l=8)))(),
            reads=[ptrb], writes=[YFMb])
    FC2ap, FC2b = fW(0, 32768, BF16, "fc2")
    FC2 = FC2ap.rearrange("p (c o) -> p c o", c=8)
    for g in range(2):
        p.dma("pool", FC2[:, 4 * g:4 * (g + 1), :],
              fc2T.rearrange("(c p) o -> p c o", p=128)[:, 4 * g:4 * (g + 1), :],
              writes=[FC2b])
    RTSs = [fX(32768, 4096, BF16, "rts0"), fX(36864, 4096, BF16, "rts1")]
    for m in range(NT):
        rt, rtb = RTSs[m % 2]
        for s in range(4):
            bank = (2 * m + s) % 4
            mms = []
            for lc in range(8):
                mms.append((
                    (lambda _l=lc, _m=m, _s=s, _bk=bank: lambda e: e.matmul(
                        PB[_bk][:, 0:512], YFM[:, _l, _m * 128:(_m + 1) * 128],
                        FC2[:, _l, _s * 512:(_s + 1) * 512],
                        start=(_l == 0), stop=(_l == 7)))(),
                    [YFMb, FC2b]))
            p.mm_group(pbb[bank], mms)
            p.comp("act", (lambda _bk=bank, _rt=rt, _s=s: lambda e: e.activation(
                _rt[:, _s * 512:(_s + 1) * 512], PB[_bk][:, 0:512], AF.Copy,
                bias=0.0))(),
                reads=[pbb[bank]], writes=[rtb])
        p.dma("sp", rout_s[m, :, :], rt, reads=[rtb], writes=[B("rout_s")])
    tap("routed", rout_s[0, :, :], [128, D], BF16, reads=[B("rout_s")])

    SDAap, SDAb = fW(0, 65536, BF16, "sda")
    SDA = SDAap.rearrange("p (c o) -> p c o", c=16)
    SDBap, SDBb = fA(0, 65536, BF16, "sdb")
    SDB = SDBap.rearrange("p (c o) -> p c o", c=16)
    sdn4 = sdnT.rearrange("(c p) o -> p c o", p=128)
    for g in range(4):
        p.dma("pool", SDA[:, 4 * g:4 * (g + 1), :], sdn4[:, 4 * g:4 * (g + 1), :],
              writes=[SDAb])
    for g in range(4):
        p.dma("pool", SDB[:, 4 * g:4 * (g + 1), :], sdn4[:, 16 + 4 * g:16 + 4 * (g + 1), :],
              writes=[SDBb])

    if stage_limit < 10:
        return finish()

    MIDs = [fX(0, 8192, BF16, "midb0"), fX(8192, 8192, BF16, "midb1")]
    RTBs = [fX(16384, 4096, BF16, "rtb0"), fX(20480, 4096, BF16, "rtb1")]
    X1B, X1Bb = fX(24576, 8192, F32, "x1b")
    F1, F1b = fX(32768, 8192, F32, "f1")
    OUTs = [(lambda a_, b_: (a_, b_))(*fX(40960, 8192, F32, "outb0")),
            (lambda a_, b_: (a_, b_))(*fX(49152, 8192, F32, "outb1"))]

    def s8_loads(m):
        mb, mbb = MIDs[m % 2]
        mbv = mb.rearrange("p (c t) -> p c t", c=32)
        for g in range(4):
            p.dma("sp", mbv[:, 8 * g:8 * (g + 1), :],
                  mid_s.rearrange("c p t -> p c t")[:, 8 * g:8 * (g + 1),
                                                    m * 128:(m + 1) * 128],
                  reads=[B("mid_s")], writes=[mbb])
        rt, rtb = RTBs[m % 2]
        p.dma("sp", rt, rout_s[m, :, :], reads=[B("rout_s")], writes=[rtb])

    s8_loads(0)
    for m in range(NT):
        mb, mbb = MIDs[m % 2]
        mbv = mb.rearrange("p (c t) -> p c t", c=32)
        rt, rtb = RTBs[m % 2]
        if m + 1 < NT:
            s8_loads(m + 1)
        p.dma("sp", X1B, x1_s[m * 128:(m + 1) * 128, :], reads=[B("x1_s")],
              writes=[X1Bb])
        for s in range(4):
            bank = (2 * m + s) % 4
            mms = []
            for mo in range(32):
                sap = SDA if mo < 16 else SDB
                sbu = SDAb if mo < 16 else SDBb
                mms.append((
                    (lambda _mo=mo, _s=s, _bk=bank, _mv=mbv, _sa=sap: lambda e: e.matmul(
                        PB[_bk][:, 0:512], _mv[:, _mo, :],
                        _sa[:, _mo % 16, _s * 512:(_s + 1) * 512],
                        start=(_mo == 0), stop=(_mo == 31)))(),
                    [mbb, sbu]))
            p.mm_group(pbb[bank], mms)
            p.comp("dve", (lambda _s=s, _bk=bank, _rt=rt: lambda e: e.tensor_tensor(
                F1[:, _s * 512:(_s + 1) * 512], PB[_bk][:, 0:512],
                _rt[:, _s * 512:(_s + 1) * 512], OP.add))(),
                reads=[pbb[bank], rtb], writes=[F1b])
        ob, obb = OUTs[m % 2]
        sqo, sqob = OUTs[(m + 1) % 2]
        cm, cr, bm, br = ln_stats(F1, F1b, sqo, sqob, EPS_LN)
        p.comp("dve", (lambda _m=cm, _r=cr: lambda e: e.scalar_tensor_tensor(
            colE[:, 1:2], _m, -1.0, _r, OP.mult, OP.mult))(),
            reads=[bm, br], writes=[B("cE1")])
        p.comp("dve", lambda e: e.tensor_scalar_add(X1B, X1B, colE[:, 1:2]),
               reads=[X1Bb, B("cE1")], writes=[X1Bb])
        p.comp("dve", (lambda _r=cr, _ob=ob: lambda e: e.scalar_tensor_tensor(
            _ob, F1, _r, X1B, OP.mult, OP.add))(),
            reads=[F1b, br, X1Bb], writes=[obb])
        p.dma("sp", y_out[m * 128:(m + 1) * 128, :], ob, reads=[obb])

    return finish()


def _host_constants():
    arb = np.broadcast_to(np.arange(1, 257, dtype=np.float32)[None, :],
                          (128, 256)).copy()
    pp = np.arange(128, dtype=np.float32)
    kdm = np.stack([255.0 - pp, 255.0 - (128.0 + pp)], axis=1).astype(np.float32)
    mtt = np.empty((128, 512), np.float32)
    q = np.arange(256, dtype=np.float32)
    for c in range(2):
        k = c * 128 + np.arange(128, dtype=np.float32)
        d = q[None, :] - k[:, None]
        mtt[:, c * 256:(c + 1) * 256] = np.where(d >= 0, d, BIG)
    ident = np.eye(128, dtype=ml_dtypes.bfloat16)
    ones_bf = np.ones((128, 128), dtype=ml_dtypes.bfloat16)
    onesf = np.ones((1, 128), dtype=np.float32)
    return arb, kdm, mtt, ident, ones_bf, onesf


def make_in_maps(inputs):
    x = np.asarray(inputs["L0_attn_hidden_states"], dtype=np.float32)
    slope = np.asarray(inputs["L0_attn_slope_rate"], dtype=np.float32)
    wqkv = np.asarray(inputs["L0_attn_w_qkv"], dtype=np.float32)
    wg = np.asarray(inputs["L0_attn_w_gate"], dtype=np.float32)
    wo = np.asarray(inputs["L0_attn_w_out"], dtype=np.float32)
    nw = np.asarray(inputs["L0_attn_norm_w"], dtype=np.float32)
    gw = np.asarray(inputs["L0_ffn_gate_w"], dtype=np.float32)
    gb = np.asarray(inputs["L0_ffn_gate_bias"], dtype=np.float32)
    fc1 = np.asarray(inputs["L0_ffn_fc1_w"], dtype=np.float32)
    fc2 = np.asarray(inputs["L0_ffn_fc2_w"], dtype=np.float32)
    w1 = np.asarray(inputs["L0_ffn_w1"], dtype=np.float32)
    w2 = np.asarray(inputs["L0_ffn_w2"], dtype=np.float32)
    sup = np.asarray(inputs["L0_ffn_shared_up_w"], dtype=np.float32)
    sdn = np.asarray(inputs["L0_ffn_shared_down_w"], dtype=np.float32)

    arb, kdm, mtt, ident, ones_bf, onesf = _host_constants()
    common = dict(
        wqkvT=np.ascontiguousarray(wqkv.T),
        wgT=np.ascontiguousarray(wg.T),
        woT=np.ascontiguousarray(wo.T),
        fc1T=np.ascontiguousarray(fc1.T),
        fc2T=np.ascontiguousarray(fc2.T),
        supT=np.ascontiguousarray(sup.T),
        sdnT=np.ascontiguousarray(sdn.T),
        w1T=np.ascontiguousarray(w1.transpose(0, 2, 1)),
        w2T=np.ascontiguousarray(w2.transpose(0, 2, 1)),
        gwT=np.ascontiguousarray(gw.T),
        sl_row=np.ascontiguousarray(slope[None, :]),
        gb_row=np.ascontiguousarray(gb[None, :]),
        nw_pc=np.ascontiguousarray(nw.reshape(16, 128).T),
        arb=arb, kdm=kdm, mtt=mtt, ident=ident, ones_bf=ones_bf, onesf=onesf,
    )
    in_maps = []
    for c in range(8):
        b, pos = c // 4, c % 4
        dist = np.full((1, 8), BIG, np.float32)
        for i in range(8):
            qq = i - 4 * b
            if 0 <= qq < pos:
                dist[0, i] = pos - 1 - qq
        m = dict(common)
        m["x"] = np.ascontiguousarray(x[b, pos * T:(pos + 1) * T, :])
        m["dist"] = dist
        in_maps.append(m)
    return in_maps


def kernel(**inputs):
    nc = build_program()
    in_maps = make_in_maps(inputs)
    res = run_bass_kernel_spmd(nc, in_maps, core_ids=list(range(8)))
    out = np.empty((2, 8192, 2048), np.float32)
    for c in range(8):
        b, pos = c // 4, c % 4
        out[b, pos * T:(pos + 1) * T, :] = np.asarray(res.results[c]["y"])
    return out


# version 15
# speedup vs baseline: 1.0074x; 1.0074x over previous
import numpy as np
import ml_dtypes
from contextlib import ExitStack

import concourse.bass as bass
import concourse.mybir as mybir
from concourse.bass_utils import run_bass_kernel_spmd
from concourse.alu_op_type import AluOpType

F32 = mybir.dt.float32
BF16 = mybir.dt.bfloat16
AF = mybir.ActivationFunctionType
AX = mybir.AxisListType
OP = AluOpType

ENGINES = ("pe", "dve", "act", "pool", "sp")
NLANES = 8
QLANES = {q: tuple(f"q{q}{i}" for i in range(NLANES)) for q in ("sp", "act", "pool")}
SEMS = ("pe", "dve", "act", "pool", "cc") + QLANES["sp"] + QLANES["act"] + QLANES["pool"]


class Buf:
    __slots__ = ("name", "w", "r")

    def __init__(self, name, prior=()):
        self.name = name
        self.w = {}
        self.r = {}
        for b in prior:
            for k, v in b.dep_write().items():
                if self.w.get(k, 0) < v:
                    self.w[k] = v

    def dep_read(self):
        return dict(self.w)

    def dep_write(self):
        d = dict(self.w)
        for k, v in self.r.items():
            if d.get(k, 0) < v:
                d[k] = v
        return d

    def note_w(self, key, val):
        if self.w.get(key, 0) < val:
            self.w[key] = val

    def note_r(self, key, val):
        if self.r.get(key, 0) < val:
            self.r[key] = val


def _merge(dst, src):
    for k, v in src.items():
        if dst.get(k, 0) < v:
            dst[k] = v


class Prog:
    def __init__(self, nc):
        self.nc = nc
        self.ops = {e: [] for e in ENGINES}
        self.cnt = {k: 0 for k in SEMS}
        self.waited = {e: {} for e in ENGINES}
        self.lane_rr = {q: 0 for q in QLANES}

    def _deps(self, reads, writes, extra):
        d = {}
        for b in reads:
            _merge(d, b.dep_read())
        for b in writes:
            _merge(d, b.dep_write())
        for ev in extra:
            if ev is not None:
                _merge(d, {ev[0]: ev[1]})
        return d

    def _prune(self, eng, deps, self_sync=True):
        out = {}
        wt = self.waited[eng]
        for k, v in deps.items():
            if not self_sync and k == eng:
                continue
            if k == eng and eng == "pe":
                continue
            if wt.get(k, 0) >= v:
                continue
            out[k] = v
        for k, v in out.items():
            wt[k] = v
        return out

    def _rec(self, eng, waits, fn, inc_key, inc_amt):
        self.ops[eng].append((waits, fn, inc_key, inc_amt))

    def comp(self, eng, fn, reads=(), writes=(), extra=()):
        deps = self._deps(reads, writes, extra)
        waits = self._prune(eng, deps, True)
        self.cnt[eng] += 1
        ev = (eng, self.cnt[eng])
        self._rec(eng, waits, fn, eng, 1)
        for b in writes:
            b.note_w(eng, ev[1])
        for b in reads:
            b.note_r(eng, ev[1])
        return ev

    def dma(self, queue, out_ap, in_ap, reads=(), writes=(), extra=(), **kw):
        r = self.lane_rr[queue]
        self.lane_rr[queue] = (r + 1) % NLANES
        qk = QLANES[queue][r]
        deps = self._deps(reads, writes, extra)
        _merge(deps, {qk: self.cnt[qk]})
        waits = self._prune(queue, deps, True)
        self.cnt[qk] += 16
        val = self.cnt[qk]

        def fn(eng, _o=out_ap, _i=in_ap, _kw=dict(kw)):
            return eng.dma_start(out=_o, in_=_i, **_kw)

        self._rec(queue, waits, fn, qk, 16)
        for b in writes:
            b.note_w(qk, val)
        for b in reads:
            b.note_r(qk, val)
        return (qk, val)

    def mm_group(self, psum_buf, mms):
        n = len(mms)
        ev = None
        for i, (fn, reads) in enumerate(mms):
            deps = self._deps(reads, (psum_buf,) if i == 0 else (), ())
            waits = self._prune("pe", deps, True)
            if i == n - 1:
                self.cnt["pe"] += 1
                ev = ("pe", self.cnt["pe"])
                self._rec("pe", waits, fn, "pe", 1)
                psum_buf.note_w("pe", ev[1])
                for _f2, reads2 in mms:
                    for b in reads2:
                        b.note_r("pe", ev[1])
            else:
                self._rec("pe", waits, fn, None, 0)
        return ev

    def wait_only(self, eng, extra):
        deps = {}
        for e in extra:
            _merge(deps, {e[0]: e[1]})
        waits = self._prune(eng, deps, True)
        if waits:
            self._rec(eng, waits, None, None, 0)

    def drain_queues(self, eng="sp"):
        evs = [(lk, self.cnt[lk]) for lanes in QLANES.values() for lk in lanes
               if self.cnt[lk] > 0]
        self.wait_only(eng, evs)

    def verify(self):
        idx = {e: 0 for e in ENGINES}
        sem = {k: 0 for k in SEMS}
        total = sum(len(v) for v in self.ops.values())
        done = 0
        progressed = True
        while progressed:
            progressed = False
            for e in ENGINES:
                while idx[e] < len(self.ops[e]):
                    waits, fn, ik, ia = self.ops[e][idx[e]]
                    if any(sem[k] < v for k, v in waits.items()):
                        break
                    if ik is not None:
                        sem[ik] += ia
                    idx[e] += 1
                    done += 1
                    progressed = True
        if done != total:
            lines = []
            for e in ENGINES:
                if idx[e] < len(self.ops[e]):
                    waits = self.ops[e][idx[e]][0]
                    missing = {k: (v, sem[k]) for k, v in waits.items() if sem[k] < v}
                    lines.append(f"  {e} stuck at {idx[e]}/{len(self.ops[e])} on {missing}")
            raise RuntimeError("DEADLOCK in schedule:\n" + "\n".join(lines))

    def replay(self):
        nc = self.nc
        with ExitStack() as st:
            sems = {k: st.enter_context(nc.semaphore("sem_" + k)) for k in SEMS}
            block = st.enter_context(nc.Block())

            def mk(engname):
                def run(eng):
                    for waits, fn, ik, ia in self.ops[engname]:
                        for k in sorted(waits):
                            eng.wait_ge(sems[k], waits[k])
                        if fn is not None:
                            ins = fn(eng)
                            if ik is not None:
                                ins.then_inc(sems[ik], ia)
                return run

            block.tensor(mk("pe"))
            block.vector(mk("dve"))
            block.scalar(mk("act"))
            block.gpsimd(mk("pool"))
            block.sync(mk("sp"))


class Flex:

    def __init__(self):
        self.reg = {}

    def view(self, arena_handle, arena_name, byte0, nbytes, dtype, tag):
        es = 4 if dtype == F32 else 2
        assert byte0 % es == 0 and nbytes % es == 0
        ap = arena_handle[:, :]
        if dtype != ap.dtype:
            ap = ap.bitcast(dtype)
        ap = ap[:, byte0 // es:(byte0 + nbytes) // es]
        lst = self.reg.setdefault(arena_name, [])
        b1 = byte0 + nbytes
        priors = [bu for (a0, a1, bu) in lst if a0 < b1 and byte0 < a1]
        buf = Buf(tag, priors)
        self.reg[arena_name] = [(a0, a1, bu) for (a0, a1, bu) in lst
                                if not (a0 < b1 and byte0 < a1)]
        self.reg[arena_name].append((byte0, b1, buf))
        return ap, buf


T = 2048
D = 2048
NT = 16
NH = 16
NB = 8
NE = 32
LAT = 1024
SH = 4096
EPS_LN = np.float32(1e-5)
EPS_RMS = np.float32(1e-6)
BIG = 1.0e6
BIGR = 1.0e9


def build_program(stage_limit=99, taps=()):
    nc = bass.Bass("TRN2", target_bir_lowering=False, debug=False)
    p = Prog(nc)
    taps = set(taps)

    def di(name, shape, dt=F32):
        return nc.dram_tensor(name, shape, dt, kind="ExternalInput").ap()

    x_in = di("x", [T, D])
    wqkvT = di("wqkvT", [D, 3 * D])
    wgT = di("wgT", [D, D])
    woT = di("woT", [D, D])
    fc1T = di("fc1T", [D, LAT])
    fc2T = di("fc2T", [LAT, D])
    supT = di("supT", [D, SH])
    sdnT = di("sdnT", [SH, D])
    w1T = di("w1T", [NE, LAT, LAT])
    w2T = di("w2T", [NE, LAT, LAT])
    gwT = di("gwT", [D, NE])
    sl_row = di("sl_row", [1, NH])
    gb_row = di("gb_row", [1, NE])
    nw_pc = di("nw_pc", [128, 16])
    dist_in = di("dist", [1, 8])
    arb_in = di("arb", [128, 256])
    kdm_in = di("kdm", [128, 2])
    mtt_in = di("mtt", [128, 512])
    ident_in = di("ident", [128, 128], BF16)
    ones_bf_in = di("ones_bf", [128, 128], BF16)
    onesf_in = di("onesf", [1, 128])

    y_out = nc.dram_tensor("y", [T, D], F32, kind="ExternalOutput").ap()

    def ds(name, shape, dt):
        return nc.dram_tensor(name, shape, dt, kind="Internal").ap()

    q_fm_s = ds("q_fm_s", [NB, 128, NH * 256], BF16)
    k_fm_s = ds("k_fm_s", [NB, 128, NH * 256], BF16)
    k_tm_s = ds("k_tm_s", [NT, 128, D], BF16)
    v_tm_s = ds("v_tm_s", [NT, 128, D], BF16)
    g_s = ds("g_s", [NH, 128, T], BF16)
    attn_s = ds("attn_s", [NT, 128, NH * 128], BF16)
    S_loc = ds("S_loc", [NH * 128, 128], F32)
    S_gth = ds("S_gth", [8 * NH * 128, 128], F32)
    x1_s = ds("x1_s", [T, D], F32)
    mid_s = ds("mid_s", [32, 128, T], BF16)
    rout_s = ds("rout_s", [NT, 128, D], BF16)

    def sb(name, shape, dt):
        return nc.alloc_sbuf_tensor(name, shape, dt)

    ident = sb("identS", [128, 128], BF16)
    ones_bf = sb("ones_bfS", [128, 128], BF16)
    onesf = sb("onesfS", [1, 128], F32)
    arb = sb("arbS", [128, 256], F32)
    kdm = sb("kdmS", [128, 2], F32)
    mtt = sb("mttS", [128, 512], F32)
    slr = sb("slrS", [1, NH], F32)
    gbr = sb("gbrS", [1, NE], F32)
    dsr = sb("dsrS", [1, 8], F32)
    nwc = sb("nwcS", [128, 16], F32)
    s_bc = sb("s_bc", [128, NH], F32)
    negs = sb("negs", [128, NH], F32)
    lam = sb("lam", [128, NH], F32)
    distb = sb("distb", [128, 8], F32)
    wkv = sb("wkv", [128, 8 * NH], F32)
    gb_bc = sb("gb_bc", [128, NE], F32)
    gb_rep = sb("gb_rep", [128, 512], F32)
    kdc = sb("kdc", [128, NH * 2], F32)
    cw = sb("cw", [128, 512], F32)
    rstdc = sb("rstdc", [128, 16], F32)
    gw_sb = sb("gw_sbS", [128, 512], BF16)
    stT = sb("stT", [128, 192], F32)
    agT = sb("agT", [128, 16], F32)
    colD = sb("colD", [128, 8], F32)
    colE = sb("colE", [128, 8], F32)

    BIGA = sb("BIGA", [128, 32768], BF16)
    BIGW = sb("BIGW", [128, 32768], BF16)
    FLXT = sb("FLX", [128, 30720], BF16)

    PB = [nc.alloc_psum_tensor(f"pb{i}", [128, 512], F32) for i in range(7)]
    PTR = nc.alloc_psum_tensor("ptrT", [128, 1024], BF16)

    flex = Flex()

    def fA(byte0, nbytes, dtype, tag):
        return flex.view(BIGA, "A", byte0, nbytes, dtype, tag)

    def fW(byte0, nbytes, dtype, tag):
        return flex.view(BIGW, "W", byte0, nbytes, dtype, tag)

    def fX(byte0, nbytes, dtype, tag):
        return flex.view(FLXT, "X", byte0, nbytes, dtype, tag)

    bufs = {}

    def B(name):
        if name not in bufs:
            bufs[name] = Buf(name)
        return bufs[name]

    pbb = [B(f"pb{i}") for i in range(7)]
    ptrb = B("ptrT")

    dtt, dttb = flex.view(FLXT, "X", 36864, NH * 1024, BF16, "dtt")
    qdb, qdbb = flex.view(FLXT, "X", 53248, NH * 512, BF16, "qdb")

    def tap(name, src_ap, shape, dt=F32, reads=()):
        if name in taps:
            o = nc.dram_tensor("tap_" + name, shape, dt, kind="ExternalOutput").ap()
            p.dma("sp", o, src_ap, reads=reads)

    _statrot = [0]

    def ln_stats(x_ap, xbuf_b, sq_scr, sq_b, eps, eps_extra=None):
        n = x_ap.shape[1]
        assert n % 512 == 0
        nj = n // 512
        i = _statrot[0] % 8
        _statrot[0] += 1
        st = stT[:, 24 * i:24 * i + 6 * nj]
        ag = agT[:, 2 * i:2 * i + 2]
        var = agT[:, 2 * i + 1:2 * i + 2]
        c_r = colD[:, i:i + 1]
        b_st, b_ag, b_r = B(f"st{i}"), B(f"ag{i}"), B(f"cD{i}")
        for j in range(nj):
            p.comp("dve", (lambda _j=j: lambda e: e.bn_stats(
                st[:, 6 * _j:6 * (_j + 1)], x_ap[:, 512 * _j:512 * (_j + 1)]))(),
                reads=[xbuf_b], writes=[b_st])
        p.comp("dve", lambda e: e.bn_aggr(ag, st), reads=[b_st], writes=[b_ag])
        if eps_extra is not None:
            ea, eb = eps_extra
            p.comp("dve", lambda e: e.tensor_tensor(var, var, ea, OP.add),
                   reads=[b_ag, eb], writes=[b_ag])
            eps = np.float32(0.0)
        p.comp("act", lambda e: e.activation(c_r, var, AF.Ln, bias=eps),
               reads=[b_ag], writes=[b_r])
        p.comp("act", lambda e: e.activation(c_r, c_r, AF.Exp, scale=-0.5),
               reads=[b_r], writes=[b_r])
        return agT[:, 2 * i:2 * i + 1], c_r, b_ag, b_r

    for t_, src, bn in [(ident, ident_in, "ident"), (ones_bf, ones_bf_in, "ones"),
                        (onesf, onesf_in, "onesf"), (arb, arb_in, "arb"),
                        (kdm, kdm_in, "kdm"), (mtt, mtt_in, "mtt"),
                        (slr, sl_row, "slr"), (gbr, gb_row, "gbr"),
                        (dsr, dist_in, "dsr"), (nwc, nw_pc, "nwc")]:
        p.dma("sp", t_[:, :], src, writes=[B(bn)])
    p.dma("pool", gw_sb[:, :].rearrange("p (f e) -> p f e", f=16),
          gwT.rearrange("(f p) e -> p f e", p=128), writes=[B("gw_sb")])

    p.mm_group(pbb[0], [
        (lambda e: e.matmul(PB[0][:, 0:NH], onesf[0:1, :], slr[0:1, :],
                            start=True, stop=True), [B("onesf"), B("slr")]),
        (lambda e: e.matmul(PB[0][:, 32:32 + NE], onesf[0:1, :], gbr[0:1, :],
                            start=True, stop=True), [B("onesf"), B("gbr")]),
        (lambda e: e.matmul(PB[0][:, 64:72], onesf[0:1, :], dsr[0:1, :],
                            start=True, stop=True), [B("onesf"), B("dsr")]),
    ])
    p.comp("dve", lambda e: e.tensor_copy(s_bc[:, :], PB[0][:, 0:NH]),
           reads=[pbb[0]], writes=[B("s_bc")])
    p.comp("dve", lambda e: e.tensor_copy(gb_bc[:, :], PB[0][:, 32:32 + NE]),
           reads=[pbb[0]], writes=[B("gb_bc")])
    p.comp("dve", lambda e: e.tensor_scalar_mul(distb[:, :], PB[0][:, 64:72], -2048.0),
           reads=[pbb[0]], writes=[B("distb")])
    p.comp("dve", lambda e: e.tensor_scalar_mul(negs[:, :], s_bc[:, :], -1.0),
           reads=[B("s_bc")], writes=[B("negs")])
    p.comp("act", lambda e: e.activation(lam[:, :], s_bc[:, :], AF.Exp, scale=-256.0),
           reads=[B("s_bc")], writes=[B("lam")])
    for i in range(8):
        p.comp("act", (lambda ii: lambda e: e.activation(
            wkv[:, ii * NH:(ii + 1) * NH], s_bc[:, :], AF.Exp,
            scale=distb[:, ii:ii + 1]))(i),
            reads=[B("s_bc"), B("distb")], writes=[B("wkv")])
    for j in range(16):
        p.comp("dve", (lambda jj: lambda e: e.tensor_copy(
            gb_rep[:, jj * NE:(jj + 1) * NE], gb_bc[:, :]))(j),
            reads=[B("gb_bc")], writes=[B("gb_rep")])
    for h in range(NH):
        p.comp("act", (lambda hh: lambda e: e.activation(
            dtt[:, hh * 512:(hh + 1) * 512], mtt[:, :], AF.Exp,
            scale=negs[:, hh:hh + 1]))(h),
            reads=[B("mtt"), B("negs")], writes=[dttb])
        p.comp("act", (lambda hh: lambda e: e.activation(
            qdb[:, hh * 256:(hh + 1) * 256], arb[:, :], AF.Exp,
            scale=negs[:, hh:hh + 1]))(h),
            reads=[B("arb"), B("negs")], writes=[qdbb])
        p.comp("act", (lambda hh: lambda e: e.activation(
            kdc[:, hh * 2:(hh + 1) * 2], kdm[:, :], AF.Exp,
            scale=negs[:, hh:hh + 1]))(h),
            reads=[B("kdm"), B("negs")], writes=[B("kdc")])
    tap("wkv", wkv[:, :], [128, 128], F32, reads=[B("wkv")])

    def finish():
        if stage_limit < 10:
            sink = sb("sinkS", [1, 64], F32)
            skb = B("sink")
            for ap in [x_in[0:1, 0:64], wqkvT[0:1, 0:64], wgT[0:1, 0:64],
                       woT[0:1, 0:64], fc1T[0:1, 0:64], fc2T[0:1, 0:64],
                       supT[0:1, 0:64], sdnT[0:1, 0:64], w1T[0, 0:1, 0:64],
                       w2T[0, 0:1, 0:64]]:
                p.dma("sp", sink[:, :], ap, reads=[skb], writes=[skb])
        p.drain_queues("sp")
        p.verify()
        p.replay()
        return nc

    if stage_limit < 1:
        return finish()

    lnA, lnAb = fA(0, 65536, BF16, "ln1fm")
    lnfm = lnA.rearrange("p (f t) -> p f t", f=16)
    XT0, XT0b = fX(0, 8192, F32, "xt0")
    XT1, XT1b = fX(8192, 8192, F32, "xt1")
    SQS, SQSb = fX(16384, 8192, F32, "sqs")
    LNT, LNTb = fX(24576, 4096, BF16, "lnt")
    XTl = [(XT0, XT0b), (XT1, XT1b)]

    def transpose_tile(src_ap, src_b, dst_view, dst_b, m, also_w=None):
        for half in range(2):
            mms = []
            for c in range(8):
                cc = half * 8 + c
                mms.append((
                    (lambda _c=cc, _o=c: lambda e: e.transpose(
                        PTR[:, _o * 128:(_o + 1) * 128],
                        src_ap[:, _c * 128:(_c + 1) * 128], ident[:, :]))(),
                    [src_b, B("ident")]))
            p.mm_group(ptrb, mms)
            ev = p.comp("dve", (lambda _m=m, _h=half: lambda e: e.tensor_copy(
                dst_view[:, _h * 8:(_h + 1) * 8, _m * 128:(_m + 1) * 128],
                PTR[:, :].rearrange("p (f t) -> p f t", f=8)))(),
                reads=[ptrb], writes=[dst_b])
            if also_w is not None:
                also_w.note_w(*ev)

    lnCh = [Buf(f"lnch{m}") for m in range(NT)]
    for m in range(NT):
        xb, xbb = XTl[m % 2]
        p.dma("sp", xb, x_in[m * 128:(m + 1) * 128, :], writes=[xbb])
        cm, cr, bm, br = ln_stats(xb, xbb, SQS, SQSb, EPS_LN)
        p.comp("dve", (lambda _x=xb, _m=cm, _r=cr: lambda e: e.tensor_scalar(
            LNT, _x, _m, _r, OP.subtract, OP.mult))(),
            reads=[xbb, bm, br], writes=[LNTb])
        transpose_tile(LNT, LNTb, lnfm, lnCh[m], m, also_w=lnAb)
    tap("ln1fm", lnA[:, 0:2048], [128, 2048], BF16, reads=[lnAb])

    if stage_limit < 2:
        return finish()

    wslice = [None, None, None, None]

    def load_wslice(src_ap, tag, s):
        wap, _junk = fW(0, 65536, BF16, tag + f"_s{s}") if False else (BIGW, None)
        prior = [wslice[s]] if wslice[s] is not None else []
        gb_ = Buf(tag + f"_s{s}", prior)
        wv_ = BIGW[:, :].rearrange("p (f n) -> p f n", f=16)
        p.dma("pool", wv_[:, :, 512 * s:512 * (s + 1)],
              src_ap[:, 512 * s:512 * (s + 1)].rearrange("(f p) n -> p f n", p=128),
              writes=[gb_])
        wslice[s] = gb_
        return gb_

    def load_wfull(src_ap, tag):
        wv_ = BIGW[:, :].rearrange("p (f n) -> p f n", f=16)
        sbufs = [load_wslice(src_ap, tag, s) for s in range(4)]
        return wv_, (None, sbufs)

    def wslice_into_flex(owner=None):
        if owner is None:
            owner = Buf("wslices_owner", [b for b in wslice if b is not None])
        flex.reg.setdefault("W", []).append((0, 65536, owner))

    def tm_pass(wv_, wbt_, dst_s, dst_db, chbufs, next_load=None):
        _junk, wgb_ = wbt_
        for s in range(4):
            for m in range(NT):
                vc, vcb = chbufs[m % 2]
                bank = (2 * (m % 2) + (s % 2)) % 4
                mms = []
                for f in range(16):
                    mms.append((
                        (lambda _f=f, _m=m, _s=s, _bk=bank: lambda e: e.matmul(
                            PB[_bk][:, 0:512],
                            lnfm[:, _f, _m * 128:(_m + 1) * 128],
                            wv_[:, _f, _s * 512:(_s + 1) * 512],
                            start=(_f == 0), stop=(_f == 15)))(),
                        [lnCh[m], wgb_[s]]))
                p.mm_group(pbb[bank], mms)
                p.comp("act", (lambda _s=s, _bk=bank, _v=vc: lambda e: e.activation(
                    _v[:, 0:512], PB[_bk][:, 0:512], AF.Silu))(),
                    reads=[pbb[bank]], writes=[vcb])
                p.dma("sp", dst_s[m, :, 512 * s:512 * (s + 1)], vc[:, 0:512],
                      reads=[vcb], writes=[dst_db])
            if next_load is not None:
                next_load(s)

    VCH0, VCH0b = fX(0, 2048, BF16, "vch0")
    VCH1, VCH1b = fX(2048, 2048, BF16, "vch1")
    wv, wvb = load_wfull(wqkvT[:, 2 * D:3 * D], "w_v")
    tm_pass(wv, wvb, v_tm_s, B("v_tm_s"), [(VCH0, VCH0b), (VCH1, VCH1b)],
            next_load=lambda s: load_wslice(wqkvT[:, D:2 * D], "w_k", s))

    wk = BIGW[:, :].rearrange("p (f n) -> p f n", f=16)
    wkb = (None, [wslice[s] for s in range(4)])
    KCH0, KCH0b = fX(0, 2048, BF16, "kch0")
    KCH1, KCH1b = fX(2048, 2048, BF16, "kch1")
    tm_pass(wk, wkb, k_tm_s, B("k_tm_s"), [(KCH0, KCH0b), (KCH1, KCH1b)],
            next_load=lambda s: load_wslice(wqkvT[:, D:2 * D], "w_k2", s))

    KRA = [fX(8192, 8192, BF16, "kra0")]
    VRA = [fX(16384, 8192, BF16, "vra0")]
    SALL, SALLb = fX(24576, 8192, F32, "sall")
    p.comp("dve", lambda e: e.memset(SALL, 0.0), writes=[SALLb])

    def passA_load(i):
        kra, krab = KRA[0]
        vra, vrab = VRA[0]
        p.dma("pool", kra.rearrange("p (c n) -> p c n", c=2),
              k_tm_s.rearrange("m p n -> p m n")[:, 2 * i:2 * i + 2, :],
              reads=[B("k_tm_s")], writes=[krab])
        p.dma("pool", vra.rearrange("p (c n) -> p c n", c=2),
              v_tm_s.rearrange("m p n -> p m n")[:, 2 * i:2 * i + 2, :],
              reads=[B("v_tm_s")], writes=[vrab])

    def passA_block(i):
        kra, krab = KRA[0]
        vra, vrab = VRA[0]
        for cc_ in range(2):
            for h in range(NH):
                p.comp("dve", (lambda _h=h, _c=cc_, _v=vra:
                               lambda e: e.tensor_scalar_mul(
                    _v[:, _c * 2048 + _h * 128:_c * 2048 + (_h + 1) * 128],
                    _v[:, _c * 2048 + _h * 128:_c * 2048 + (_h + 1) * 128],
                    kdc[:, 2 * _h + _c:2 * _h + _c + 1]))(),
                    reads=[vrab, B("kdc")], writes=[vrab])
        for h in range(NH):
            bank = 4 + (h % 2)
            mms = []
            for cc_ in range(2):
                mms.append((
                    (lambda _h=h, _c=cc_, _k=kra, _V=vra, _bk=bank:
                     lambda e: e.matmul(
                        PB[_bk][:, 0:128],
                        _k[:, _c * 2048 + _h * 128:_c * 2048 + (_h + 1) * 128],
                        _V[:, _c * 2048 + _h * 128:_c * 2048 + (_h + 1) * 128],
                        start=(_c == 0), stop=(_c == 1)))(),
                    [krab, vrab]))
            p.mm_group(pbb[bank], mms)
            p.comp("dve", (lambda _h=h, _bk=bank: lambda e: e.scalar_tensor_tensor(
                SALL[:, _h * 128:(_h + 1) * 128], SALL[:, _h * 128:(_h + 1) * 128],
                lam[:, _h:_h + 1], PB[_bk][:, 0:128], OP.mult, OP.add))(),
                reads=[SALLb, B("lam"), pbb[bank]], writes=[SALLb])

    STG0, STG0b = fX(0, 4096, BF16, "stg0")
    STG1, STG1b = fX(4096, 4096, BF16, "stg1")
    STG = [(STG0, STG0b), (STG1, STG1b)]

    def fm_pass(wv_, wbt_, dst_s, dst_db, actf, next_load=None, on_head_pre=None):
        _junk, wgb_ = wbt_
        for h in range(NH):
            if on_head_pre is not None:
                on_head_pre(h)
            st, stb = STG[h % 2]
            for s in range(4):
                bank = (2 * (h % 2) + (s % 2)) % 4
                mms = []
                for f in range(16):
                    mms.append((
                        (lambda _f=f, _h=h, _s=s, _bk=bank: lambda e: e.matmul(
                            PB[_bk][:, 0:512],
                            wv_[:, _f, _h * 128:(_h + 1) * 128],
                            lnfm[:, _f, _s * 512:(_s + 1) * 512],
                            start=(_f == 0), stop=(_f == 15)))(),
                        [lnCh[4 * s], lnCh[4 * s + 1], lnCh[4 * s + 2],
                         lnCh[4 * s + 3], wgb_[h // 4]]))
                p.mm_group(pbb[bank], mms)
                p.comp("act", (lambda _s=s, _bk=bank, _t=st, _a=actf: lambda e: e.activation(
                    _t[:, _s * 512:(_s + 1) * 512], PB[_bk][:, 0:512], _a))(),
                    reads=[pbb[bank]], writes=[stb])
            if dst_s.shape[0] == NB:
                p.dma("sp", dst_s[:, :, h * 256:(h + 1) * 256].rearrange("b p t -> p b t"),
                      st.rearrange("p (b t) -> p b t", b=NB),
                      reads=[stb], writes=[dst_db])
            else:
                p.dma("sp", dst_s[h, :, :], st, reads=[stb], writes=[dst_db])
            if next_load is not None and h % 4 == 3:
                next_load(h // 4)


    wk2 = BIGW[:, :].rearrange("p (f n) -> p f n", f=16)
    wk2b = (None, [wslice[s] for s in range(4)])
    def kfm_head_hook(h):
        if h % 2 == 0:
            i = h // 2
            passA_load(i)
            passA_block(i)

    fm_pass(wk2, wk2b, k_fm_s, B("k_fm_s"), AF.Silu,
            next_load=lambda s: load_wslice(wqkvT[:, 0:D], "w_q", s),
            on_head_pre=kfm_head_hook)

    p.dma("sp", S_loc.rearrange("(h d) e -> d h e", d=128),
          SALL.rearrange("p (h e) -> p h e", h=NH), reads=[SALLb], writes=[B("S_loc")])
    p.comp("pool", lambda e: e.collective_compute(
        "AllGather", OP.bypass, replica_groups=[list(range(8))],
        ins=[S_loc.opt()], outs=[S_gth.opt()]),
        reads=[B("S_loc")], writes=[B("S_gth")])
    tap("sall", SALL, [128, 2048], F32, reads=[SALLb])

    if stage_limit < 3:
        return finish()

    wq = BIGW[:, :].rearrange("p (f n) -> p f n", f=16)
    wqb = (None, [wslice[s] for s in range(4)])
    fm_pass(wq, wqb, q_fm_s, B("q_fm_s"), AF.Silu,
            next_load=lambda s: load_wslice(wgT, "w_g", s))
    wg = BIGW[:, :].rearrange("p (f n) -> p f n", f=16)
    wgb = (None, [wslice[s] for s in range(4)])
    fm_pass(wg, wgb, g_s, B("g_s"), AF.Sigmoid)

    wo = BIGW[:, :].rearrange("p (f n) -> p f n", f=16)
    for s in range(4):
        load_wslice(woT, "w_o", s)
    wob = Buf("w_o_all", [wslice[s] for s in range(4)])
    for f in range(16):
        p.comp("dve", (lambda _f=f: lambda e: e.tensor_scalar_mul(
            wo[:, _f, :], wo[:, _f, :], nwc[:, _f:_f + 1]))(),
            reads=[wob, B("nwc")], writes=[wob])
    wslice_into_flex(owner=wob)
    tap("gfm", g_s[0, :, :], [128, T], BF16, reads=[B("g_s")])

    if stage_limit < 4:
        return finish()

    KVST, KVSTb = fX(8192, 8192, F32, "kvst")
    KVG, KVGb = fX(16384, 4096, F32, "kvg")
    sg4 = S_gth.rearrange("(r c p) e -> p r c e", r=8, c=16)
    for h in range(NH):
        p.dma("sp", KVG.rearrange("p (r e) -> p r e", r=8), sg4[:, :, h, :],
              reads=[B("S_gth")], writes=[KVGb])
        p.comp("dve", (lambda _h=h: lambda e: e.tensor_scalar_mul(
            KVST[:, _h * 128:(_h + 1) * 128], KVG[:, 0:128],
            wkv[:, _h:_h + 1]))(),
            reads=[KVGb, B("wkv")], writes=[KVSTb])
        for i in range(1, 8):
            p.comp("dve", (lambda _h=h, _i=i: lambda e: e.scalar_tensor_tensor(
                KVST[:, _h * 128:(_h + 1) * 128], KVG[:, _i * 128:(_i + 1) * 128],
                wkv[:, _i * NH + _h:_i * NH + _h + 1],
                KVST[:, _h * 128:(_h + 1) * 128], OP.mult, OP.add))(),
                reads=[KVGb, B("wkv"), KVSTb], writes=[KVSTb])
    tap("kv0", KVST, [128, 2048], F32, reads=[KVSTb])

    ROWS, ROWSb = fX(24576, 8192, F32, "rows")
    ssrow = ROWS[0:1, :]
    rsrow = ROWS[32:33, :]
    RSBF, RSBFb = fX(32768, 4096, BF16, "rsbf")
    rsbf = RSBF[0:1, :]
    QBs = [fA(0, 8192, BF16, "qb0"), fA(8192, 8192, BF16, "qb1")]
    KBs = [fA(16384, 8192, BF16, "kb0"), fA(24576, 8192, BF16, "kb1")]
    KTBs = [fA(32768, 8192, BF16, "ktb0"), fA(40960, 8192, BF16, "ktb1")]
    VBs = [fA(49152, 8192, BF16, "vb0"), fA(57344, 8192, BF16, "vb1")]
    ATTS, ATTSb = fX(0, 8192, BF16, "atts")
    STBs = [fX(20480, 1024, BF16, "stb0"), fX(21504, 1024, BF16, "stb1")]
    QQs = [fX(22528, 512, BF16, "qq0"), fX(23040, 512, BF16, "qq1")]
    SQG, SQGb = fX(23552, 512, BF16, "sqg")
    KVBFs = [fX(16384, 256, BF16, "kvbf0"), fX(16640, 256, BF16, "kvbf1")]
    VKDPd = [[fX(16896, 256, BF16, "vkdp00"), fX(17152, 256, BF16, "vkdp01")],
             [fX(17408, 256, BF16, "vkdp10"), fX(17664, 256, BF16, "vkdp11")]]
    GBs = [fX(17920, 512, BF16, "gb0"), fX(18432, 512, BF16, "gb1")]

    def loadblk(i, par):
        p.dma("sp", QBs[par][0], q_fm_s[i, :, :],
              reads=[B("q_fm_s")], writes=[QBs[par][1]])
        p.dma("sp", KBs[par][0], k_fm_s[i, :, :],
              reads=[B("k_fm_s")], writes=[KBs[par][1]])
        p.dma("sp", KTBs[par][0].rearrange("p (c n) -> p c n", c=2),
              k_tm_s.rearrange("m p n -> p m n")[:, 2 * i:2 * i + 2, :],
              reads=[B("k_tm_s")], writes=[KTBs[par][1]])
        p.dma("sp", VBs[par][0].rearrange("p (c n) -> p c n", c=2),
              v_tm_s.rearrange("m p n -> p m n")[:, 2 * i:2 * i + 2, :],
              reads=[B("v_tm_s")], writes=[VBs[par][1]])

    loadblk(0, 0)
    for i in range(NB):
        par = i % 2
        if i + 1 < NB:
            loadblk(i + 1, (i + 1) % 2)
        qb, qbb = QBs[par]
        kb, kbb = KBs[par]
        ktb, ktbb = KTBs[par]
        vb, vbb = VBs[par]
        ssbank = 2
        for h in range(NH):
            pbP = 5 if h % 2 == 0 else 3
            pbO = 6 if h % 2 == 0 else 0
            pbC = 4 if h % 2 == 0 else 1
            KVBF, KVBFb = KVBFs[h % 2]
            QQ, QQb = QQs[h % 2]
            STB, STBb = STBs[h % 2]
            VKDP = VKDPd[h % 2]
            p.comp("act", (lambda _h=h, _K=KVBF: lambda e: e.activation(
                _K, KVST[:, _h * 128:(_h + 1) * 128], AF.Copy, bias=0.0))(),
                reads=[KVSTb], writes=[KVBFb])
            p.comp("dve", (lambda _h=h, _q=qb, _Q=QQ: lambda e: e.tensor_mul(
                _Q, _q[:, _h * 256:(_h + 1) * 256],
                qdb[:, _h * 256:(_h + 1) * 256]))(),
                reads=[qbb, qdbb], writes=[QQb])
            p.mm_group(pbb[pbP], [
                ((lambda _h=h, _c=c, _k=kb, _q=qb, _p=pbP: lambda e: e.matmul(
                    PB[_p][:, _c * 256:(_c + 1) * 256],
                    _k[:, _h * 256 + _c * 128:_h * 256 + (_c + 1) * 128],
                    _q[:, _h * 256:(_h + 1) * 256],
                    start=True, stop=True))(), [kbb, qbb])
                for c in range(2)])
            p.comp("dve", (lambda _h=h, _S=STB, _p=pbP: lambda e: e.tensor_tensor(
                _S, PB[_p][:, 0:512], dtt[:, _h * 512:(_h + 1) * 512], OP.mult))(),
                reads=[pbb[pbP], dttb], writes=[STBb])
            p.mm_group(pbb[pbO], [
                ((lambda _h=h, _v=vb, _S=STB, _p=pbO: lambda e: e.matmul(
                    PB[_p][:, 0:256], _v[:, _h * 128:(_h + 1) * 128],
                    _S[:, 0:256], start=True, stop=False))(), [vbb, STBb]),
                ((lambda _h=h, _v=vb, _S=STB, _p=pbO: lambda e: e.matmul(
                    PB[_p][:, 0:256], _v[:, 2048 + _h * 128:2048 + (_h + 1) * 128],
                    _S[:, 256:512], start=False, stop=False))(), [vbb, STBb]),
                ((lambda _K=KVBF, _Q=QQ, _p=pbO: lambda e: e.matmul(
                    PB[_p][:, 0:256], _K, _Q, start=False, stop=True))(),
                 [KVBFb, QQb]),
            ])
            p.comp("act", (lambda _h=h, _p=pbO: lambda e: e.activation(
                ATTS[:, _h * 256:(_h + 1) * 256], PB[_p][:, 0:256], AF.Copy,
                bias=0.0))(),
                reads=[pbb[pbO]], writes=[ATTSb])
            p.comp("act", (lambda _h=h: lambda e: e.activation(
                SQG[:, 0:256], ATTS[:, _h * 256:(_h + 1) * 256], AF.Square))(),
                reads=[ATTSb], writes=[SQGb])
            gbuf, gbufb = GBs[h % 2]
            p.dma("act", gbuf, g_s[h, :, 256 * i:256 * (i + 1)],
                  reads=[B("g_s")], writes=[gbufb])
            p.comp("pool", (lambda _h=h, _g=gbuf: lambda e: e.tensor_mul(
                ATTS[:, _h * 256:(_h + 1) * 256],
                ATTS[:, _h * 256:(_h + 1) * 256], _g))(),
                reads=[ATTSb, gbufb], writes=[ATTSb])
            p.mm_group(pbb[ssbank], [
                ((lambda _h=h, _b=ssbank: lambda e: e.matmul(
                    PB[_b][:, 0:256], ones_bf[:, :], SQG[:, 0:256],
                    start=(_h == 0), stop=(_h == NH - 1)))(), [B("ones"), SQGb])])
            for cc_ in range(2):
                p.comp("dve", (lambda _h=h, _c=cc_, _v=vb, _V=VKDP[cc_][0]:
                               lambda e: e.tensor_scalar_mul(
                    _V, _v[:, _c * 2048 + _h * 128:_c * 2048 + (_h + 1) * 128],
                    kdc[:, 2 * _h + _c:2 * _h + _c + 1]))(),
                    reads=[vbb, B("kdc")], writes=[VKDP[cc_][1]])
            p.mm_group(pbb[pbC], [
                ((lambda _h=h, _c=c, _k=ktb, _V=VKDP[c][0], _p=pbC: lambda e: e.matmul(
                    PB[_p][:, 0:128],
                    _k[:, _c * 2048 + _h * 128:_c * 2048 + (_h + 1) * 128],
                    _V, start=(_c == 0), stop=(_c == 1)))(),
                 [ktbb, VKDP[c][1]])
                for c in range(2)])
            p.comp("dve", (lambda _h=h, _p=pbC: lambda e: e.scalar_tensor_tensor(
                KVST[:, _h * 128:(_h + 1) * 128], KVST[:, _h * 128:(_h + 1) * 128],
                lam[:, _h:_h + 1], PB[_p][:, 0:128], OP.mult, OP.add))(),
                reads=[KVSTb, B("lam"), pbb[pbC]], writes=[KVSTb])
        p.comp("dve", (lambda _i=i, _b=ssbank: lambda e: e.tensor_copy(
            ssrow[:, 256 * _i:256 * (_i + 1)], PB[_b][0:1, 0:256]))(),
            reads=[pbb[ssbank]], writes=[ROWSb])
        for cc_ in range(2):
            p.dma("sp", attn_s[2 * i + cc_, :, :].rearrange("p (h t) -> p h t", h=NH),
                  ATTS.rearrange("p (h c t) -> p c h t", h=NH, c=2)[:, cc_, :, :],
                  reads=[ATTSb], writes=[B("attn_s")])
    tap("attnc0", attn_s[0, :, :], [128, NH * 128], BF16, reads=[B("attn_s")])

    p.comp("dve", lambda e: e.tensor_scalar(rsrow[:, :], ssrow[:, :],
                                            float(EPS_LN / D),
                                            float(EPS_RMS * EPS_LN),
                                            OP.mult, OP.add),
           reads=[ROWSb], writes=[ROWSb])
    p.comp("dve", lambda e: e.tensor_copy(rsbf[:, :], rsrow[:, :]),
           reads=[ROWSb], writes=[RSBFb])
    p.mm_group(pbb[4], [
        ((lambda _m=m: lambda e: e.matmul(
            PB[4][:, _m:_m + 1], rsbf[:, _m * 128:(_m + 1) * 128],
            ones_bf[0:1, 0:1], start=True, stop=True))(), [RSBFb, B("ones")])
        for m in range(NT)])
    p.comp("dve", lambda e: e.tensor_copy(rstdc[:, :], PB[4][:, 0:16]),
           reads=[pbb[4]], writes=[B("rstdc")])
    tap("rstdc", rstdc[:, :], [128, 16], F32, reads=[B("rstdc")])

    if stage_limit < 5:
        return finish()

    OSC, OSCb = fX(0, 8192, F32, "osc")
    XADJ, XADJb = fX(8192, 8192, F32, "xadj")
    XP, XPb = fX(16384, 8192, F32, "xp")
    XT4, XT4b = fX(24576, 8192, F32, "xt4")
    ATC0, ATC0b = fX(40960, 4096, BF16, "atc0")
    ATC1, ATC1b = fX(45056, 4096, BF16, "atc1")
    ATC = [(ATC0, ATC0b), (ATC1, ATC1b)]
    GTC, GTCb = fX(49152, 4096, BF16, "gtc")
    LN2T, LN2Tb = fX(57344, 4096, BF16, "ln2t")

    ln2A, ln2Ab = fA(0, 65536, BF16, "ln2fm")
    lnfm2 = ln2A.rearrange("p (f t) -> p f t", f=16)

    ATC2, ATC2b = fX(49152, 4096, BF16, "atc2")
    ATC3, ATC3b = fX(53248, 4096, BF16, "atc3")
    ATC = [ATC[0], ATC[1], (ATC2, ATC2b), (ATC3, ATC3b)]
    LN2T1, LN2T1b = fX(32768, 4096, BF16, "ln2t1")
    LN2Ts = [(LN2T, LN2Tb), (LN2T1, LN2T1b)]

    def s4_loads(m):
        atc, atcb = ATC[m % 4]
        p.dma("sp", atc, attn_s[m, :, :], reads=[B("attn_s")], writes=[atcb])

    def s4_main(m):
        atc, atcb = ATC[m % 4]
        ln2t, ln2tb = LN2Ts[m % 2]
        p.dma("sp", XT4, x_in[m * 128:(m + 1) * 128, :], writes=[XT4b])
        for s in range(4):
            mms = []
            for f in range(16):
                mms.append((
                    (lambda _f=f, _s=s, _a=atc: lambda e: e.matmul(
                        PB[_s][:, 0:512], _a[:, _f * 128:(_f + 1) * 128],
                        wo[:, _f, _s * 512:(_s + 1) * 512],
                        start=(_f == 0), stop=(_f == 15)))(),
                    [atcb, wob]))
            p.mm_group(pbb[s], mms)
        for s in range(4):
            p.comp("dve", (lambda _s=s: lambda e: e.tensor_copy(
                OSC[:, _s * 512:(_s + 1) * 512], PB[_s][:, 0:512]))(),
                reads=[pbb[s]], writes=[OSCb])
        cm, cr, bm, br = ln_stats(OSC, OSCb, XP, XPb, EPS_LN,
                                  eps_extra=(rstdc[:, m:m + 1], B("rstdc")))
        p.comp("dve", (lambda _m=cm, _r=cr: lambda e: e.scalar_tensor_tensor(
            colE[:, 0:1], _m, -1.0, _r, OP.mult, OP.mult))(),
            reads=[bm, br], writes=[B("cE0")])
        p.comp("dve", lambda e: e.tensor_scalar_add(XADJ, XT4, colE[:, 0:1]),
               reads=[XT4b, B("cE0")], writes=[XADJb])
        p.comp("dve", (lambda _r=cr: lambda e: e.scalar_tensor_tensor(
            XP, OSC, _r, XADJ, OP.mult, OP.add))(),
            reads=[OSCb, br, XADJb], writes=[XPb])
        p.dma("sp", x1_s[m * 128:(m + 1) * 128, :], XP, reads=[XPb],
              writes=[B("x1_s")])
        cm, cr, bm, br = ln_stats(XP, XPb, XADJ, XADJb, EPS_LN)
        p.comp("dve", (lambda _m=cm, _r=cr, _t=ln2t: lambda e: e.tensor_scalar(
            _t, XP, _m, _r, OP.subtract, OP.mult))(),
            reads=[XPb, bm, br], writes=[ln2tb])

    def s4_tail(m):
        ln2t, ln2tb = LN2Ts[m % 2]
        transpose_tile(ln2t, ln2tb, lnfm2, ln2Ab, m)
        mms = []
        for f in range(16):
            mms.append((
                (lambda _f=f, _m=m: lambda e: e.matmul(
                    PB[4][:, _m * 32:(_m + 1) * 32],
                    lnfm2[:, _f, _m * 128:(_m + 1) * 128],
                    gw_sb[:, _f * 32:(_f + 1) * 32],
                    start=(_f == 0), stop=(_f == 15)))(),
                [ln2Ab, B("gw_sb")]))
        p.mm_group(pbb[4], mms)

    s4_loads(0)
    s4_loads(1)
    s4_loads(2)
    for m in range(NT):
        if m + 3 < NT:
            s4_loads(m + 3)
        s4_main(m)
        if m >= 1:
            s4_tail(m - 1)
    s4_tail(NT - 1)
    tap("ln2fm", ln2A[:, 0:2048], [128, 2048], BF16, reads=[ln2Ab])
    tap("x1", x1_s[0:128, :], [128, D], F32, reads=[B("x1_s")])

    if stage_limit < 6:
        return finish()

    SC, SCb = fX(0, 2048, F32, "sc5")
    SB5, SB5b = fX(2048, 2048, F32, "sb5")
    WK1, WK1b = fX(4096, 2048, F32, "wk1")
    MK, MKb = fX(6144, 2048, F32, "mk5")
    CUR, CURb = fX(8192, 2048, F32, "cur5")
    T15, T15b = fX(10240, 2048, F32, "t15")
    M1, M1b = fX(12288, 256, F32, "m15")
    M2, M2b = fX(12544, 256, F32, "m25")
    GS, GSb = fX(12800, 256, F32, "gs5")
    G1, G1b = fX(13056, 64, F32, "g15")
    G2, G2b = fX(13120, 64, F32, "g25")
    TH, THb = fX(13184, 64, F32, "th5")
    SSUM, SSUMb = fX(13248, 64, F32, "ssum5")

    def v3(ap, a, b_):
        return ap.rearrange("p (a b) -> p a b", a=a, b=b_)

    def vcol(ap, a):
        return ap.rearrange("p (a o) -> p a o", a=a, o=1)

    p.comp("act", lambda e: e.activation(SC, PB[4][:, 0:512], AF.Sigmoid),
           reads=[pbb[4]], writes=[SCb])
    p.comp("dve", lambda e: e.tensor_tensor(SB5, SC, gb_rep[:, :], OP.add),
           reads=[SCb, B("gb_rep")], writes=[SB5b])
    p.comp("dve", lambda e: e.tensor_reduce(vcol(M1, 64), v3(SB5, 64, 8), AX.X, OP.max),
           reads=[SB5b], writes=[M1b])
    p.comp("dve", lambda e: e.tensor_tensor(
        v3(WK1, 64, 8), v3(SB5, 64, 8), vcol(M1, 64).broadcast_to((128, 64, 8)),
        OP.is_ge), reads=[SB5b, M1b], writes=[WK1b])
    p.comp("dve", lambda e: e.scalar_tensor_tensor(WK1, WK1, -BIGR, SB5, OP.mult, OP.add),
           reads=[WK1b, SB5b], writes=[WK1b])
    p.comp("dve", lambda e: e.tensor_reduce(vcol(M2, 64), v3(WK1, 64, 8), AX.X, OP.max),
           reads=[WK1b], writes=[M2b])
    p.comp("dve", lambda e: e.tensor_tensor(GS[:, 0:64], M1[:, 0:64], M2[:, 0:64], OP.add),
           reads=[M1b, M2b], writes=[GSb])
    p.comp("dve", lambda e: e.tensor_reduce(vcol(G1, 16), v3(GS[:, 0:64], 16, 4), AX.X, OP.max),
           reads=[GSb], writes=[G1b])
    p.comp("dve", lambda e: e.tensor_tensor(
        v3(M1[:, 0:64], 16, 4), v3(GS[:, 0:64], 16, 4),
        vcol(G1, 16).broadcast_to((128, 16, 4)), OP.is_ge),
        reads=[GSb, G1b], writes=[M1b])
    p.comp("dve", lambda e: e.scalar_tensor_tensor(
        M1[:, 0:64], M1[:, 0:64], -BIGR, GS[:, 0:64], OP.mult, OP.add),
        reads=[M1b, GSb], writes=[M1b])
    p.comp("dve", lambda e: e.tensor_reduce(vcol(G2, 16), v3(M1[:, 0:64], 16, 4), AX.X, OP.max),
           reads=[M1b], writes=[G2b])
    p.comp("dve", lambda e: e.tensor_tensor(
        v3(M2[:, 0:64], 16, 4), v3(GS[:, 0:64], 16, 4),
        vcol(G2, 16).broadcast_to((128, 16, 4)), OP.is_ge),
        reads=[GSb, G2b], writes=[M2b])
    p.comp("dve", lambda e: e.tensor_scalar(M2[:, 0:64], M2[:, 0:64], BIGR, -BIGR,
                                            OP.mult, OP.add),
           reads=[M2b], writes=[M2b])
    p.comp("dve", lambda e: e.tensor_tensor(
        v3(MK, 64, 8), v3(SB5, 64, 8), vcol(M2[:, 0:64], 64).broadcast_to((128, 64, 8)),
        OP.add), reads=[SB5b, M2b], writes=[MKb])
    p.comp("dve", lambda e: e.tensor_copy(CUR, MK), reads=[MKb], writes=[CURb])
    for t_ in range(8):
        p.comp("dve", lambda e: e.tensor_reduce(vcol(TH, 16), v3(CUR, 16, 32), AX.X, OP.max),
               reads=[CURb], writes=[THb])
        if t_ < 7:
            p.comp("dve", lambda e: e.tensor_tensor(
                v3(T15, 16, 32), v3(CUR, 16, 32), vcol(TH, 16).broadcast_to((128, 16, 32)),
                OP.is_ge), reads=[CURb, THb], writes=[T15b])
            p.comp("dve", lambda e: e.scalar_tensor_tensor(CUR, T15, -BIGR, CUR,
                                                           OP.mult, OP.add),
                   reads=[T15b, CURb], writes=[CURb])
    p.comp("dve", lambda e: e.tensor_tensor(
        v3(T15, 16, 32), v3(MK, 16, 32), vcol(TH, 16).broadcast_to((128, 16, 32)),
        OP.is_ge), reads=[MKb, THb], writes=[T15b])
    p.comp("dve", lambda e: e.tensor_tensor(T15, T15, SC, OP.mult),
           reads=[T15b, SCb], writes=[T15b])
    p.comp("dve", lambda e: e.tensor_reduce(vcol(SSUM, 16), v3(T15, 16, 32), AX.X, OP.add),
           reads=[T15b], writes=[SSUMb])
    p.comp("dve", lambda e: e.reciprocal(SSUM[:, 0:16], SSUM[:, 0:16]),
           reads=[SSUMb], writes=[SSUMb])
    p.comp("dve", lambda e: e.tensor_scalar_mul(SSUM[:, 0:16], SSUM[:, 0:16], 2.5),
           reads=[SSUMb], writes=[SSUMb])
    p.comp("dve", lambda e: e.tensor_tensor(
        v3(cw[:, :], 16, 32), v3(T15, 16, 32), vcol(SSUM[:, 0:16], 16).broadcast_to((128, 16, 32)),
        OP.mult), reads=[T15b, SSUMb], writes=[B("cw")])
    tap("cw", cw[:, :], [128, 512], F32, reads=[B("cw")])

    if stage_limit < 7:
        return finish()

    FC1ap, FC1b = fW(0, 32768, BF16, "fc1")
    FC1 = FC1ap.rearrange("p (f l) -> p f l", f=16)
    for g in range(2):
        p.dma("pool", FC1[:, 8 * g:8 * (g + 1), :],
              fc1T.rearrange("(f p) l -> p f l", p=128)[:, 8 * g:8 * (g + 1), :],
              writes=[FC1b])
    XLTap, XLTb = fX(0, 32768, BF16, "xlt")
    XLT = XLTap.rearrange("p (l t) -> p l t", l=8)
    for lo in range(8):
        for s in range(4):
            bank = (2 * lo + s) % 4
            mms = []
            for f in range(16):
                mms.append((
                    (lambda _f=f, _lo=lo, _s=s, _bk=bank: lambda e: e.matmul(
                        PB[_bk][:, 0:512], FC1[:, _f, _lo * 128:(_lo + 1) * 128],
                        lnfm2[:, _f, _s * 512:(_s + 1) * 512],
                        start=(_f == 0), stop=(_f == 15)))(),
                    [FC1b, ln2Ab]))
            p.mm_group(pbb[bank], mms)
            p.comp("dve", (lambda _lo=lo, _s=s, _bk=bank: lambda e: e.tensor_copy(
                XLT[:, _lo, _s * 512:(_s + 1) * 512], PB[_bk][:, 0:512]))(),
                reads=[pbb[bank]], writes=[XLTb])
    tap("xlat", XLTap[:, 0:2048], [128, 2048], BF16, reads=[XLTb])

    SUPs = [fW(32768, 16384, BF16, "supb0"), fW(49152, 16384, BF16, "supb1")]
    RSXs = [fX(57344, 2048, F32, "rsx0"), fX(59392, 2048, F32, "rsx1")]
    MSTs = [fX(32768, 4096, BF16, "mst0"), fX(36864, 4096, BF16, "mst1")]
    for j in range(8):
        sj, sjb = SUPs[j % 2]
        sjv = sj.rearrange("p (f n) -> p f n", f=16)
        p.dma("pool", sjv[:, :, :],
              supT[:, 512 * j:512 * (j + 1)].rearrange("(f p) n -> p f n", p=128),
              writes=[sjb])
        for sub in range(4):
            mo = j * 4 + sub
            mstp, mstb = MSTs[mo % 2]
            for s in range(4):
                bank = (2 * sub + s) % 4
                mms = []
                for f in range(16):
                    mms.append((
                        (lambda _f=f, _sub=sub, _s=s, _bk=bank, _sj=sjv: lambda e: e.matmul(
                            PB[_bk][:, 0:512], _sj[:, _f, _sub * 128:(_sub + 1) * 128],
                            lnfm2[:, _f, _s * 512:(_s + 1) * 512],
                            start=(_f == 0), stop=(_f == 15)))(),
                        [sjb, ln2Ab]))
                p.mm_group(pbb[bank], mms)
                rs, rsb = RSXs[s % 2]
                p.comp("act", (lambda _bk=bank, _rs=rs: lambda e: e.activation(
                    _rs[:, 0:512], PB[_bk][:, 0:512], AF.Relu))(),
                    reads=[pbb[bank]], writes=[rsb])
                p.comp("act", (lambda _rs=rs, _mt=mstp, _s=s: lambda e: e.activation(
                    _mt[:, _s * 512:(_s + 1) * 512], _rs[:, 0:512], AF.Square))(),
                    reads=[rsb], writes=[mstb])
            p.dma("sp", mid_s[mo, :, :], mstp, reads=[mstb], writes=[B("mid_s")])
    tap("mid", mid_s[0, :, :], [128, T], BF16, reads=[B("mid_s")])

    if stage_limit < 8:
        return finish()

    W1ap, W1b_ = fW(0, 16384, BF16, "w1b")
    W1B = W1ap.rearrange("p (c i) -> p c i", c=8)
    W2ap, W2b_ = fW(16384, 16384, BF16, "w2b")
    W2B = W2ap.rearrange("p (c l) -> p c l", c=8)
    HSBap, HSBb = fW(32768, 32768, BF16, "hsb")
    HSB = HSBap.rearrange("p (c t) -> p c t", c=8)
    RELs = [fX(57344, 2048, F32, "rels0"), fX(59392, 2048, F32, "rels1")]
    YACCap, YACCb = fA(0, 65536, F32, "yacc")
    for e_ in range(NE):
        p.dma("pool", W1B[:, :, :], w1T[e_].rearrange("(c p) i -> p c i", p=128),
              writes=[W1b_])
        p.dma("pool", W2B[:, :, :], w2T[e_].rearrange("(c p) l -> p c l", p=128),
              writes=[W2b_])
        for ic in range(8):
            for s in range(4):
                bank = (2 * ic + s) % 4
                mms = []
                for lc in range(8):
                    mms.append((
                        (lambda _l=lc, _i=ic, _s=s, _bk=bank: lambda e: e.matmul(
                            PB[_bk][:, 0:512], W1B[:, _l, _i * 128:(_i + 1) * 128],
                            XLT[:, _l, _s * 512:(_s + 1) * 512],
                            start=(_l == 0), stop=(_l == 7)))(),
                        [W1b_, XLTb]))
                p.mm_group(pbb[bank], mms)
                rs, rsb = RELs[s % 2]
                p.comp("act", (lambda _bk=bank, _rs=rs: lambda e: e.activation(
                    _rs[:, 0:512], PB[_bk][:, 0:512], AF.Relu))(),
                    reads=[pbb[bank]], writes=[rsb])
                p.comp("act", (lambda _rs=rs, _i=ic, _s=s: lambda e: e.activation(
                    HSB[:, _i, _s * 512:(_s + 1) * 512], _rs[:, 0:512], AF.Square))(),
                    reads=[rsb], writes=[HSBb])
        for m in range(NT):
            for ls in range(2):
                bank = 4 + ((2 * m + ls) % 2)
                mms = []
                for ic in range(8):
                    mms.append((
                        (lambda _i=ic, _m=m, _ls=ls, _bk=bank: lambda e: e.matmul(
                            PB[_bk][:, 0:512], HSB[:, _i, _m * 128:(_m + 1) * 128],
                            W2B[:, _i, _ls * 512:(_ls + 1) * 512],
                            start=(_i == 0), stop=(_i == 7)))(),
                        [HSBb, W2b_]))
                p.mm_group(pbb[bank], mms)
                if e_ == 0:
                    p.comp("dve", (lambda _m=m, _ls=ls, _e=e_, _bk=bank:
                                   lambda e: e.tensor_scalar_mul(
                        YACCap[:, _m * 1024 + _ls * 512:_m * 1024 + (_ls + 1) * 512],
                        PB[_bk][:, 0:512], cw[:, _m * 32 + _e:_m * 32 + _e + 1]))(),
                        reads=[pbb[bank], B("cw")], writes=[YACCb])
                else:
                    p.comp("dve", (lambda _m=m, _ls=ls, _e=e_, _bk=bank:
                                   lambda e: e.scalar_tensor_tensor(
                        YACCap[:, _m * 1024 + _ls * 512:_m * 1024 + (_ls + 1) * 512],
                        PB[_bk][:, 0:512], cw[:, _m * 32 + _e:_m * 32 + _e + 1],
                        YACCap[:, _m * 1024 + _ls * 512:_m * 1024 + (_ls + 1) * 512],
                        OP.mult, OP.add))(),
                        reads=[pbb[bank], B("cw"), YACCb], writes=[YACCb])
    tap("yacc", YACCap[:, 0:2048], [128, 2048], F32, reads=[YACCb])

    if stage_limit < 9:
        return finish()

    YBFap, YBFb = fX(0, 32768, BF16, "ybf")
    for m in range(NT):
        p.comp("act", (lambda _m=m: lambda e: e.activation(
            YBFap[:, _m * 1024:(_m + 1) * 1024],
            YACCap[:, _m * 1024:(_m + 1) * 1024], AF.Copy, bias=0.0))(),
            reads=[YACCb], writes=[YBFb])
    YFMap, YFMb = fW(32768, 32768, BF16, "yfm")
    YFM = YFMap.rearrange("p (c t) -> p c t", c=8)
    for m in range(NT):
        mms = []
        for lc in range(8):
            mms.append((
                (lambda _m=m, _l=lc: lambda e: e.transpose(
                    PTR[:, _l * 128:(_l + 1) * 128],
                    YBFap[:, _m * 1024 + _l * 128:_m * 1024 + (_l + 1) * 128],
                    ident[:, :]))(),
                [YBFb, B("ident")]))
        p.mm_group(ptrb, mms)
        p.comp("dve", (lambda _m=m: lambda e: e.tensor_copy(
            YFM[:, :, _m * 128:(_m + 1) * 128],
            PTR[:, :].rearrange("p (l t) -> p l t", l=8)))(),
            reads=[ptrb], writes=[YFMb])
    FC2ap, FC2b = fW(0, 32768, BF16, "fc2")
    FC2 = FC2ap.rearrange("p (c o) -> p c o", c=8)
    for g in range(2):
        p.dma("pool", FC2[:, 4 * g:4 * (g + 1), :],
              fc2T.rearrange("(c p) o -> p c o", p=128)[:, 4 * g:4 * (g + 1), :],
              writes=[FC2b])
    RTSs = [fX(32768, 4096, BF16, "rts0"), fX(36864, 4096, BF16, "rts1")]
    for m in range(NT):
        rt, rtb = RTSs[m % 2]
        for s in range(4):
            bank = (2 * m + s) % 4
            mms = []
            for lc in range(8):
                mms.append((
                    (lambda _l=lc, _m=m, _s=s, _bk=bank: lambda e: e.matmul(
                        PB[_bk][:, 0:512], YFM[:, _l, _m * 128:(_m + 1) * 128],
                        FC2[:, _l, _s * 512:(_s + 1) * 512],
                        start=(_l == 0), stop=(_l == 7)))(),
                    [YFMb, FC2b]))
            p.mm_group(pbb[bank], mms)
            p.comp("act", (lambda _bk=bank, _rt=rt, _s=s: lambda e: e.activation(
                _rt[:, _s * 512:(_s + 1) * 512], PB[_bk][:, 0:512], AF.Copy,
                bias=0.0))(),
                reads=[pbb[bank]], writes=[rtb])
        p.dma("sp", rout_s[m, :, :], rt, reads=[rtb], writes=[B("rout_s")])
    tap("routed", rout_s[0, :, :], [128, D], BF16, reads=[B("rout_s")])

    SDAap, SDAb = fW(0, 65536, BF16, "sda")
    SDA = SDAap.rearrange("p (c o) -> p c o", c=16)
    SDBap, SDBb = fA(0, 65536, BF16, "sdb")
    SDB = SDBap.rearrange("p (c o) -> p c o", c=16)
    sdn4 = sdnT.rearrange("(c p) o -> p c o", p=128)
    for g in range(4):
        p.dma("pool", SDA[:, 4 * g:4 * (g + 1), :], sdn4[:, 4 * g:4 * (g + 1), :],
              writes=[SDAb])
    for g in range(4):
        p.dma("pool", SDB[:, 4 * g:4 * (g + 1), :], sdn4[:, 16 + 4 * g:16 + 4 * (g + 1), :],
              writes=[SDBb])

    if stage_limit < 10:
        return finish()

    MIDs = [fX(0, 8192, BF16, "midb0"), fX(8192, 8192, BF16, "midb1")]
    RTBs = [fX(16384, 4096, BF16, "rtb0"), fX(20480, 4096, BF16, "rtb1")]
    X1B, X1Bb = fX(24576, 8192, F32, "x1b")
    F1, F1b = fX(32768, 8192, F32, "f1")
    OUTs = [(lambda a_, b_: (a_, b_))(*fX(40960, 8192, F32, "outb0")),
            (lambda a_, b_: (a_, b_))(*fX(49152, 8192, F32, "outb1"))]

    def s8_loads(m):
        mb, mbb = MIDs[m % 2]
        mbv = mb.rearrange("p (c t) -> p c t", c=32)
        for g in range(4):
            p.dma("sp", mbv[:, 8 * g:8 * (g + 1), :],
                  mid_s.rearrange("c p t -> p c t")[:, 8 * g:8 * (g + 1),
                                                    m * 128:(m + 1) * 128],
                  reads=[B("mid_s")], writes=[mbb])
        rt, rtb = RTBs[m % 2]
        p.dma("sp", rt, rout_s[m, :, :], reads=[B("rout_s")], writes=[rtb])

    s8_loads(0)
    for m in range(NT):
        mb, mbb = MIDs[m % 2]
        mbv = mb.rearrange("p (c t) -> p c t", c=32)
        rt, rtb = RTBs[m % 2]
        if m + 1 < NT:
            s8_loads(m + 1)
        p.dma("sp", X1B, x1_s[m * 128:(m + 1) * 128, :], reads=[B("x1_s")],
              writes=[X1Bb])
        for s in range(4):
            bank = (2 * m + s) % 4
            mms = []
            for mo in range(32):
                sap = SDA if mo < 16 else SDB
                sbu = SDAb if mo < 16 else SDBb
                mms.append((
                    (lambda _mo=mo, _s=s, _bk=bank, _mv=mbv, _sa=sap: lambda e: e.matmul(
                        PB[_bk][:, 0:512], _mv[:, _mo, :],
                        _sa[:, _mo % 16, _s * 512:(_s + 1) * 512],
                        start=(_mo == 0), stop=(_mo == 31)))(),
                    [mbb, sbu]))
            p.mm_group(pbb[bank], mms)
            p.comp("dve", (lambda _s=s, _bk=bank, _rt=rt: lambda e: e.tensor_tensor(
                F1[:, _s * 512:(_s + 1) * 512], PB[_bk][:, 0:512],
                _rt[:, _s * 512:(_s + 1) * 512], OP.add))(),
                reads=[pbb[bank], rtb], writes=[F1b])
        ob, obb = OUTs[m % 2]
        sqo, sqob = OUTs[(m + 1) % 2]
        cm, cr, bm, br = ln_stats(F1, F1b, sqo, sqob, EPS_LN)
        p.comp("dve", (lambda _m=cm, _r=cr: lambda e: e.scalar_tensor_tensor(
            colE[:, 1:2], _m, -1.0, _r, OP.mult, OP.mult))(),
            reads=[bm, br], writes=[B("cE1")])
        p.comp("dve", lambda e: e.tensor_scalar_add(X1B, X1B, colE[:, 1:2]),
               reads=[X1Bb, B("cE1")], writes=[X1Bb])
        p.comp("dve", (lambda _r=cr, _ob=ob: lambda e: e.scalar_tensor_tensor(
            _ob, F1, _r, X1B, OP.mult, OP.add))(),
            reads=[F1b, br, X1Bb], writes=[obb])
        p.dma("sp", y_out[m * 128:(m + 1) * 128, :], ob, reads=[obb])

    return finish()


def _host_constants():
    arb = np.broadcast_to(np.arange(1, 257, dtype=np.float32)[None, :],
                          (128, 256)).copy()
    pp = np.arange(128, dtype=np.float32)
    kdm = np.stack([255.0 - pp, 255.0 - (128.0 + pp)], axis=1).astype(np.float32)
    mtt = np.empty((128, 512), np.float32)
    q = np.arange(256, dtype=np.float32)
    for c in range(2):
        k = c * 128 + np.arange(128, dtype=np.float32)
        d = q[None, :] - k[:, None]
        mtt[:, c * 256:(c + 1) * 256] = np.where(d >= 0, d, BIG)
    ident = np.eye(128, dtype=ml_dtypes.bfloat16)
    ones_bf = np.ones((128, 128), dtype=ml_dtypes.bfloat16)
    onesf = np.ones((1, 128), dtype=np.float32)
    return arb, kdm, mtt, ident, ones_bf, onesf


def make_in_maps(inputs):
    x = np.asarray(inputs["L0_attn_hidden_states"], dtype=np.float32)
    slope = np.asarray(inputs["L0_attn_slope_rate"], dtype=np.float32)
    wqkv = np.asarray(inputs["L0_attn_w_qkv"], dtype=np.float32)
    wg = np.asarray(inputs["L0_attn_w_gate"], dtype=np.float32)
    wo = np.asarray(inputs["L0_attn_w_out"], dtype=np.float32)
    nw = np.asarray(inputs["L0_attn_norm_w"], dtype=np.float32)
    gw = np.asarray(inputs["L0_ffn_gate_w"], dtype=np.float32)
    gb = np.asarray(inputs["L0_ffn_gate_bias"], dtype=np.float32)
    fc1 = np.asarray(inputs["L0_ffn_fc1_w"], dtype=np.float32)
    fc2 = np.asarray(inputs["L0_ffn_fc2_w"], dtype=np.float32)
    w1 = np.asarray(inputs["L0_ffn_w1"], dtype=np.float32)
    w2 = np.asarray(inputs["L0_ffn_w2"], dtype=np.float32)
    sup = np.asarray(inputs["L0_ffn_shared_up_w"], dtype=np.float32)
    sdn = np.asarray(inputs["L0_ffn_shared_down_w"], dtype=np.float32)

    arb, kdm, mtt, ident, ones_bf, onesf = _host_constants()
    common = dict(
        wqkvT=np.ascontiguousarray(wqkv.T),
        wgT=np.ascontiguousarray(wg.T),
        woT=np.ascontiguousarray(wo.T),
        fc1T=np.ascontiguousarray(fc1.T),
        fc2T=np.ascontiguousarray(fc2.T),
        supT=np.ascontiguousarray(sup.T),
        sdnT=np.ascontiguousarray(sdn.T),
        w1T=np.ascontiguousarray(w1.transpose(0, 2, 1)),
        w2T=np.ascontiguousarray(w2.transpose(0, 2, 1)),
        gwT=np.ascontiguousarray(gw.T),
        sl_row=np.ascontiguousarray(slope[None, :]),
        gb_row=np.ascontiguousarray(gb[None, :]),
        nw_pc=np.ascontiguousarray(nw.reshape(16, 128).T),
        arb=arb, kdm=kdm, mtt=mtt, ident=ident, ones_bf=ones_bf, onesf=onesf,
    )
    in_maps = []
    for c in range(8):
        b, pos = c // 4, c % 4
        dist = np.full((1, 8), BIG, np.float32)
        for i in range(8):
            qq = i - 4 * b
            if 0 <= qq < pos:
                dist[0, i] = pos - 1 - qq
        m = dict(common)
        m["x"] = np.ascontiguousarray(x[b, pos * T:(pos + 1) * T, :])
        m["dist"] = dist
        in_maps.append(m)
    return in_maps


def kernel(**inputs):
    nc = build_program()
    in_maps = make_in_maps(inputs)
    res = run_bass_kernel_spmd(nc, in_maps, core_ids=list(range(8)))
    out = np.empty((2, 8192, 2048), np.float32)
    for c in range(8):
        b, pos = c // 4, c % 4
        out[b, pos * T:(pos + 1) * T, :] = np.asarray(res.results[c]["y"])
    return out


# version 16
# speedup vs baseline: 1.0085x; 1.0011x over previous
import numpy as np
import ml_dtypes
from contextlib import ExitStack

import concourse.bass as bass
import concourse.mybir as mybir
from concourse.bass_utils import run_bass_kernel_spmd
from concourse.alu_op_type import AluOpType

F32 = mybir.dt.float32
BF16 = mybir.dt.bfloat16
AF = mybir.ActivationFunctionType
AX = mybir.AxisListType
OP = AluOpType

ENGINES = ("pe", "dve", "act", "pool", "sp")
NLANES = 8
QLANES = {q: tuple(f"q{q}{i}" for i in range(NLANES)) for q in ("sp", "act", "pool")}
SEMS = ("pe", "dve", "act", "pool", "cc") + QLANES["sp"] + QLANES["act"] + QLANES["pool"]


class Buf:
    __slots__ = ("name", "w", "r")

    def __init__(self, name, prior=()):
        self.name = name
        self.w = {}
        self.r = {}
        for b in prior:
            for k, v in b.dep_write().items():
                if self.w.get(k, 0) < v:
                    self.w[k] = v

    def dep_read(self):
        return dict(self.w)

    def dep_write(self):
        d = dict(self.w)
        for k, v in self.r.items():
            if d.get(k, 0) < v:
                d[k] = v
        return d

    def note_w(self, key, val):
        if self.w.get(key, 0) < val:
            self.w[key] = val

    def note_r(self, key, val):
        if self.r.get(key, 0) < val:
            self.r[key] = val


def _merge(dst, src):
    for k, v in src.items():
        if dst.get(k, 0) < v:
            dst[k] = v


class Prog:
    def __init__(self, nc):
        self.nc = nc
        self.ops = {e: [] for e in ENGINES}
        self.cnt = {k: 0 for k in SEMS}
        self.waited = {e: {} for e in ENGINES}
        self.lane_rr = {q: 0 for q in QLANES}

    def _deps(self, reads, writes, extra):
        d = {}
        for b in reads:
            _merge(d, b.dep_read())
        for b in writes:
            _merge(d, b.dep_write())
        for ev in extra:
            if ev is not None:
                _merge(d, {ev[0]: ev[1]})
        return d

    def _prune(self, eng, deps, self_sync=True):
        out = {}
        wt = self.waited[eng]
        for k, v in deps.items():
            if not self_sync and k == eng:
                continue
            if k == eng and eng == "pe":
                continue
            if wt.get(k, 0) >= v:
                continue
            out[k] = v
        for k, v in out.items():
            wt[k] = v
        return out

    def _rec(self, eng, waits, fn, inc_key, inc_amt):
        self.ops[eng].append((waits, fn, inc_key, inc_amt))

    def comp(self, eng, fn, reads=(), writes=(), extra=()):
        deps = self._deps(reads, writes, extra)
        waits = self._prune(eng, deps, True)
        self.cnt[eng] += 1
        ev = (eng, self.cnt[eng])
        self._rec(eng, waits, fn, eng, 1)
        for b in writes:
            b.note_w(eng, ev[1])
        for b in reads:
            b.note_r(eng, ev[1])
        return ev

    def dma(self, queue, out_ap, in_ap, reads=(), writes=(), extra=(), **kw):
        r = self.lane_rr[queue]
        self.lane_rr[queue] = (r + 1) % NLANES
        qk = QLANES[queue][r]
        deps = self._deps(reads, writes, extra)
        _merge(deps, {qk: self.cnt[qk]})
        waits = self._prune(queue, deps, True)
        self.cnt[qk] += 16
        val = self.cnt[qk]

        def fn(eng, _o=out_ap, _i=in_ap, _kw=dict(kw)):
            return eng.dma_start(out=_o, in_=_i, **_kw)

        self._rec(queue, waits, fn, qk, 16)
        for b in writes:
            b.note_w(qk, val)
        for b in reads:
            b.note_r(qk, val)
        return (qk, val)

    def mm_group(self, psum_buf, mms):
        n = len(mms)
        ev = None
        for i, (fn, reads) in enumerate(mms):
            deps = self._deps(reads, (psum_buf,) if i == 0 else (), ())
            waits = self._prune("pe", deps, True)
            if i == n - 1:
                self.cnt["pe"] += 1
                ev = ("pe", self.cnt["pe"])
                self._rec("pe", waits, fn, "pe", 1)
                psum_buf.note_w("pe", ev[1])
                for _f2, reads2 in mms:
                    for b in reads2:
                        b.note_r("pe", ev[1])
            else:
                self._rec("pe", waits, fn, None, 0)
        return ev

    def wait_only(self, eng, extra):
        deps = {}
        for e in extra:
            _merge(deps, {e[0]: e[1]})
        waits = self._prune(eng, deps, True)
        if waits:
            self._rec(eng, waits, None, None, 0)

    def drain_queues(self, eng="sp"):
        evs = [(lk, self.cnt[lk]) for lanes in QLANES.values() for lk in lanes
               if self.cnt[lk] > 0]
        self.wait_only(eng, evs)

    def verify(self):
        idx = {e: 0 for e in ENGINES}
        sem = {k: 0 for k in SEMS}
        total = sum(len(v) for v in self.ops.values())
        done = 0
        progressed = True
        while progressed:
            progressed = False
            for e in ENGINES:
                while idx[e] < len(self.ops[e]):
                    waits, fn, ik, ia = self.ops[e][idx[e]]
                    if any(sem[k] < v for k, v in waits.items()):
                        break
                    if ik is not None:
                        sem[ik] += ia
                    idx[e] += 1
                    done += 1
                    progressed = True
        if done != total:
            lines = []
            for e in ENGINES:
                if idx[e] < len(self.ops[e]):
                    waits = self.ops[e][idx[e]][0]
                    missing = {k: (v, sem[k]) for k, v in waits.items() if sem[k] < v}
                    lines.append(f"  {e} stuck at {idx[e]}/{len(self.ops[e])} on {missing}")
            raise RuntimeError("DEADLOCK in schedule:\n" + "\n".join(lines))

    def replay(self):
        nc = self.nc
        with ExitStack() as st:
            sems = {k: st.enter_context(nc.semaphore("sem_" + k)) for k in SEMS}
            block = st.enter_context(nc.Block())

            def mk(engname):
                def run(eng):
                    for waits, fn, ik, ia in self.ops[engname]:
                        for k in sorted(waits):
                            eng.wait_ge(sems[k], waits[k])
                        if fn is not None:
                            ins = fn(eng)
                            if ik is not None:
                                ins.then_inc(sems[ik], ia)
                return run

            block.tensor(mk("pe"))
            block.vector(mk("dve"))
            block.scalar(mk("act"))
            block.gpsimd(mk("pool"))
            block.sync(mk("sp"))


class Flex:

    def __init__(self):
        self.reg = {}

    def view(self, arena_handle, arena_name, byte0, nbytes, dtype, tag):
        es = 4 if dtype == F32 else 2
        assert byte0 % es == 0 and nbytes % es == 0
        ap = arena_handle[:, :]
        if dtype != ap.dtype:
            ap = ap.bitcast(dtype)
        ap = ap[:, byte0 // es:(byte0 + nbytes) // es]
        lst = self.reg.setdefault(arena_name, [])
        b1 = byte0 + nbytes
        priors = [bu for (a0, a1, bu) in lst if a0 < b1 and byte0 < a1]
        buf = Buf(tag, priors)
        self.reg[arena_name] = [(a0, a1, bu) for (a0, a1, bu) in lst
                                if not (a0 < b1 and byte0 < a1)]
        self.reg[arena_name].append((byte0, b1, buf))
        return ap, buf


T = 2048
D = 2048
NT = 16
NH = 16
NB = 8
NE = 32
LAT = 1024
SH = 4096
EPS_LN = np.float32(1e-5)
EPS_RMS = np.float32(1e-6)
BIG = 1.0e6
BIGR = 1.0e9


def build_program(stage_limit=99, taps=()):
    nc = bass.Bass("TRN2", target_bir_lowering=False, debug=False)
    p = Prog(nc)
    taps = set(taps)

    def di(name, shape, dt=F32):
        return nc.dram_tensor(name, shape, dt, kind="ExternalInput").ap()

    x_in = di("x", [T, D])
    wqkvT = di("wqkvT", [D, 3 * D])
    wgT = di("wgT", [D, D])
    woT = di("woT", [D, D])
    fc1T = di("fc1T", [D, LAT])
    fc2T = di("fc2T", [LAT, D])
    supT = di("supT", [D, SH])
    sdnT = di("sdnT", [SH, D])
    w1T = di("w1T", [NE, LAT, LAT])
    w2T = di("w2T", [NE, LAT, LAT])
    gwT = di("gwT", [D, NE])
    sl_row = di("sl_row", [1, NH])
    gb_row = di("gb_row", [1, NE])
    nw_pc = di("nw_pc", [128, 16])
    dist_in = di("dist", [1, 8])
    arb_in = di("arb", [128, 256])
    kdm_in = di("kdm", [128, 2])
    mtt_in = di("mtt", [128, 512])
    ident_in = di("ident", [128, 128], BF16)
    ones_bf_in = di("ones_bf", [128, 128], BF16)
    onesf_in = di("onesf", [1, 128])

    y_out = nc.dram_tensor("y", [T, D], F32, kind="ExternalOutput").ap()

    def ds(name, shape, dt):
        return nc.dram_tensor(name, shape, dt, kind="Internal").ap()

    q_fm_s = ds("q_fm_s", [NB, 128, NH * 256], BF16)
    k_fm_s = ds("k_fm_s", [NB, 128, NH * 256], BF16)
    k_tm_s = ds("k_tm_s", [NT, 128, D], BF16)
    v_tm_s = ds("v_tm_s", [NT, 128, D], BF16)
    g_s = ds("g_s", [NH, 128, T], BF16)
    attn_s = ds("attn_s", [NT, 128, NH * 128], BF16)
    S_loc = ds("S_loc", [NH * 128, 128], F32)
    S_gth = ds("S_gth", [8 * NH * 128, 128], F32)
    x1_s = ds("x1_s", [T, D], F32)
    mid_s = ds("mid_s", [32, 128, T], BF16)
    rout_s = ds("rout_s", [NT, 128, D], BF16)

    def sb(name, shape, dt):
        return nc.alloc_sbuf_tensor(name, shape, dt)

    ident = sb("identS", [128, 128], BF16)
    ones_bf = sb("ones_bfS", [128, 128], BF16)
    onesf = sb("onesfS", [1, 128], F32)
    arb = sb("arbS", [128, 256], F32)
    kdm = sb("kdmS", [128, 2], F32)
    mtt = sb("mttS", [128, 512], F32)
    slr = sb("slrS", [1, NH], F32)
    gbr = sb("gbrS", [1, NE], F32)
    dsr = sb("dsrS", [1, 8], F32)
    nwc = sb("nwcS", [128, 16], F32)
    s_bc = sb("s_bc", [128, NH], F32)
    negs = sb("negs", [128, NH], F32)
    lam = sb("lam", [128, NH], F32)
    distb = sb("distb", [128, 8], F32)
    wkv = sb("wkv", [128, 8 * NH], F32)
    gb_bc = sb("gb_bc", [128, NE], F32)
    gb_rep = sb("gb_rep", [128, 512], F32)
    kdc = sb("kdc", [128, NH * 2], F32)
    cw = sb("cw", [128, 512], F32)
    rstdc = sb("rstdc", [128, 16], F32)
    gw_sb = sb("gw_sbS", [128, 512], BF16)
    stT = sb("stT", [128, 192], F32)
    agT = sb("agT", [128, 16], F32)
    colD = sb("colD", [128, 8], F32)
    colE = sb("colE", [128, 8], F32)

    BIGA = sb("BIGA", [128, 32768], BF16)
    BIGW = sb("BIGW", [128, 32768], BF16)
    FLXT = sb("FLX", [128, 30720], BF16)

    PB = [nc.alloc_psum_tensor(f"pb{i}", [128, 512], F32) for i in range(7)]
    PTR = nc.alloc_psum_tensor("ptrT", [128, 1024], BF16)

    flex = Flex()

    def fA(byte0, nbytes, dtype, tag):
        return flex.view(BIGA, "A", byte0, nbytes, dtype, tag)

    def fW(byte0, nbytes, dtype, tag):
        return flex.view(BIGW, "W", byte0, nbytes, dtype, tag)

    def fX(byte0, nbytes, dtype, tag):
        return flex.view(FLXT, "X", byte0, nbytes, dtype, tag)

    bufs = {}

    def B(name):
        if name not in bufs:
            bufs[name] = Buf(name)
        return bufs[name]

    pbb = [B(f"pb{i}") for i in range(7)]
    ptrb = B("ptrT")

    dtt, dttb = flex.view(FLXT, "X", 36864, NH * 1024, BF16, "dtt")
    qdb, qdbb = flex.view(FLXT, "X", 53248, NH * 512, BF16, "qdb")

    def tap(name, src_ap, shape, dt=F32, reads=()):
        if name in taps:
            o = nc.dram_tensor("tap_" + name, shape, dt, kind="ExternalOutput").ap()
            p.dma("sp", o, src_ap, reads=reads)

    _statrot = [0]

    def ln_stats(x_ap, xbuf_b, sq_scr, sq_b, eps, eps_extra=None):
        n = x_ap.shape[1]
        assert n % 512 == 0
        nj = n // 512
        i = _statrot[0] % 8
        _statrot[0] += 1
        st = stT[:, 24 * i:24 * i + 6 * nj]
        ag = agT[:, 2 * i:2 * i + 2]
        var = agT[:, 2 * i + 1:2 * i + 2]
        c_r = colD[:, i:i + 1]
        b_st, b_ag, b_r = B(f"st{i}"), B(f"ag{i}"), B(f"cD{i}")
        for j in range(nj):
            p.comp("dve", (lambda _j=j: lambda e: e.bn_stats(
                st[:, 6 * _j:6 * (_j + 1)], x_ap[:, 512 * _j:512 * (_j + 1)]))(),
                reads=[xbuf_b], writes=[b_st])
        p.comp("dve", lambda e: e.bn_aggr(ag, st), reads=[b_st], writes=[b_ag])
        if eps_extra is not None:
            ea, eb = eps_extra
            p.comp("dve", lambda e: e.tensor_tensor(var, var, ea, OP.add),
                   reads=[b_ag, eb], writes=[b_ag])
            eps = np.float32(0.0)
        p.comp("act", lambda e: e.activation(c_r, var, AF.Ln, bias=eps),
               reads=[b_ag], writes=[b_r])
        p.comp("act", lambda e: e.activation(c_r, c_r, AF.Exp, scale=-0.5),
               reads=[b_r], writes=[b_r])
        return agT[:, 2 * i:2 * i + 1], c_r, b_ag, b_r

    for t_, src, bn in [(ident, ident_in, "ident"), (ones_bf, ones_bf_in, "ones"),
                        (onesf, onesf_in, "onesf"), (arb, arb_in, "arb"),
                        (kdm, kdm_in, "kdm"), (mtt, mtt_in, "mtt"),
                        (slr, sl_row, "slr"), (gbr, gb_row, "gbr"),
                        (dsr, dist_in, "dsr"), (nwc, nw_pc, "nwc")]:
        p.dma("sp", t_[:, :], src, writes=[B(bn)])
    p.dma("pool", gw_sb[:, :].rearrange("p (f e) -> p f e", f=16),
          gwT.rearrange("(f p) e -> p f e", p=128), writes=[B("gw_sb")])

    p.mm_group(pbb[0], [
        (lambda e: e.matmul(PB[0][:, 0:NH], onesf[0:1, :], slr[0:1, :],
                            start=True, stop=True), [B("onesf"), B("slr")]),
        (lambda e: e.matmul(PB[0][:, 32:32 + NE], onesf[0:1, :], gbr[0:1, :],
                            start=True, stop=True), [B("onesf"), B("gbr")]),
        (lambda e: e.matmul(PB[0][:, 64:72], onesf[0:1, :], dsr[0:1, :],
                            start=True, stop=True), [B("onesf"), B("dsr")]),
    ])
    p.comp("dve", lambda e: e.tensor_copy(s_bc[:, :], PB[0][:, 0:NH]),
           reads=[pbb[0]], writes=[B("s_bc")])
    p.comp("dve", lambda e: e.tensor_copy(gb_bc[:, :], PB[0][:, 32:32 + NE]),
           reads=[pbb[0]], writes=[B("gb_bc")])
    p.comp("dve", lambda e: e.tensor_scalar_mul(distb[:, :], PB[0][:, 64:72], -2048.0),
           reads=[pbb[0]], writes=[B("distb")])
    p.comp("dve", lambda e: e.tensor_scalar_mul(negs[:, :], s_bc[:, :], -1.0),
           reads=[B("s_bc")], writes=[B("negs")])
    p.comp("act", lambda e: e.activation(lam[:, :], s_bc[:, :], AF.Exp, scale=-256.0),
           reads=[B("s_bc")], writes=[B("lam")])
    for i in range(8):
        p.comp("act", (lambda ii: lambda e: e.activation(
            wkv[:, ii * NH:(ii + 1) * NH], s_bc[:, :], AF.Exp,
            scale=distb[:, ii:ii + 1]))(i),
            reads=[B("s_bc"), B("distb")], writes=[B("wkv")])
    for j in range(16):
        p.comp("dve", (lambda jj: lambda e: e.tensor_copy(
            gb_rep[:, jj * NE:(jj + 1) * NE], gb_bc[:, :]))(j),
            reads=[B("gb_bc")], writes=[B("gb_rep")])
    for h in range(NH):
        p.comp("act", (lambda hh: lambda e: e.activation(
            dtt[:, hh * 512:(hh + 1) * 512], mtt[:, :], AF.Exp,
            scale=negs[:, hh:hh + 1]))(h),
            reads=[B("mtt"), B("negs")], writes=[dttb])
        p.comp("act", (lambda hh: lambda e: e.activation(
            qdb[:, hh * 256:(hh + 1) * 256], arb[:, :], AF.Exp,
            scale=negs[:, hh:hh + 1]))(h),
            reads=[B("arb"), B("negs")], writes=[qdbb])
        p.comp("act", (lambda hh: lambda e: e.activation(
            kdc[:, hh * 2:(hh + 1) * 2], kdm[:, :], AF.Exp,
            scale=negs[:, hh:hh + 1]))(h),
            reads=[B("kdm"), B("negs")], writes=[B("kdc")])
    tap("wkv", wkv[:, :], [128, 128], F32, reads=[B("wkv")])

    def finish():
        if stage_limit < 10:
            sink = sb("sinkS", [1, 64], F32)
            skb = B("sink")
            for ap in [x_in[0:1, 0:64], wqkvT[0:1, 0:64], wgT[0:1, 0:64],
                       woT[0:1, 0:64], fc1T[0:1, 0:64], fc2T[0:1, 0:64],
                       supT[0:1, 0:64], sdnT[0:1, 0:64], w1T[0, 0:1, 0:64],
                       w2T[0, 0:1, 0:64]]:
                p.dma("sp", sink[:, :], ap, reads=[skb], writes=[skb])
        p.drain_queues("sp")
        p.verify()
        p.replay()
        return nc

    if stage_limit < 1:
        return finish()

    lnA, lnAb = fA(0, 65536, BF16, "ln1fm")
    lnfm = lnA.rearrange("p (f t) -> p f t", f=16)
    XT0, XT0b = fX(0, 8192, F32, "xt0")
    XT1, XT1b = fX(8192, 8192, F32, "xt1")
    SQS, SQSb = fX(16384, 8192, F32, "sqs")
    LNT, LNTb = fX(24576, 4096, BF16, "lnt")
    XTl = [(XT0, XT0b), (XT1, XT1b)]

    def transpose_tile(src_ap, src_b, dst_view, dst_b, m, also_w=None):
        for half in range(2):
            mms = []
            for c in range(8):
                cc = half * 8 + c
                mms.append((
                    (lambda _c=cc, _o=c: lambda e: e.transpose(
                        PTR[:, _o * 128:(_o + 1) * 128],
                        src_ap[:, _c * 128:(_c + 1) * 128], ident[:, :]))(),
                    [src_b, B("ident")]))
            p.mm_group(ptrb, mms)
            ev = p.comp("dve", (lambda _m=m, _h=half: lambda e: e.tensor_copy(
                dst_view[:, _h * 8:(_h + 1) * 8, _m * 128:(_m + 1) * 128],
                PTR[:, :].rearrange("p (f t) -> p f t", f=8)))(),
                reads=[ptrb], writes=[dst_b])
            if also_w is not None:
                also_w.note_w(*ev)

    lnCh = [Buf(f"lnch{m}") for m in range(NT)]
    for m in range(NT):
        xb, xbb = XTl[m % 2]
        p.dma("sp" if m % 2 == 0 else "act", xb,
              x_in[m * 128:(m + 1) * 128, :], writes=[xbb])
        cm, cr, bm, br = ln_stats(xb, xbb, SQS, SQSb, EPS_LN)
        p.comp("dve", (lambda _x=xb, _m=cm, _r=cr: lambda e: e.tensor_scalar(
            LNT, _x, _m, _r, OP.subtract, OP.mult))(),
            reads=[xbb, bm, br], writes=[LNTb])
        transpose_tile(LNT, LNTb, lnfm, lnCh[m], m, also_w=lnAb)
    tap("ln1fm", lnA[:, 0:2048], [128, 2048], BF16, reads=[lnAb])

    if stage_limit < 2:
        return finish()

    wslice = [None, None, None, None]

    def load_wslice(src_ap, tag, s):
        wap, _junk = fW(0, 65536, BF16, tag + f"_s{s}") if False else (BIGW, None)
        prior = [wslice[s]] if wslice[s] is not None else []
        gb_ = Buf(tag + f"_s{s}", prior)
        wv_ = BIGW[:, :].rearrange("p (f n) -> p f n", f=16)
        p.dma("pool", wv_[:, :, 512 * s:512 * (s + 1)],
              src_ap[:, 512 * s:512 * (s + 1)].rearrange("(f p) n -> p f n", p=128),
              writes=[gb_])
        wslice[s] = gb_
        return gb_

    def load_wfull(src_ap, tag):
        wv_ = BIGW[:, :].rearrange("p (f n) -> p f n", f=16)
        sbufs = [load_wslice(src_ap, tag, s) for s in range(4)]
        return wv_, (None, sbufs)

    def wslice_into_flex(owner=None):
        if owner is None:
            owner = Buf("wslices_owner", [b for b in wslice if b is not None])
        flex.reg.setdefault("W", []).append((0, 65536, owner))

    def tm_pass(wv_, wbt_, dst_s, dst_db, chbufs, next_load=None):
        _junk, wgb_ = wbt_
        for s in range(4):
            for m in range(NT):
                vc, vcb = chbufs[m % 2]
                bank = (2 * (m % 2) + (s % 2)) % 4
                mms = []
                for f in range(16):
                    mms.append((
                        (lambda _f=f, _m=m, _s=s, _bk=bank: lambda e: e.matmul(
                            PB[_bk][:, 0:512],
                            lnfm[:, _f, _m * 128:(_m + 1) * 128],
                            wv_[:, _f, _s * 512:(_s + 1) * 512],
                            start=(_f == 0), stop=(_f == 15)))(),
                        [lnCh[m], wgb_[s]]))
                p.mm_group(pbb[bank], mms)
                p.comp("act", (lambda _s=s, _bk=bank, _v=vc: lambda e: e.activation(
                    _v[:, 0:512], PB[_bk][:, 0:512], AF.Silu))(),
                    reads=[pbb[bank]], writes=[vcb])
                p.dma("sp", dst_s[m, :, 512 * s:512 * (s + 1)], vc[:, 0:512],
                      reads=[vcb], writes=[dst_db])
            if next_load is not None:
                next_load(s)

    VCH0, VCH0b = fX(0, 2048, BF16, "vch0")
    VCH1, VCH1b = fX(2048, 2048, BF16, "vch1")
    wv, wvb = load_wfull(wqkvT[:, 2 * D:3 * D], "w_v")
    tm_pass(wv, wvb, v_tm_s, B("v_tm_s"), [(VCH0, VCH0b), (VCH1, VCH1b)],
            next_load=lambda s: load_wslice(wqkvT[:, D:2 * D], "w_k", s))

    wk = BIGW[:, :].rearrange("p (f n) -> p f n", f=16)
    wkb = (None, [wslice[s] for s in range(4)])
    KCH0, KCH0b = fX(0, 2048, BF16, "kch0")
    KCH1, KCH1b = fX(2048, 2048, BF16, "kch1")
    tm_pass(wk, wkb, k_tm_s, B("k_tm_s"), [(KCH0, KCH0b), (KCH1, KCH1b)],
            next_load=lambda s: load_wslice(wqkvT[:, D:2 * D], "w_k2", s))

    KRA = [fX(8192, 8192, BF16, "kra0")]
    VRA = [fX(16384, 8192, BF16, "vra0")]
    SALL, SALLb = fX(24576, 8192, F32, "sall")
    p.comp("dve", lambda e: e.memset(SALL, 0.0), writes=[SALLb])

    def passA_load(i):
        kra, krab = KRA[0]
        vra, vrab = VRA[0]
        p.dma("pool", kra.rearrange("p (c n) -> p c n", c=2),
              k_tm_s.rearrange("m p n -> p m n")[:, 2 * i:2 * i + 2, :],
              reads=[B("k_tm_s")], writes=[krab])
        p.dma("pool", vra.rearrange("p (c n) -> p c n", c=2),
              v_tm_s.rearrange("m p n -> p m n")[:, 2 * i:2 * i + 2, :],
              reads=[B("v_tm_s")], writes=[vrab])

    def passA_block(i):
        kra, krab = KRA[0]
        vra, vrab = VRA[0]
        for cc_ in range(2):
            for h in range(NH):
                p.comp("dve", (lambda _h=h, _c=cc_, _v=vra:
                               lambda e: e.tensor_scalar_mul(
                    _v[:, _c * 2048 + _h * 128:_c * 2048 + (_h + 1) * 128],
                    _v[:, _c * 2048 + _h * 128:_c * 2048 + (_h + 1) * 128],
                    kdc[:, 2 * _h + _c:2 * _h + _c + 1]))(),
                    reads=[vrab, B("kdc")], writes=[vrab])
        for h in range(NH):
            bank = 4 + (h % 2)
            mms = []
            for cc_ in range(2):
                mms.append((
                    (lambda _h=h, _c=cc_, _k=kra, _V=vra, _bk=bank:
                     lambda e: e.matmul(
                        PB[_bk][:, 0:128],
                        _k[:, _c * 2048 + _h * 128:_c * 2048 + (_h + 1) * 128],
                        _V[:, _c * 2048 + _h * 128:_c * 2048 + (_h + 1) * 128],
                        start=(_c == 0), stop=(_c == 1)))(),
                    [krab, vrab]))
            p.mm_group(pbb[bank], mms)
            p.comp("dve", (lambda _h=h, _bk=bank: lambda e: e.scalar_tensor_tensor(
                SALL[:, _h * 128:(_h + 1) * 128], SALL[:, _h * 128:(_h + 1) * 128],
                lam[:, _h:_h + 1], PB[_bk][:, 0:128], OP.mult, OP.add))(),
                reads=[SALLb, B("lam"), pbb[bank]], writes=[SALLb])

    STG0, STG0b = fX(0, 4096, BF16, "stg0")
    STG1, STG1b = fX(4096, 4096, BF16, "stg1")
    STG = [(STG0, STG0b), (STG1, STG1b)]

    def fm_pass(wv_, wbt_, dst_s, dst_db, actf, next_load=None, on_head_pre=None):
        _junk, wgb_ = wbt_
        for h in range(NH):
            if on_head_pre is not None:
                on_head_pre(h)
            st, stb = STG[h % 2]
            for s in range(4):
                bank = (2 * (h % 2) + (s % 2)) % 4
                mms = []
                for f in range(16):
                    mms.append((
                        (lambda _f=f, _h=h, _s=s, _bk=bank: lambda e: e.matmul(
                            PB[_bk][:, 0:512],
                            wv_[:, _f, _h * 128:(_h + 1) * 128],
                            lnfm[:, _f, _s * 512:(_s + 1) * 512],
                            start=(_f == 0), stop=(_f == 15)))(),
                        [lnCh[4 * s], lnCh[4 * s + 1], lnCh[4 * s + 2],
                         lnCh[4 * s + 3], wgb_[h // 4]]))
                p.mm_group(pbb[bank], mms)
                p.comp("act", (lambda _s=s, _bk=bank, _t=st, _a=actf: lambda e: e.activation(
                    _t[:, _s * 512:(_s + 1) * 512], PB[_bk][:, 0:512], _a))(),
                    reads=[pbb[bank]], writes=[stb])
            if dst_s.shape[0] == NB:
                p.dma("sp", dst_s[:, :, h * 256:(h + 1) * 256].rearrange("b p t -> p b t"),
                      st.rearrange("p (b t) -> p b t", b=NB),
                      reads=[stb], writes=[dst_db])
            else:
                p.dma("sp", dst_s[h, :, :], st, reads=[stb], writes=[dst_db])
            if next_load is not None and h % 4 == 3:
                next_load(h // 4)


    wk2 = BIGW[:, :].rearrange("p (f n) -> p f n", f=16)
    wk2b = (None, [wslice[s] for s in range(4)])
    def kfm_head_hook(h):
        if h % 2 == 0:
            i = h // 2
            passA_load(i)
            passA_block(i)

    fm_pass(wk2, wk2b, k_fm_s, B("k_fm_s"), AF.Silu,
            next_load=lambda s: load_wslice(wqkvT[:, 0:D], "w_q", s),
            on_head_pre=kfm_head_hook)

    p.dma("sp", S_loc.rearrange("(h d) e -> d h e", d=128),
          SALL.rearrange("p (h e) -> p h e", h=NH), reads=[SALLb], writes=[B("S_loc")])
    p.comp("pool", lambda e: e.collective_compute(
        "AllGather", OP.bypass, replica_groups=[list(range(8))],
        ins=[S_loc.opt()], outs=[S_gth.opt()]),
        reads=[B("S_loc")], writes=[B("S_gth")])
    tap("sall", SALL, [128, 2048], F32, reads=[SALLb])

    if stage_limit < 3:
        return finish()

    wq = BIGW[:, :].rearrange("p (f n) -> p f n", f=16)
    wqb = (None, [wslice[s] for s in range(4)])
    fm_pass(wq, wqb, q_fm_s, B("q_fm_s"), AF.Silu,
            next_load=lambda s: load_wslice(wgT, "w_g", s))
    wg = BIGW[:, :].rearrange("p (f n) -> p f n", f=16)
    wgb = (None, [wslice[s] for s in range(4)])
    fm_pass(wg, wgb, g_s, B("g_s"), AF.Sigmoid)

    wo = BIGW[:, :].rearrange("p (f n) -> p f n", f=16)
    for s in range(4):
        load_wslice(woT, "w_o", s)
    wob = Buf("w_o_all", [wslice[s] for s in range(4)])
    for f in range(16):
        p.comp("dve", (lambda _f=f: lambda e: e.tensor_scalar_mul(
            wo[:, _f, :], wo[:, _f, :], nwc[:, _f:_f + 1]))(),
            reads=[wob, B("nwc")], writes=[wob])
    wslice_into_flex(owner=wob)
    tap("gfm", g_s[0, :, :], [128, T], BF16, reads=[B("g_s")])

    if stage_limit < 4:
        return finish()

    KVST, KVSTb = fX(8192, 8192, F32, "kvst")
    KVG, KVGb = fX(16384, 4096, F32, "kvg")
    sg4 = S_gth.rearrange("(r c p) e -> p r c e", r=8, c=16)
    for h in range(NH):
        p.dma("sp", KVG.rearrange("p (r e) -> p r e", r=8), sg4[:, :, h, :],
              reads=[B("S_gth")], writes=[KVGb])
        p.comp("dve", (lambda _h=h: lambda e: e.tensor_scalar_mul(
            KVST[:, _h * 128:(_h + 1) * 128], KVG[:, 0:128],
            wkv[:, _h:_h + 1]))(),
            reads=[KVGb, B("wkv")], writes=[KVSTb])
        for i in range(1, 8):
            p.comp("dve", (lambda _h=h, _i=i: lambda e: e.scalar_tensor_tensor(
                KVST[:, _h * 128:(_h + 1) * 128], KVG[:, _i * 128:(_i + 1) * 128],
                wkv[:, _i * NH + _h:_i * NH + _h + 1],
                KVST[:, _h * 128:(_h + 1) * 128], OP.mult, OP.add))(),
                reads=[KVGb, B("wkv"), KVSTb], writes=[KVSTb])
    tap("kv0", KVST, [128, 2048], F32, reads=[KVSTb])

    ROWS, ROWSb = fX(24576, 8192, F32, "rows")
    ssrow = ROWS[0:1, :]
    rsrow = ROWS[32:33, :]
    RSBF, RSBFb = fX(32768, 4096, BF16, "rsbf")
    rsbf = RSBF[0:1, :]
    QBs = [fA(0, 8192, BF16, "qb0"), fA(8192, 8192, BF16, "qb1")]
    KBs = [fA(16384, 8192, BF16, "kb0"), fA(24576, 8192, BF16, "kb1")]
    KTBs = [fA(32768, 8192, BF16, "ktb0"), fA(40960, 8192, BF16, "ktb1")]
    VBs = [fA(49152, 8192, BF16, "vb0"), fA(57344, 8192, BF16, "vb1")]
    ATTS, ATTSb = fX(0, 8192, BF16, "atts")
    STBs = [fX(20480, 1024, BF16, "stb0"), fX(21504, 1024, BF16, "stb1")]
    QQs = [fX(22528, 512, BF16, "qq0"), fX(23040, 512, BF16, "qq1")]
    SQG, SQGb = fX(23552, 512, BF16, "sqg")
    KVBFs = [fX(16384, 256, BF16, "kvbf0"), fX(16640, 256, BF16, "kvbf1")]
    VKDPd = [[fX(16896, 256, BF16, "vkdp00"), fX(17152, 256, BF16, "vkdp01")],
             [fX(17408, 256, BF16, "vkdp10"), fX(17664, 256, BF16, "vkdp11")]]
    GBs = [fX(17920, 512, BF16, "gb0"), fX(18432, 512, BF16, "gb1")]

    def loadblk(i, par):
        p.dma("sp", QBs[par][0], q_fm_s[i, :, :],
              reads=[B("q_fm_s")], writes=[QBs[par][1]])
        p.dma("sp", KBs[par][0], k_fm_s[i, :, :],
              reads=[B("k_fm_s")], writes=[KBs[par][1]])
        p.dma("sp", KTBs[par][0].rearrange("p (c n) -> p c n", c=2),
              k_tm_s.rearrange("m p n -> p m n")[:, 2 * i:2 * i + 2, :],
              reads=[B("k_tm_s")], writes=[KTBs[par][1]])
        p.dma("sp", VBs[par][0].rearrange("p (c n) -> p c n", c=2),
              v_tm_s.rearrange("m p n -> p m n")[:, 2 * i:2 * i + 2, :],
              reads=[B("v_tm_s")], writes=[VBs[par][1]])

    loadblk(0, 0)
    for i in range(NB):
        par = i % 2
        if i + 1 < NB:
            loadblk(i + 1, (i + 1) % 2)
        qb, qbb = QBs[par]
        kb, kbb = KBs[par]
        ktb, ktbb = KTBs[par]
        vb, vbb = VBs[par]
        ssbank = 2
        for h in range(NH):
            pbP = 5 if h % 2 == 0 else 3
            pbO = 6 if h % 2 == 0 else 0
            pbC = 4 if h % 2 == 0 else 1
            KVBF, KVBFb = KVBFs[h % 2]
            QQ, QQb = QQs[h % 2]
            STB, STBb = STBs[h % 2]
            VKDP = VKDPd[h % 2]
            p.comp("act", (lambda _h=h, _K=KVBF: lambda e: e.activation(
                _K, KVST[:, _h * 128:(_h + 1) * 128], AF.Copy, bias=0.0))(),
                reads=[KVSTb], writes=[KVBFb])
            p.comp("dve", (lambda _h=h, _q=qb, _Q=QQ: lambda e: e.tensor_mul(
                _Q, _q[:, _h * 256:(_h + 1) * 256],
                qdb[:, _h * 256:(_h + 1) * 256]))(),
                reads=[qbb, qdbb], writes=[QQb])
            p.mm_group(pbb[pbP], [
                ((lambda _h=h, _c=c, _k=kb, _q=qb, _p=pbP: lambda e: e.matmul(
                    PB[_p][:, _c * 256:(_c + 1) * 256],
                    _k[:, _h * 256 + _c * 128:_h * 256 + (_c + 1) * 128],
                    _q[:, _h * 256:(_h + 1) * 256],
                    start=True, stop=True))(), [kbb, qbb])
                for c in range(2)])
            p.comp("dve", (lambda _h=h, _S=STB, _p=pbP: lambda e: e.tensor_tensor(
                _S, PB[_p][:, 0:512], dtt[:, _h * 512:(_h + 1) * 512], OP.mult))(),
                reads=[pbb[pbP], dttb], writes=[STBb])
            p.mm_group(pbb[pbO], [
                ((lambda _h=h, _v=vb, _S=STB, _p=pbO: lambda e: e.matmul(
                    PB[_p][:, 0:256], _v[:, _h * 128:(_h + 1) * 128],
                    _S[:, 0:256], start=True, stop=False))(), [vbb, STBb]),
                ((lambda _h=h, _v=vb, _S=STB, _p=pbO: lambda e: e.matmul(
                    PB[_p][:, 0:256], _v[:, 2048 + _h * 128:2048 + (_h + 1) * 128],
                    _S[:, 256:512], start=False, stop=False))(), [vbb, STBb]),
                ((lambda _K=KVBF, _Q=QQ, _p=pbO: lambda e: e.matmul(
                    PB[_p][:, 0:256], _K, _Q, start=False, stop=True))(),
                 [KVBFb, QQb]),
            ])
            p.comp("act", (lambda _h=h, _p=pbO: lambda e: e.activation(
                ATTS[:, _h * 256:(_h + 1) * 256], PB[_p][:, 0:256], AF.Copy,
                bias=0.0))(),
                reads=[pbb[pbO]], writes=[ATTSb])
            p.comp("act", (lambda _h=h: lambda e: e.activation(
                SQG[:, 0:256], ATTS[:, _h * 256:(_h + 1) * 256], AF.Square))(),
                reads=[ATTSb], writes=[SQGb])
            gbuf, gbufb = GBs[h % 2]
            p.dma("act", gbuf, g_s[h, :, 256 * i:256 * (i + 1)],
                  reads=[B("g_s")], writes=[gbufb])
            p.comp("pool", (lambda _h=h, _g=gbuf: lambda e: e.tensor_mul(
                ATTS[:, _h * 256:(_h + 1) * 256],
                ATTS[:, _h * 256:(_h + 1) * 256], _g))(),
                reads=[ATTSb, gbufb], writes=[ATTSb])
            p.mm_group(pbb[ssbank], [
                ((lambda _h=h, _b=ssbank: lambda e: e.matmul(
                    PB[_b][:, 0:256], ones_bf[:, :], SQG[:, 0:256],
                    start=(_h == 0), stop=(_h == NH - 1)))(), [B("ones"), SQGb])])
            for cc_ in range(2):
                p.comp("dve", (lambda _h=h, _c=cc_, _v=vb, _V=VKDP[cc_][0]:
                               lambda e: e.tensor_scalar_mul(
                    _V, _v[:, _c * 2048 + _h * 128:_c * 2048 + (_h + 1) * 128],
                    kdc[:, 2 * _h + _c:2 * _h + _c + 1]))(),
                    reads=[vbb, B("kdc")], writes=[VKDP[cc_][1]])
            p.mm_group(pbb[pbC], [
                ((lambda _h=h, _c=c, _k=ktb, _V=VKDP[c][0], _p=pbC: lambda e: e.matmul(
                    PB[_p][:, 0:128],
                    _k[:, _c * 2048 + _h * 128:_c * 2048 + (_h + 1) * 128],
                    _V, start=(_c == 0), stop=(_c == 1)))(),
                 [ktbb, VKDP[c][1]])
                for c in range(2)])
            p.comp("dve", (lambda _h=h, _p=pbC: lambda e: e.scalar_tensor_tensor(
                KVST[:, _h * 128:(_h + 1) * 128], KVST[:, _h * 128:(_h + 1) * 128],
                lam[:, _h:_h + 1], PB[_p][:, 0:128], OP.mult, OP.add))(),
                reads=[KVSTb, B("lam"), pbb[pbC]], writes=[KVSTb])
        p.comp("dve", (lambda _i=i, _b=ssbank: lambda e: e.tensor_copy(
            ssrow[:, 256 * _i:256 * (_i + 1)], PB[_b][0:1, 0:256]))(),
            reads=[pbb[ssbank]], writes=[ROWSb])
        for cc_ in range(2):
            p.dma("sp", attn_s[2 * i + cc_, :, :].rearrange("p (h t) -> p h t", h=NH),
                  ATTS.rearrange("p (h c t) -> p c h t", h=NH, c=2)[:, cc_, :, :],
                  reads=[ATTSb], writes=[B("attn_s")])
    tap("attnc0", attn_s[0, :, :], [128, NH * 128], BF16, reads=[B("attn_s")])

    p.comp("dve", lambda e: e.tensor_scalar(rsrow[:, :], ssrow[:, :],
                                            float(EPS_LN / D),
                                            float(EPS_RMS * EPS_LN),
                                            OP.mult, OP.add),
           reads=[ROWSb], writes=[ROWSb])
    p.comp("dve", lambda e: e.tensor_copy(rsbf[:, :], rsrow[:, :]),
           reads=[ROWSb], writes=[RSBFb])
    p.mm_group(pbb[4], [
        ((lambda _m=m: lambda e: e.matmul(
            PB[4][:, _m:_m + 1], rsbf[:, _m * 128:(_m + 1) * 128],
            ones_bf[0:1, 0:1], start=True, stop=True))(), [RSBFb, B("ones")])
        for m in range(NT)])
    p.comp("dve", lambda e: e.tensor_copy(rstdc[:, :], PB[4][:, 0:16]),
           reads=[pbb[4]], writes=[B("rstdc")])
    tap("rstdc", rstdc[:, :], [128, 16], F32, reads=[B("rstdc")])

    if stage_limit < 5:
        return finish()

    OSC, OSCb = fX(0, 8192, F32, "osc")
    XADJ, XADJb = fX(8192, 8192, F32, "xadj")
    XP, XPb = fX(16384, 8192, F32, "xp")
    XT4, XT4b = fX(24576, 8192, F32, "xt4")
    ATC0, ATC0b = fX(40960, 4096, BF16, "atc0")
    ATC1, ATC1b = fX(45056, 4096, BF16, "atc1")
    ATC = [(ATC0, ATC0b), (ATC1, ATC1b)]
    GTC, GTCb = fX(49152, 4096, BF16, "gtc")
    LN2T, LN2Tb = fX(57344, 4096, BF16, "ln2t")

    ln2A, ln2Ab = fA(0, 65536, BF16, "ln2fm")
    lnfm2 = ln2A.rearrange("p (f t) -> p f t", f=16)

    ATC2, ATC2b = fX(49152, 4096, BF16, "atc2")
    ATC3, ATC3b = fX(53248, 4096, BF16, "atc3")
    ATC = [ATC[0], ATC[1], (ATC2, ATC2b), (ATC3, ATC3b)]
    LN2T1, LN2T1b = fX(32768, 4096, BF16, "ln2t1")
    LN2Ts = [(LN2T, LN2Tb), (LN2T1, LN2T1b)]

    def s4_loads(m):
        atc, atcb = ATC[m % 4]
        p.dma("sp", atc, attn_s[m, :, :], reads=[B("attn_s")], writes=[atcb])

    def s4_main(m):
        atc, atcb = ATC[m % 4]
        ln2t, ln2tb = LN2Ts[m % 2]
        p.dma("act", XT4, x_in[m * 128:(m + 1) * 128, :], writes=[XT4b])
        for s in range(4):
            mms = []
            for f in range(16):
                mms.append((
                    (lambda _f=f, _s=s, _a=atc: lambda e: e.matmul(
                        PB[_s][:, 0:512], _a[:, _f * 128:(_f + 1) * 128],
                        wo[:, _f, _s * 512:(_s + 1) * 512],
                        start=(_f == 0), stop=(_f == 15)))(),
                    [atcb, wob]))
            p.mm_group(pbb[s], mms)
        for s in range(4):
            p.comp("dve", (lambda _s=s: lambda e: e.tensor_copy(
                OSC[:, _s * 512:(_s + 1) * 512], PB[_s][:, 0:512]))(),
                reads=[pbb[s]], writes=[OSCb])
        cm, cr, bm, br = ln_stats(OSC, OSCb, XP, XPb, EPS_LN,
                                  eps_extra=(rstdc[:, m:m + 1], B("rstdc")))
        p.comp("dve", (lambda _m=cm, _r=cr: lambda e: e.scalar_tensor_tensor(
            colE[:, 0:1], _m, -1.0, _r, OP.mult, OP.mult))(),
            reads=[bm, br], writes=[B("cE0")])
        p.comp("dve", lambda e: e.tensor_scalar_add(XADJ, XT4, colE[:, 0:1]),
               reads=[XT4b, B("cE0")], writes=[XADJb])
        p.comp("dve", (lambda _r=cr: lambda e: e.scalar_tensor_tensor(
            XP, OSC, _r, XADJ, OP.mult, OP.add))(),
            reads=[OSCb, br, XADJb], writes=[XPb])
        p.dma("sp", x1_s[m * 128:(m + 1) * 128, :], XP, reads=[XPb],
              writes=[B("x1_s")])
        cm, cr, bm, br = ln_stats(XP, XPb, XADJ, XADJb, EPS_LN)
        p.comp("dve", (lambda _m=cm, _r=cr, _t=ln2t: lambda e: e.tensor_scalar(
            _t, XP, _m, _r, OP.subtract, OP.mult))(),
            reads=[XPb, bm, br], writes=[ln2tb])

    def s4_tail(m):
        ln2t, ln2tb = LN2Ts[m % 2]
        transpose_tile(ln2t, ln2tb, lnfm2, ln2Ab, m)
        mms = []
        for f in range(16):
            mms.append((
                (lambda _f=f, _m=m: lambda e: e.matmul(
                    PB[4][:, _m * 32:(_m + 1) * 32],
                    lnfm2[:, _f, _m * 128:(_m + 1) * 128],
                    gw_sb[:, _f * 32:(_f + 1) * 32],
                    start=(_f == 0), stop=(_f == 15)))(),
                [ln2Ab, B("gw_sb")]))
        p.mm_group(pbb[4], mms)

    s4_loads(0)
    s4_loads(1)
    s4_loads(2)
    for m in range(NT):
        if m + 3 < NT:
            s4_loads(m + 3)
        s4_main(m)
        if m >= 1:
            s4_tail(m - 1)
    s4_tail(NT - 1)
    tap("ln2fm", ln2A[:, 0:2048], [128, 2048], BF16, reads=[ln2Ab])
    tap("x1", x1_s[0:128, :], [128, D], F32, reads=[B("x1_s")])

    if stage_limit < 6:
        return finish()

    SC, SCb = fX(0, 2048, F32, "sc5")
    SB5, SB5b = fX(2048, 2048, F32, "sb5")
    WK1, WK1b = fX(4096, 2048, F32, "wk1")
    MK, MKb = fX(6144, 2048, F32, "mk5")
    CUR, CURb = fX(8192, 2048, F32, "cur5")
    T15, T15b = fX(10240, 2048, F32, "t15")
    M1, M1b = fX(12288, 256, F32, "m15")
    M2, M2b = fX(12544, 256, F32, "m25")
    GS, GSb = fX(12800, 256, F32, "gs5")
    G1, G1b = fX(13056, 64, F32, "g15")
    G2, G2b = fX(13120, 64, F32, "g25")
    TH, THb = fX(13184, 64, F32, "th5")
    SSUM, SSUMb = fX(13248, 64, F32, "ssum5")

    def v3(ap, a, b_):
        return ap.rearrange("p (a b) -> p a b", a=a, b=b_)

    def vcol(ap, a):
        return ap.rearrange("p (a o) -> p a o", a=a, o=1)

    p.comp("act", lambda e: e.activation(SC, PB[4][:, 0:512], AF.Sigmoid),
           reads=[pbb[4]], writes=[SCb])
    p.comp("dve", lambda e: e.tensor_tensor(SB5, SC, gb_rep[:, :], OP.add),
           reads=[SCb, B("gb_rep")], writes=[SB5b])
    p.comp("dve", lambda e: e.tensor_reduce(vcol(M1, 64), v3(SB5, 64, 8), AX.X, OP.max),
           reads=[SB5b], writes=[M1b])
    p.comp("dve", lambda e: e.tensor_tensor(
        v3(WK1, 64, 8), v3(SB5, 64, 8), vcol(M1, 64).broadcast_to((128, 64, 8)),
        OP.is_ge), reads=[SB5b, M1b], writes=[WK1b])
    p.comp("dve", lambda e: e.scalar_tensor_tensor(WK1, WK1, -BIGR, SB5, OP.mult, OP.add),
           reads=[WK1b, SB5b], writes=[WK1b])
    p.comp("dve", lambda e: e.tensor_reduce(vcol(M2, 64), v3(WK1, 64, 8), AX.X, OP.max),
           reads=[WK1b], writes=[M2b])
    p.comp("dve", lambda e: e.tensor_tensor(GS[:, 0:64], M1[:, 0:64], M2[:, 0:64], OP.add),
           reads=[M1b, M2b], writes=[GSb])
    p.comp("dve", lambda e: e.tensor_reduce(vcol(G1, 16), v3(GS[:, 0:64], 16, 4), AX.X, OP.max),
           reads=[GSb], writes=[G1b])
    p.comp("dve", lambda e: e.tensor_tensor(
        v3(M1[:, 0:64], 16, 4), v3(GS[:, 0:64], 16, 4),
        vcol(G1, 16).broadcast_to((128, 16, 4)), OP.is_ge),
        reads=[GSb, G1b], writes=[M1b])
    p.comp("dve", lambda e: e.scalar_tensor_tensor(
        M1[:, 0:64], M1[:, 0:64], -BIGR, GS[:, 0:64], OP.mult, OP.add),
        reads=[M1b, GSb], writes=[M1b])
    p.comp("dve", lambda e: e.tensor_reduce(vcol(G2, 16), v3(M1[:, 0:64], 16, 4), AX.X, OP.max),
           reads=[M1b], writes=[G2b])
    p.comp("dve", lambda e: e.tensor_tensor(
        v3(M2[:, 0:64], 16, 4), v3(GS[:, 0:64], 16, 4),
        vcol(G2, 16).broadcast_to((128, 16, 4)), OP.is_ge),
        reads=[GSb, G2b], writes=[M2b])
    p.comp("dve", lambda e: e.tensor_scalar(M2[:, 0:64], M2[:, 0:64], BIGR, -BIGR,
                                            OP.mult, OP.add),
           reads=[M2b], writes=[M2b])
    p.comp("dve", lambda e: e.tensor_tensor(
        v3(MK, 64, 8), v3(SB5, 64, 8), vcol(M2[:, 0:64], 64).broadcast_to((128, 64, 8)),
        OP.add), reads=[SB5b, M2b], writes=[MKb])
    p.comp("dve", lambda e: e.tensor_copy(CUR, MK), reads=[MKb], writes=[CURb])
    for t_ in range(8):
        p.comp("dve", lambda e: e.tensor_reduce(vcol(TH, 16), v3(CUR, 16, 32), AX.X, OP.max),
               reads=[CURb], writes=[THb])
        if t_ < 7:
            p.comp("dve", lambda e: e.tensor_tensor(
                v3(T15, 16, 32), v3(CUR, 16, 32), vcol(TH, 16).broadcast_to((128, 16, 32)),
                OP.is_ge), reads=[CURb, THb], writes=[T15b])
            p.comp("dve", lambda e: e.scalar_tensor_tensor(CUR, T15, -BIGR, CUR,
                                                           OP.mult, OP.add),
                   reads=[T15b, CURb], writes=[CURb])
    p.comp("dve", lambda e: e.tensor_tensor(
        v3(T15, 16, 32), v3(MK, 16, 32), vcol(TH, 16).broadcast_to((128, 16, 32)),
        OP.is_ge), reads=[MKb, THb], writes=[T15b])
    p.comp("dve", lambda e: e.tensor_tensor(T15, T15, SC, OP.mult),
           reads=[T15b, SCb], writes=[T15b])
    p.comp("dve", lambda e: e.tensor_reduce(vcol(SSUM, 16), v3(T15, 16, 32), AX.X, OP.add),
           reads=[T15b], writes=[SSUMb])
    p.comp("dve", lambda e: e.reciprocal(SSUM[:, 0:16], SSUM[:, 0:16]),
           reads=[SSUMb], writes=[SSUMb])
    p.comp("dve", lambda e: e.tensor_scalar_mul(SSUM[:, 0:16], SSUM[:, 0:16], 2.5),
           reads=[SSUMb], writes=[SSUMb])
    p.comp("dve", lambda e: e.tensor_tensor(
        v3(cw[:, :], 16, 32), v3(T15, 16, 32), vcol(SSUM[:, 0:16], 16).broadcast_to((128, 16, 32)),
        OP.mult), reads=[T15b, SSUMb], writes=[B("cw")])
    tap("cw", cw[:, :], [128, 512], F32, reads=[B("cw")])

    if stage_limit < 7:
        return finish()

    FC1ap, FC1b = fW(0, 32768, BF16, "fc1")
    FC1 = FC1ap.rearrange("p (f l) -> p f l", f=16)
    for g in range(2):
        p.dma("pool", FC1[:, 8 * g:8 * (g + 1), :],
              fc1T.rearrange("(f p) l -> p f l", p=128)[:, 8 * g:8 * (g + 1), :],
              writes=[FC1b])
    XLTap, XLTb = fX(0, 32768, BF16, "xlt")
    XLT = XLTap.rearrange("p (l t) -> p l t", l=8)
    for lo in range(8):
        for s in range(4):
            bank = (2 * lo + s) % 4
            mms = []
            for f in range(16):
                mms.append((
                    (lambda _f=f, _lo=lo, _s=s, _bk=bank: lambda e: e.matmul(
                        PB[_bk][:, 0:512], FC1[:, _f, _lo * 128:(_lo + 1) * 128],
                        lnfm2[:, _f, _s * 512:(_s + 1) * 512],
                        start=(_f == 0), stop=(_f == 15)))(),
                    [FC1b, ln2Ab]))
            p.mm_group(pbb[bank], mms)
            p.comp("dve", (lambda _lo=lo, _s=s, _bk=bank: lambda e: e.tensor_copy(
                XLT[:, _lo, _s * 512:(_s + 1) * 512], PB[_bk][:, 0:512]))(),
                reads=[pbb[bank]], writes=[XLTb])
    tap("xlat", XLTap[:, 0:2048], [128, 2048], BF16, reads=[XLTb])

    SUPs = [fW(32768, 16384, BF16, "supb0"), fW(49152, 16384, BF16, "supb1")]
    RSXs = [fX(57344, 2048, F32, "rsx0"), fX(59392, 2048, F32, "rsx1")]
    MSTs = [fX(32768, 4096, BF16, "mst0"), fX(36864, 4096, BF16, "mst1")]
    for j in range(8):
        sj, sjb = SUPs[j % 2]
        sjv = sj.rearrange("p (f n) -> p f n", f=16)
        p.dma("pool", sjv[:, :, :],
              supT[:, 512 * j:512 * (j + 1)].rearrange("(f p) n -> p f n", p=128),
              writes=[sjb])
        for sub in range(4):
            mo = j * 4 + sub
            mstp, mstb = MSTs[mo % 2]
            for s in range(4):
                bank = (2 * sub + s) % 4
                mms = []
                for f in range(16):
                    mms.append((
                        (lambda _f=f, _sub=sub, _s=s, _bk=bank, _sj=sjv: lambda e: e.matmul(
                            PB[_bk][:, 0:512], _sj[:, _f, _sub * 128:(_sub + 1) * 128],
                            lnfm2[:, _f, _s * 512:(_s + 1) * 512],
                            start=(_f == 0), stop=(_f == 15)))(),
                        [sjb, ln2Ab]))
                p.mm_group(pbb[bank], mms)
                rs, rsb = RSXs[s % 2]
                p.comp("act", (lambda _bk=bank, _rs=rs: lambda e: e.activation(
                    _rs[:, 0:512], PB[_bk][:, 0:512], AF.Relu))(),
                    reads=[pbb[bank]], writes=[rsb])
                p.comp("act", (lambda _rs=rs, _mt=mstp, _s=s: lambda e: e.activation(
                    _mt[:, _s * 512:(_s + 1) * 512], _rs[:, 0:512], AF.Square))(),
                    reads=[rsb], writes=[mstb])
            p.dma("sp", mid_s[mo, :, :], mstp, reads=[mstb], writes=[B("mid_s")])
    tap("mid", mid_s[0, :, :], [128, T], BF16, reads=[B("mid_s")])

    if stage_limit < 8:
        return finish()

    W1ap, W1b_ = fW(0, 16384, BF16, "w1b")
    W1B = W1ap.rearrange("p (c i) -> p c i", c=8)
    W2ap, W2b_ = fW(16384, 16384, BF16, "w2b")
    W2B = W2ap.rearrange("p (c l) -> p c l", c=8)
    HSBap, HSBb = fW(32768, 32768, BF16, "hsb")
    HSB = HSBap.rearrange("p (c t) -> p c t", c=8)
    RELs = [fX(57344, 2048, F32, "rels0"), fX(59392, 2048, F32, "rels1")]
    YACCap, YACCb = fA(0, 65536, F32, "yacc")
    for e_ in range(NE):
        p.dma("pool", W1B[:, :, :], w1T[e_].rearrange("(c p) i -> p c i", p=128),
              writes=[W1b_])
        p.dma("pool", W2B[:, :, :], w2T[e_].rearrange("(c p) l -> p c l", p=128),
              writes=[W2b_])
        for ic in range(8):
            for s in range(4):
                bank = (2 * ic + s) % 4
                mms = []
                for lc in range(8):
                    mms.append((
                        (lambda _l=lc, _i=ic, _s=s, _bk=bank: lambda e: e.matmul(
                            PB[_bk][:, 0:512], W1B[:, _l, _i * 128:(_i + 1) * 128],
                            XLT[:, _l, _s * 512:(_s + 1) * 512],
                            start=(_l == 0), stop=(_l == 7)))(),
                        [W1b_, XLTb]))
                p.mm_group(pbb[bank], mms)
                rs, rsb = RELs[s % 2]
                p.comp("act", (lambda _bk=bank, _rs=rs: lambda e: e.activation(
                    _rs[:, 0:512], PB[_bk][:, 0:512], AF.Relu))(),
                    reads=[pbb[bank]], writes=[rsb])
                p.comp("act", (lambda _rs=rs, _i=ic, _s=s: lambda e: e.activation(
                    HSB[:, _i, _s * 512:(_s + 1) * 512], _rs[:, 0:512], AF.Square))(),
                    reads=[rsb], writes=[HSBb])
        for m in range(NT):
            for ls in range(2):
                bank = 4 + ((2 * m + ls) % 2)
                mms = []
                for ic in range(8):
                    mms.append((
                        (lambda _i=ic, _m=m, _ls=ls, _bk=bank: lambda e: e.matmul(
                            PB[_bk][:, 0:512], HSB[:, _i, _m * 128:(_m + 1) * 128],
                            W2B[:, _i, _ls * 512:(_ls + 1) * 512],
                            start=(_i == 0), stop=(_i == 7)))(),
                        [HSBb, W2b_]))
                p.mm_group(pbb[bank], mms)
                if e_ == 0:
                    p.comp("dve", (lambda _m=m, _ls=ls, _e=e_, _bk=bank:
                                   lambda e: e.tensor_scalar_mul(
                        YACCap[:, _m * 1024 + _ls * 512:_m * 1024 + (_ls + 1) * 512],
                        PB[_bk][:, 0:512], cw[:, _m * 32 + _e:_m * 32 + _e + 1]))(),
                        reads=[pbb[bank], B("cw")], writes=[YACCb])
                else:
                    p.comp("dve", (lambda _m=m, _ls=ls, _e=e_, _bk=bank:
                                   lambda e: e.scalar_tensor_tensor(
                        YACCap[:, _m * 1024 + _ls * 512:_m * 1024 + (_ls + 1) * 512],
                        PB[_bk][:, 0:512], cw[:, _m * 32 + _e:_m * 32 + _e + 1],
                        YACCap[:, _m * 1024 + _ls * 512:_m * 1024 + (_ls + 1) * 512],
                        OP.mult, OP.add))(),
                        reads=[pbb[bank], B("cw"), YACCb], writes=[YACCb])
    tap("yacc", YACCap[:, 0:2048], [128, 2048], F32, reads=[YACCb])

    if stage_limit < 9:
        return finish()

    YBFap, YBFb = fX(0, 32768, BF16, "ybf")
    for m in range(NT):
        p.comp("act", (lambda _m=m: lambda e: e.activation(
            YBFap[:, _m * 1024:(_m + 1) * 1024],
            YACCap[:, _m * 1024:(_m + 1) * 1024], AF.Copy, bias=0.0))(),
            reads=[YACCb], writes=[YBFb])
    YFMap, YFMb = fW(32768, 32768, BF16, "yfm")
    YFM = YFMap.rearrange("p (c t) -> p c t", c=8)
    for m in range(NT):
        mms = []
        for lc in range(8):
            mms.append((
                (lambda _m=m, _l=lc: lambda e: e.transpose(
                    PTR[:, _l * 128:(_l + 1) * 128],
                    YBFap[:, _m * 1024 + _l * 128:_m * 1024 + (_l + 1) * 128],
                    ident[:, :]))(),
                [YBFb, B("ident")]))
        p.mm_group(ptrb, mms)
        p.comp("dve", (lambda _m=m: lambda e: e.tensor_copy(
            YFM[:, :, _m * 128:(_m + 1) * 128],
            PTR[:, :].rearrange("p (l t) -> p l t", l=8)))(),
            reads=[ptrb], writes=[YFMb])
    FC2ap, FC2b = fW(0, 32768, BF16, "fc2")
    FC2 = FC2ap.rearrange("p (c o) -> p c o", c=8)
    for g in range(2):
        p.dma("pool", FC2[:, 4 * g:4 * (g + 1), :],
              fc2T.rearrange("(c p) o -> p c o", p=128)[:, 4 * g:4 * (g + 1), :],
              writes=[FC2b])
    RTSs = [fX(32768, 4096, BF16, "rts0"), fX(36864, 4096, BF16, "rts1")]
    for m in range(NT):
        rt, rtb = RTSs[m % 2]
        for s in range(4):
            bank = (2 * m + s) % 4
            mms = []
            for lc in range(8):
                mms.append((
                    (lambda _l=lc, _m=m, _s=s, _bk=bank: lambda e: e.matmul(
                        PB[_bk][:, 0:512], YFM[:, _l, _m * 128:(_m + 1) * 128],
                        FC2[:, _l, _s * 512:(_s + 1) * 512],
                        start=(_l == 0), stop=(_l == 7)))(),
                    [YFMb, FC2b]))
            p.mm_group(pbb[bank], mms)
            p.comp("act", (lambda _bk=bank, _rt=rt, _s=s: lambda e: e.activation(
                _rt[:, _s * 512:(_s + 1) * 512], PB[_bk][:, 0:512], AF.Copy,
                bias=0.0))(),
                reads=[pbb[bank]], writes=[rtb])
        p.dma("sp", rout_s[m, :, :], rt, reads=[rtb], writes=[B("rout_s")])
    tap("routed", rout_s[0, :, :], [128, D], BF16, reads=[B("rout_s")])

    SDAap, SDAb = fW(0, 65536, BF16, "sda")
    SDA = SDAap.rearrange("p (c o) -> p c o", c=16)
    SDBap, SDBb = fA(0, 65536, BF16, "sdb")
    SDB = SDBap.rearrange("p (c o) -> p c o", c=16)
    sdn4 = sdnT.rearrange("(c p) o -> p c o", p=128)
    for g in range(4):
        p.dma("pool", SDA[:, 4 * g:4 * (g + 1), :], sdn4[:, 4 * g:4 * (g + 1), :],
              writes=[SDAb])
    for g in range(4):
        p.dma("pool", SDB[:, 4 * g:4 * (g + 1), :], sdn4[:, 16 + 4 * g:16 + 4 * (g + 1), :],
              writes=[SDBb])

    if stage_limit < 10:
        return finish()

    MIDs = [fX(0, 8192, BF16, "midb0"), fX(8192, 8192, BF16, "midb1")]
    RTBs = [fX(16384, 4096, BF16, "rtb0"), fX(20480, 4096, BF16, "rtb1")]
    X1B, X1Bb = fX(24576, 8192, F32, "x1b")
    F1, F1b = fX(32768, 8192, F32, "f1")
    OUTs = [(lambda a_, b_: (a_, b_))(*fX(40960, 8192, F32, "outb0")),
            (lambda a_, b_: (a_, b_))(*fX(49152, 8192, F32, "outb1"))]

    def s8_loads(m):
        mb, mbb = MIDs[m % 2]
        mbv = mb.rearrange("p (c t) -> p c t", c=32)
        for g in range(4):
            p.dma("sp", mbv[:, 8 * g:8 * (g + 1), :],
                  mid_s.rearrange("c p t -> p c t")[:, 8 * g:8 * (g + 1),
                                                    m * 128:(m + 1) * 128],
                  reads=[B("mid_s")], writes=[mbb])
        rt, rtb = RTBs[m % 2]
        p.dma("sp", rt, rout_s[m, :, :], reads=[B("rout_s")], writes=[rtb])

    s8_loads(0)
    for m in range(NT):
        mb, mbb = MIDs[m % 2]
        mbv = mb.rearrange("p (c t) -> p c t", c=32)
        rt, rtb = RTBs[m % 2]
        if m + 1 < NT:
            s8_loads(m + 1)
        p.dma("sp", X1B, x1_s[m * 128:(m + 1) * 128, :], reads=[B("x1_s")],
              writes=[X1Bb])
        for s in range(4):
            bank = (2 * m + s) % 4
            mms = []
            for mo in range(32):
                sap = SDA if mo < 16 else SDB
                sbu = SDAb if mo < 16 else SDBb
                mms.append((
                    (lambda _mo=mo, _s=s, _bk=bank, _mv=mbv, _sa=sap: lambda e: e.matmul(
                        PB[_bk][:, 0:512], _mv[:, _mo, :],
                        _sa[:, _mo % 16, _s * 512:(_s + 1) * 512],
                        start=(_mo == 0), stop=(_mo == 31)))(),
                    [mbb, sbu]))
            p.mm_group(pbb[bank], mms)
            p.comp("dve", (lambda _s=s, _bk=bank, _rt=rt: lambda e: e.tensor_tensor(
                F1[:, _s * 512:(_s + 1) * 512], PB[_bk][:, 0:512],
                _rt[:, _s * 512:(_s + 1) * 512], OP.add))(),
                reads=[pbb[bank], rtb], writes=[F1b])
        ob, obb = OUTs[m % 2]
        sqo, sqob = OUTs[(m + 1) % 2]
        cm, cr, bm, br = ln_stats(F1, F1b, sqo, sqob, EPS_LN)
        p.comp("dve", (lambda _m=cm, _r=cr: lambda e: e.scalar_tensor_tensor(
            colE[:, 1:2], _m, -1.0, _r, OP.mult, OP.mult))(),
            reads=[bm, br], writes=[B("cE1")])
        p.comp("dve", lambda e: e.tensor_scalar_add(X1B, X1B, colE[:, 1:2]),
               reads=[X1Bb, B("cE1")], writes=[X1Bb])
        p.comp("dve", (lambda _r=cr, _ob=ob: lambda e: e.scalar_tensor_tensor(
            _ob, F1, _r, X1B, OP.mult, OP.add))(),
            reads=[F1b, br, X1Bb], writes=[obb])
        p.dma("sp", y_out[m * 128:(m + 1) * 128, :], ob, reads=[obb])

    return finish()


def _host_constants():
    arb = np.broadcast_to(np.arange(1, 257, dtype=np.float32)[None, :],
                          (128, 256)).copy()
    pp = np.arange(128, dtype=np.float32)
    kdm = np.stack([255.0 - pp, 255.0 - (128.0 + pp)], axis=1).astype(np.float32)
    mtt = np.empty((128, 512), np.float32)
    q = np.arange(256, dtype=np.float32)
    for c in range(2):
        k = c * 128 + np.arange(128, dtype=np.float32)
        d = q[None, :] - k[:, None]
        mtt[:, c * 256:(c + 1) * 256] = np.where(d >= 0, d, BIG)
    ident = np.eye(128, dtype=ml_dtypes.bfloat16)
    ones_bf = np.ones((128, 128), dtype=ml_dtypes.bfloat16)
    onesf = np.ones((1, 128), dtype=np.float32)
    return arb, kdm, mtt, ident, ones_bf, onesf


def make_in_maps(inputs):
    x = np.asarray(inputs["L0_attn_hidden_states"], dtype=np.float32)
    slope = np.asarray(inputs["L0_attn_slope_rate"], dtype=np.float32)
    wqkv = np.asarray(inputs["L0_attn_w_qkv"], dtype=np.float32)
    wg = np.asarray(inputs["L0_attn_w_gate"], dtype=np.float32)
    wo = np.asarray(inputs["L0_attn_w_out"], dtype=np.float32)
    nw = np.asarray(inputs["L0_attn_norm_w"], dtype=np.float32)
    gw = np.asarray(inputs["L0_ffn_gate_w"], dtype=np.float32)
    gb = np.asarray(inputs["L0_ffn_gate_bias"], dtype=np.float32)
    fc1 = np.asarray(inputs["L0_ffn_fc1_w"], dtype=np.float32)
    fc2 = np.asarray(inputs["L0_ffn_fc2_w"], dtype=np.float32)
    w1 = np.asarray(inputs["L0_ffn_w1"], dtype=np.float32)
    w2 = np.asarray(inputs["L0_ffn_w2"], dtype=np.float32)
    sup = np.asarray(inputs["L0_ffn_shared_up_w"], dtype=np.float32)
    sdn = np.asarray(inputs["L0_ffn_shared_down_w"], dtype=np.float32)

    arb, kdm, mtt, ident, ones_bf, onesf = _host_constants()
    common = dict(
        wqkvT=np.ascontiguousarray(wqkv.T),
        wgT=np.ascontiguousarray(wg.T),
        woT=np.ascontiguousarray(wo.T),
        fc1T=np.ascontiguousarray(fc1.T),
        fc2T=np.ascontiguousarray(fc2.T),
        supT=np.ascontiguousarray(sup.T),
        sdnT=np.ascontiguousarray(sdn.T),
        w1T=np.ascontiguousarray(w1.transpose(0, 2, 1)),
        w2T=np.ascontiguousarray(w2.transpose(0, 2, 1)),
        gwT=np.ascontiguousarray(gw.T),
        sl_row=np.ascontiguousarray(slope[None, :]),
        gb_row=np.ascontiguousarray(gb[None, :]),
        nw_pc=np.ascontiguousarray(nw.reshape(16, 128).T),
        arb=arb, kdm=kdm, mtt=mtt, ident=ident, ones_bf=ones_bf, onesf=onesf,
    )
    in_maps = []
    for c in range(8):
        b, pos = c // 4, c % 4
        dist = np.full((1, 8), BIG, np.float32)
        for i in range(8):
            qq = i - 4 * b
            if 0 <= qq < pos:
                dist[0, i] = pos - 1 - qq
        m = dict(common)
        m["x"] = np.ascontiguousarray(x[b, pos * T:(pos + 1) * T, :])
        m["dist"] = dist
        in_maps.append(m)
    return in_maps


def kernel(**inputs):
    nc = build_program()
    in_maps = make_in_maps(inputs)
    res = run_bass_kernel_spmd(nc, in_maps, core_ids=list(range(8)))
    out = np.empty((2, 8192, 2048), np.float32)
    for c in range(8):
        b, pos = c // 4, c % 4
        out[b, pos * T:(pos + 1) * T, :] = np.asarray(res.results[c]["y"])
    return out


# version 18 (final)
# speedup vs baseline: 1.0154x; 1.0068x over previous
import numpy as np
import ml_dtypes
from contextlib import ExitStack

import concourse.bass as bass
import concourse.mybir as mybir
from concourse.bass_utils import run_bass_kernel_spmd
from concourse.alu_op_type import AluOpType

F32 = mybir.dt.float32
BF16 = mybir.dt.bfloat16
AF = mybir.ActivationFunctionType
AX = mybir.AxisListType
OP = AluOpType

ENGINES = ("pe", "dve", "act", "pool", "sp")
NLANES = 8
QLANES = {q: tuple(f"q{q}{i}" for i in range(NLANES)) for q in ("sp", "act", "pool")}
SEMS = ("pe", "dve", "act", "pool", "cc") + QLANES["sp"] + QLANES["act"] + QLANES["pool"]


class Buf:
    __slots__ = ("name", "w", "r")

    def __init__(self, name, prior=()):
        self.name = name
        self.w = {}
        self.r = {}
        for b in prior:
            for k, v in b.dep_write().items():
                if self.w.get(k, 0) < v:
                    self.w[k] = v

    def dep_read(self):
        return dict(self.w)

    def dep_write(self):
        d = dict(self.w)
        for k, v in self.r.items():
            if d.get(k, 0) < v:
                d[k] = v
        return d

    def note_w(self, key, val):
        if self.w.get(key, 0) < val:
            self.w[key] = val

    def note_r(self, key, val):
        if self.r.get(key, 0) < val:
            self.r[key] = val


def _merge(dst, src):
    for k, v in src.items():
        if dst.get(k, 0) < v:
            dst[k] = v


class Prog:
    def __init__(self, nc):
        self.nc = nc
        self.ops = {e: [] for e in ENGINES}
        self.cnt = {k: 0 for k in SEMS}
        self.waited = {e: {} for e in ENGINES}
        self.lane_rr = {q: 0 for q in QLANES}

    def _deps(self, reads, writes, extra):
        d = {}
        for b in reads:
            _merge(d, b.dep_read())
        for b in writes:
            _merge(d, b.dep_write())
        for ev in extra:
            if ev is not None:
                _merge(d, {ev[0]: ev[1]})
        return d

    def _prune(self, eng, deps, self_sync=True):
        out = {}
        wt = self.waited[eng]
        for k, v in deps.items():
            if not self_sync and k == eng:
                continue
            if k == eng and eng == "pe":
                continue
            if wt.get(k, 0) >= v:
                continue
            out[k] = v
        for k, v in out.items():
            wt[k] = v
        return out

    def _rec(self, eng, waits, fn, inc_key, inc_amt):
        self.ops[eng].append((waits, fn, inc_key, inc_amt))

    def comp(self, eng, fn, reads=(), writes=(), extra=()):
        deps = self._deps(reads, writes, extra)
        waits = self._prune(eng, deps, True)
        self.cnt[eng] += 1
        ev = (eng, self.cnt[eng])
        self._rec(eng, waits, fn, eng, 1)
        for b in writes:
            b.note_w(eng, ev[1])
        for b in reads:
            b.note_r(eng, ev[1])
        return ev

    def dma(self, queue, out_ap, in_ap, reads=(), writes=(), extra=(), **kw):
        r = self.lane_rr[queue]
        self.lane_rr[queue] = (r + 1) % NLANES
        qk = QLANES[queue][r]
        deps = self._deps(reads, writes, extra)
        _merge(deps, {qk: self.cnt[qk]})
        waits = self._prune(queue, deps, True)
        self.cnt[qk] += 16
        val = self.cnt[qk]

        def fn(eng, _o=out_ap, _i=in_ap, _kw=dict(kw)):
            return eng.dma_start(out=_o, in_=_i, **_kw)

        self._rec(queue, waits, fn, qk, 16)
        for b in writes:
            b.note_w(qk, val)
        for b in reads:
            b.note_r(qk, val)
        return (qk, val)

    def mm_group(self, psum_buf, mms):
        n = len(mms)
        ev = None
        for i, (fn, reads) in enumerate(mms):
            deps = self._deps(reads, (psum_buf,) if i == 0 else (), ())
            waits = self._prune("pe", deps, True)
            if i == n - 1:
                self.cnt["pe"] += 1
                ev = ("pe", self.cnt["pe"])
                self._rec("pe", waits, fn, "pe", 1)
                psum_buf.note_w("pe", ev[1])
                for _f2, reads2 in mms:
                    for b in reads2:
                        b.note_r("pe", ev[1])
            else:
                self._rec("pe", waits, fn, None, 0)
        return ev

    def wait_only(self, eng, extra):
        deps = {}
        for e in extra:
            _merge(deps, {e[0]: e[1]})
        waits = self._prune(eng, deps, True)
        if waits:
            self._rec(eng, waits, None, None, 0)

    def drain_queues(self, eng="sp"):
        evs = [(lk, self.cnt[lk]) for lanes in QLANES.values() for lk in lanes
               if self.cnt[lk] > 0]
        self.wait_only(eng, evs)

    def verify(self):
        idx = {e: 0 for e in ENGINES}
        sem = {k: 0 for k in SEMS}
        total = sum(len(v) for v in self.ops.values())
        done = 0
        progressed = True
        while progressed:
            progressed = False
            for e in ENGINES:
                while idx[e] < len(self.ops[e]):
                    waits, fn, ik, ia = self.ops[e][idx[e]]
                    if any(sem[k] < v for k, v in waits.items()):
                        break
                    if ik is not None:
                        sem[ik] += ia
                    idx[e] += 1
                    done += 1
                    progressed = True
        if done != total:
            lines = []
            for e in ENGINES:
                if idx[e] < len(self.ops[e]):
                    waits = self.ops[e][idx[e]][0]
                    missing = {k: (v, sem[k]) for k, v in waits.items() if sem[k] < v}
                    lines.append(f"  {e} stuck at {idx[e]}/{len(self.ops[e])} on {missing}")
            raise RuntimeError("DEADLOCK in schedule:\n" + "\n".join(lines))

    def replay(self):
        nc = self.nc
        with ExitStack() as st:
            sems = {k: st.enter_context(nc.semaphore("sem_" + k)) for k in SEMS}
            block = st.enter_context(nc.Block())

            def mk(engname):
                def run(eng):
                    for waits, fn, ik, ia in self.ops[engname]:
                        for k in sorted(waits):
                            eng.wait_ge(sems[k], waits[k])
                        if fn is not None:
                            ins = fn(eng)
                            if ik is not None:
                                ins.then_inc(sems[ik], ia)
                return run

            block.tensor(mk("pe"))
            block.vector(mk("dve"))
            block.scalar(mk("act"))
            block.gpsimd(mk("pool"))
            block.sync(mk("sp"))


class Flex:

    def __init__(self):
        self.reg = {}

    def view(self, arena_handle, arena_name, byte0, nbytes, dtype, tag):
        es = 4 if dtype == F32 else 2
        assert byte0 % es == 0 and nbytes % es == 0
        ap = arena_handle[:, :]
        if dtype != ap.dtype:
            ap = ap.bitcast(dtype)
        ap = ap[:, byte0 // es:(byte0 + nbytes) // es]
        lst = self.reg.setdefault(arena_name, [])
        b1 = byte0 + nbytes
        priors = [bu for (a0, a1, bu) in lst if a0 < b1 and byte0 < a1]
        buf = Buf(tag, priors)
        self.reg[arena_name] = [(a0, a1, bu) for (a0, a1, bu) in lst
                                if not (a0 < b1 and byte0 < a1)]
        self.reg[arena_name].append((byte0, b1, buf))
        return ap, buf


T = 2048
D = 2048
NT = 16
NH = 16
NB = 8
NE = 32
LAT = 1024
SH = 4096
EPS_LN = np.float32(1e-5)
EPS_RMS = np.float32(1e-6)
BIG = 1.0e6
BIGR = 1.0e9


def build_program(stage_limit=99, taps=()):
    nc = bass.Bass("TRN2", target_bir_lowering=False, debug=False)
    p = Prog(nc)
    taps = set(taps)

    def di(name, shape, dt=F32):
        return nc.dram_tensor(name, shape, dt, kind="ExternalInput").ap()

    x_in = di("x", [T, D])
    wqkvT = di("wqkvT", [D, 3 * D])
    wgT = di("wgT", [D, D])
    woT = di("woT", [D, D])
    fc1T = di("fc1T", [D, LAT])
    fc2T = di("fc2T", [LAT, D])
    supT = di("supT", [D, SH])
    sdnT = di("sdnT", [SH, D])
    w1T = di("w1T", [NE, LAT, LAT])
    w2T = di("w2T", [NE, LAT, LAT])
    gwT = di("gwT", [D, NE])
    sl_row = di("sl_row", [1, NH])
    gb_row = di("gb_row", [1, NE])
    nw_pc = di("nw_pc", [128, 16])
    dist_in = di("dist", [1, 8])
    arb_in = di("arb", [128, 256])
    kdm_in = di("kdm", [128, 2])
    mtt_in = di("mtt", [128, 512])
    ident_in = di("ident", [128, 128], BF16)
    ones_bf_in = di("ones_bf", [128, 128], BF16)
    onesf_in = di("onesf", [1, 128])

    y_out = nc.dram_tensor("y", [T, D], F32, kind="ExternalOutput").ap()

    def ds(name, shape, dt):
        return nc.dram_tensor(name, shape, dt, kind="Internal").ap()

    q_fm_s = ds("q_fm_s", [NB, 128, NH * 256], BF16)
    k_fm_s = ds("k_fm_s", [NB, 128, NH * 256], BF16)
    k_tm_s = ds("k_tm_s", [NT, 128, D], BF16)
    v_tm_s = ds("v_tm_s", [NT, 128, D], BF16)
    g_s = ds("g_s", [NH, 128, T], BF16)
    attn_s = ds("attn_s", [NT, 128, NH * 128], BF16)
    S_loc = ds("S_loc", [NH * 128, 128], F32)
    S_gth = ds("S_gth", [8 * NH * 128, 128], F32)
    x1_s = ds("x1_s", [T, D], F32)
    mid_s = ds("mid_s", [32, 128, T], BF16)
    rout_s = ds("rout_s", [NT, 128, D], BF16)

    def sb(name, shape, dt):
        return nc.alloc_sbuf_tensor(name, shape, dt)

    ident = sb("identS", [128, 128], BF16)
    ones_bf = sb("ones_bfS", [128, 128], BF16)
    onesf = sb("onesfS", [1, 128], F32)
    arb = sb("arbS", [128, 256], F32)
    kdm = sb("kdmS", [128, 2], F32)
    mtt = sb("mttS", [128, 512], F32)
    slr = sb("slrS", [1, NH], F32)
    gbr = sb("gbrS", [1, NE], F32)
    dsr = sb("dsrS", [1, 8], F32)
    nwc = sb("nwcS", [128, 16], F32)
    s_bc = sb("s_bc", [128, NH], F32)
    negs = sb("negs", [128, NH], F32)
    lam = sb("lam", [128, NH], F32)
    distb = sb("distb", [128, 8], F32)
    wkv = sb("wkv", [128, 8 * NH], F32)
    gb_bc = sb("gb_bc", [128, NE], F32)
    gb_rep = sb("gb_rep", [128, 512], F32)
    kdc = sb("kdc", [128, NH * 2], F32)
    cw = sb("cw", [128, 512], F32)
    rstdc = sb("rstdc", [128, 16], F32)
    gw_sb = sb("gw_sbS", [128, 512], BF16)
    colA = sb("colA", [128, 8], F32)
    colB = sb("colB", [128, 8], F32)
    colC = sb("colC", [128, 8], F32)
    colD = sb("colD", [128, 8], F32)
    colE = sb("colE", [128, 8], F32)

    BIGA = sb("BIGA", [128, 32768], BF16)
    BIGW = sb("BIGW", [128, 32768], BF16)
    FLXT = sb("FLX", [128, 30720], BF16)

    PB = [nc.alloc_psum_tensor(f"pb{i}", [128, 512], F32) for i in range(7)]
    PTR = nc.alloc_psum_tensor("ptrT", [128, 1024], BF16)

    flex = Flex()

    def fA(byte0, nbytes, dtype, tag):
        return flex.view(BIGA, "A", byte0, nbytes, dtype, tag)

    def fW(byte0, nbytes, dtype, tag):
        return flex.view(BIGW, "W", byte0, nbytes, dtype, tag)

    def fX(byte0, nbytes, dtype, tag):
        return flex.view(FLXT, "X", byte0, nbytes, dtype, tag)

    bufs = {}

    def B(name):
        if name not in bufs:
            bufs[name] = Buf(name)
        return bufs[name]

    pbb = [B(f"pb{i}") for i in range(7)]
    ptrb = B("ptrT")

    dtt, dttb = flex.view(FLXT, "X", 36864, NH * 1024, BF16, "dtt")
    qdb, qdbb = flex.view(FLXT, "X", 53248, NH * 512, BF16, "qdb")

    def tap(name, src_ap, shape, dt=F32, reads=()):
        if name in taps:
            o = nc.dram_tensor("tap_" + name, shape, dt, kind="ExternalOutput").ap()
            p.dma("sp", o, src_ap, reads=reads)

    _statrot = [0]

    def stat_cols():
        i = _statrot[0] % 8
        _statrot[0] += 1
        return (colA[:, i:i + 1], colB[:, i:i + 1], colC[:, i:i + 1], colD[:, i:i + 1],
                (B(f"cA{i}"), B(f"cB{i}"), B(f"cC{i}"), B(f"cD{i}")))

    def ln_stats(x_ap, xbuf_b, sq_scr, sq_b, eps, eps_extra=None):
        n = x_ap.shape[1]
        c_ssq, c_sum, c_mean, c_r, (b_ssq, b_sum, b_mean, b_r) = stat_cols()
        p.comp("act", lambda e: e.activation(sq_scr, x_ap, AF.Square, accum_out=c_ssq),
               reads=[xbuf_b], writes=[sq_b, b_ssq])
        p.comp("dve", lambda e: e.tensor_reduce(
            c_sum.rearrange("p (a o) -> p a o", a=1, o=1),
            x_ap.rearrange("p (a b) -> p a b", a=1, b=n), AX.X, OP.add),
            reads=[xbuf_b], writes=[b_sum])
        p.comp("dve", lambda e: e.tensor_scalar_mul(c_mean, c_sum, 1.0 / n),
               reads=[b_sum], writes=[b_mean])
        p.comp("dve", lambda e: e.scalar_tensor_tensor(c_sum, c_mean, -1.0, c_mean,
                                                       OP.mult, OP.mult),
               reads=[b_mean], writes=[b_sum])
        p.comp("dve", lambda e: e.scalar_tensor_tensor(c_ssq, c_ssq, 1.0 / n, c_sum,
                                                       OP.mult, OP.add),
               reads=[b_ssq, b_sum], writes=[b_ssq])
        if eps_extra is not None:
            ea, eb = eps_extra
            p.comp("dve", lambda e: e.tensor_tensor(c_ssq, c_ssq, ea, OP.add),
                   reads=[b_ssq, eb], writes=[b_ssq])
            eps = np.float32(0.0)
        p.comp("act", lambda e: e.activation(c_r, c_ssq, AF.Ln, bias=eps),
               reads=[b_ssq], writes=[b_r])
        p.comp("act", lambda e: e.activation(c_r, c_r, AF.Exp, scale=-0.5),
               reads=[b_r], writes=[b_r])
        return c_mean, c_r, b_mean, b_r

    for t_, src, bn in [(ident, ident_in, "ident"), (ones_bf, ones_bf_in, "ones"),
                        (onesf, onesf_in, "onesf"), (arb, arb_in, "arb"),
                        (kdm, kdm_in, "kdm"), (mtt, mtt_in, "mtt"),
                        (slr, sl_row, "slr"), (gbr, gb_row, "gbr"),
                        (dsr, dist_in, "dsr"), (nwc, nw_pc, "nwc")]:
        p.dma("sp", t_[:, :], src, writes=[B(bn)])
    p.dma("pool", gw_sb[:, :].rearrange("p (f e) -> p f e", f=16),
          gwT.rearrange("(f p) e -> p f e", p=128), writes=[B("gw_sb")])

    p.mm_group(pbb[0], [
        (lambda e: e.matmul(PB[0][:, 0:NH], onesf[0:1, :], slr[0:1, :],
                            start=True, stop=True), [B("onesf"), B("slr")]),
        (lambda e: e.matmul(PB[0][:, 32:32 + NE], onesf[0:1, :], gbr[0:1, :],
                            start=True, stop=True), [B("onesf"), B("gbr")]),
        (lambda e: e.matmul(PB[0][:, 64:72], onesf[0:1, :], dsr[0:1, :],
                            start=True, stop=True), [B("onesf"), B("dsr")]),
    ])
    p.comp("dve", lambda e: e.tensor_copy(s_bc[:, :], PB[0][:, 0:NH]),
           reads=[pbb[0]], writes=[B("s_bc")])
    p.comp("dve", lambda e: e.tensor_copy(gb_bc[:, :], PB[0][:, 32:32 + NE]),
           reads=[pbb[0]], writes=[B("gb_bc")])
    p.comp("dve", lambda e: e.tensor_scalar_mul(distb[:, :], PB[0][:, 64:72], -2048.0),
           reads=[pbb[0]], writes=[B("distb")])
    p.comp("dve", lambda e: e.tensor_scalar_mul(negs[:, :], s_bc[:, :], -1.0),
           reads=[B("s_bc")], writes=[B("negs")])
    p.comp("act", lambda e: e.activation(lam[:, :], s_bc[:, :], AF.Exp, scale=-256.0),
           reads=[B("s_bc")], writes=[B("lam")])
    for i in range(8):
        p.comp("act", (lambda ii: lambda e: e.activation(
            wkv[:, ii * NH:(ii + 1) * NH], s_bc[:, :], AF.Exp,
            scale=distb[:, ii:ii + 1]))(i),
            reads=[B("s_bc"), B("distb")], writes=[B("wkv")])
    for j in range(16):
        p.comp("dve", (lambda jj: lambda e: e.tensor_copy(
            gb_rep[:, jj * NE:(jj + 1) * NE], gb_bc[:, :]))(j),
            reads=[B("gb_bc")], writes=[B("gb_rep")])
    for h in range(NH):
        p.comp("act", (lambda hh: lambda e: e.activation(
            dtt[:, hh * 512:(hh + 1) * 512], mtt[:, :], AF.Exp,
            scale=negs[:, hh:hh + 1]))(h),
            reads=[B("mtt"), B("negs")], writes=[dttb])
        p.comp("act", (lambda hh: lambda e: e.activation(
            qdb[:, hh * 256:(hh + 1) * 256], arb[:, :], AF.Exp,
            scale=negs[:, hh:hh + 1]))(h),
            reads=[B("arb"), B("negs")], writes=[qdbb])
        p.comp("act", (lambda hh: lambda e: e.activation(
            kdc[:, hh * 2:(hh + 1) * 2], kdm[:, :], AF.Exp,
            scale=negs[:, hh:hh + 1]))(h),
            reads=[B("kdm"), B("negs")], writes=[B("kdc")])
    tap("wkv", wkv[:, :], [128, 128], F32, reads=[B("wkv")])

    def finish():
        if stage_limit < 10:
            sink = sb("sinkS", [1, 64], F32)
            skb = B("sink")
            for ap in [x_in[0:1, 0:64], wqkvT[0:1, 0:64], wgT[0:1, 0:64],
                       woT[0:1, 0:64], fc1T[0:1, 0:64], fc2T[0:1, 0:64],
                       supT[0:1, 0:64], sdnT[0:1, 0:64], w1T[0, 0:1, 0:64],
                       w2T[0, 0:1, 0:64]]:
                p.dma("sp", sink[:, :], ap, reads=[skb], writes=[skb])
        p.drain_queues("sp")
        p.verify()
        p.replay()
        return nc

    if stage_limit < 1:
        return finish()

    lnA, lnAb = fA(0, 65536, BF16, "ln1fm")
    lnfm = lnA.rearrange("p (f t) -> p f t", f=16)
    XT0, XT0b = fX(0, 8192, F32, "xt0")
    XT1, XT1b = fX(8192, 8192, F32, "xt1")
    SQS, SQSb = fX(16384, 8192, F32, "sqs")
    LNT, LNTb = fX(24576, 4096, BF16, "lnt")
    XTl = [(XT0, XT0b), (XT1, XT1b)]

    def transpose_tile(src_ap, src_b, dst_view, dst_b, m, also_w=None):
        for half in range(2):
            mms = []
            for c in range(8):
                cc = half * 8 + c
                mms.append((
                    (lambda _c=cc, _o=c: lambda e: e.transpose(
                        PTR[:, _o * 128:(_o + 1) * 128],
                        src_ap[:, _c * 128:(_c + 1) * 128], ident[:, :]))(),
                    [src_b, B("ident")]))
            p.mm_group(ptrb, mms)
            ev = p.comp("dve", (lambda _m=m, _h=half: lambda e: e.tensor_copy(
                dst_view[:, _h * 8:(_h + 1) * 8, _m * 128:(_m + 1) * 128],
                PTR[:, :].rearrange("p (f t) -> p f t", f=8)))(),
                reads=[ptrb], writes=[dst_b])
            if also_w is not None:
                also_w.note_w(*ev)

    lnCh = [Buf(f"lnch{m}") for m in range(NT)]
    for m in range(NT):
        xb, xbb = XTl[m % 2]
        p.dma("sp", xb, x_in[m * 128:(m + 1) * 128, :], writes=[xbb])
        cm, cr, bm, br = ln_stats(xb, xbb, SQS, SQSb, EPS_LN)
        p.comp("dve", (lambda _x=xb, _m=cm, _r=cr: lambda e: e.tensor_scalar(
            LNT, _x, _m, _r, OP.subtract, OP.mult))(),
            reads=[xbb, bm, br], writes=[LNTb])
        transpose_tile(LNT, LNTb, lnfm, lnCh[m], m, also_w=lnAb)
    tap("ln1fm", lnA[:, 0:2048], [128, 2048], BF16, reads=[lnAb])

    if stage_limit < 2:
        return finish()

    wslice = [None, None, None, None]

    def load_wslice(src_ap, tag, s):
        wap, _junk = fW(0, 65536, BF16, tag + f"_s{s}") if False else (BIGW, None)
        prior = [wslice[s]] if wslice[s] is not None else []
        gb_ = Buf(tag + f"_s{s}", prior)
        wv_ = BIGW[:, :].rearrange("p (f n) -> p f n", f=16)
        p.dma("pool", wv_[:, :, 512 * s:512 * (s + 1)],
              src_ap[:, 512 * s:512 * (s + 1)].rearrange("(f p) n -> p f n", p=128),
              writes=[gb_])
        wslice[s] = gb_
        return gb_

    def load_wfull(src_ap, tag):
        wv_ = BIGW[:, :].rearrange("p (f n) -> p f n", f=16)
        sbufs = [load_wslice(src_ap, tag, s) for s in range(4)]
        return wv_, (None, sbufs)

    def wslice_into_flex(owner=None):
        if owner is None:
            owner = Buf("wslices_owner", [b for b in wslice if b is not None])
        flex.reg.setdefault("W", []).append((0, 65536, owner))

    def tm_pass(wv_, wbt_, dst_s, dst_db, chbufs, next_load=None):
        _junk, wgb_ = wbt_
        for s in range(4):
            for m in range(NT):
                vc, vcb = chbufs[m % 2]
                bank = (2 * (m % 2) + (s % 2)) % 4
                mms = []
                for f in range(16):
                    mms.append((
                        (lambda _f=f, _m=m, _s=s, _bk=bank: lambda e: e.matmul(
                            PB[_bk][:, 0:512],
                            lnfm[:, _f, _m * 128:(_m + 1) * 128],
                            wv_[:, _f, _s * 512:(_s + 1) * 512],
                            start=(_f == 0), stop=(_f == 15)))(),
                        [lnCh[m], wgb_[s]]))
                p.mm_group(pbb[bank], mms)
                p.comp("act", (lambda _s=s, _bk=bank, _v=vc: lambda e: e.activation(
                    _v[:, 0:512], PB[_bk][:, 0:512], AF.Silu))(),
                    reads=[pbb[bank]], writes=[vcb])
                p.dma("sp", dst_s[m, :, 512 * s:512 * (s + 1)], vc[:, 0:512],
                      reads=[vcb], writes=[dst_db])
            if next_load is not None:
                next_load(s)

    VCH0, VCH0b = fX(0, 2048, BF16, "vch0")
    VCH1, VCH1b = fX(2048, 2048, BF16, "vch1")
    wv, wvb = load_wfull(wqkvT[:, 2 * D:3 * D], "w_v")
    tm_pass(wv, wvb, v_tm_s, B("v_tm_s"), [(VCH0, VCH0b), (VCH1, VCH1b)],
            next_load=lambda s: load_wslice(wqkvT[:, D:2 * D], "w_k", s))

    wk = BIGW[:, :].rearrange("p (f n) -> p f n", f=16)
    wkb = (None, [wslice[s] for s in range(4)])
    KCH0, KCH0b = fX(0, 2048, BF16, "kch0")
    KCH1, KCH1b = fX(2048, 2048, BF16, "kch1")
    tm_pass(wk, wkb, k_tm_s, B("k_tm_s"), [(KCH0, KCH0b), (KCH1, KCH1b)],
            next_load=lambda s: load_wslice(wqkvT[:, D:2 * D], "w_k2", s))

    KRA = [fX(8192, 8192, BF16, "kra0")]
    VRA = [fX(16384, 8192, BF16, "vra0")]
    SALL, SALLb = fX(24576, 8192, F32, "sall")
    p.comp("dve", lambda e: e.memset(SALL, 0.0), writes=[SALLb])

    def passA_load(i):
        kra, krab = KRA[0]
        vra, vrab = VRA[0]
        p.dma("pool", kra.rearrange("p (c n) -> p c n", c=2),
              k_tm_s.rearrange("m p n -> p m n")[:, 2 * i:2 * i + 2, :],
              reads=[B("k_tm_s")], writes=[krab])
        p.dma("pool", vra.rearrange("p (c n) -> p c n", c=2),
              v_tm_s.rearrange("m p n -> p m n")[:, 2 * i:2 * i + 2, :],
              reads=[B("v_tm_s")], writes=[vrab])

    def passA_block(i):
        kra, krab = KRA[0]
        vra, vrab = VRA[0]
        for cc_ in range(2):
            for h in range(NH):
                p.comp("dve", (lambda _h=h, _c=cc_, _v=vra:
                               lambda e: e.tensor_scalar_mul(
                    _v[:, _c * 2048 + _h * 128:_c * 2048 + (_h + 1) * 128],
                    _v[:, _c * 2048 + _h * 128:_c * 2048 + (_h + 1) * 128],
                    kdc[:, 2 * _h + _c:2 * _h + _c + 1]))(),
                    reads=[vrab, B("kdc")], writes=[vrab])
        for h in range(NH):
            bank = 4 + (h % 2)
            mms = []
            for cc_ in range(2):
                mms.append((
                    (lambda _h=h, _c=cc_, _k=kra, _V=vra, _bk=bank:
                     lambda e: e.matmul(
                        PB[_bk][:, 0:128],
                        _k[:, _c * 2048 + _h * 128:_c * 2048 + (_h + 1) * 128],
                        _V[:, _c * 2048 + _h * 128:_c * 2048 + (_h + 1) * 128],
                        start=(_c == 0), stop=(_c == 1)))(),
                    [krab, vrab]))
            p.mm_group(pbb[bank], mms)
            p.comp("dve", (lambda _h=h, _bk=bank: lambda e: e.scalar_tensor_tensor(
                SALL[:, _h * 128:(_h + 1) * 128], SALL[:, _h * 128:(_h + 1) * 128],
                lam[:, _h:_h + 1], PB[_bk][:, 0:128], OP.mult, OP.add))(),
                reads=[SALLb, B("lam"), pbb[bank]], writes=[SALLb])

    STG0, STG0b = fX(0, 4096, BF16, "stg0")
    STG1, STG1b = fX(4096, 4096, BF16, "stg1")
    STG = [(STG0, STG0b), (STG1, STG1b)]

    def fm_pass(wv_, wbt_, dst_s, dst_db, actf, next_load=None, on_head_pre=None):
        _junk, wgb_ = wbt_
        for h in range(NH):
            if on_head_pre is not None:
                on_head_pre(h)
            st, stb = STG[h % 2]
            for s in range(4):
                bank = (2 * (h % 2) + (s % 2)) % 4
                mms = []
                for f in range(16):
                    mms.append((
                        (lambda _f=f, _h=h, _s=s, _bk=bank: lambda e: e.matmul(
                            PB[_bk][:, 0:512],
                            wv_[:, _f, _h * 128:(_h + 1) * 128],
                            lnfm[:, _f, _s * 512:(_s + 1) * 512],
                            start=(_f == 0), stop=(_f == 15)))(),
                        [lnCh[4 * s], lnCh[4 * s + 1], lnCh[4 * s + 2],
                         lnCh[4 * s + 3], wgb_[h // 4]]))
                p.mm_group(pbb[bank], mms)
                p.comp("act", (lambda _s=s, _bk=bank, _t=st, _a=actf: lambda e: e.activation(
                    _t[:, _s * 512:(_s + 1) * 512], PB[_bk][:, 0:512], _a))(),
                    reads=[pbb[bank]], writes=[stb])
            if dst_s.shape[0] == NB:
                p.dma("sp", dst_s[:, :, h * 256:(h + 1) * 256].rearrange("b p t -> p b t"),
                      st.rearrange("p (b t) -> p b t", b=NB),
                      reads=[stb], writes=[dst_db])
            else:
                p.dma("sp", dst_s[h, :, :], st, reads=[stb], writes=[dst_db])
            if next_load is not None and h % 4 == 3:
                next_load(h // 4)


    wk2 = BIGW[:, :].rearrange("p (f n) -> p f n", f=16)
    wk2b = (None, [wslice[s] for s in range(4)])
    def kfm_head_hook(h):
        if h % 2 == 0:
            i = h // 2
            passA_load(i)
            passA_block(i)

    fm_pass(wk2, wk2b, k_fm_s, B("k_fm_s"), AF.Silu,
            next_load=lambda s: load_wslice(wqkvT[:, 0:D], "w_q", s),
            on_head_pre=kfm_head_hook)

    p.dma("sp", S_loc.rearrange("(h d) e -> d h e", d=128),
          SALL.rearrange("p (h e) -> p h e", h=NH), reads=[SALLb], writes=[B("S_loc")])
    p.comp("pool", lambda e: e.collective_compute(
        "AllGather", OP.bypass, replica_groups=[list(range(8))],
        ins=[S_loc.opt()], outs=[S_gth.opt()]),
        reads=[B("S_loc")], writes=[B("S_gth")])
    tap("sall", SALL, [128, 2048], F32, reads=[SALLb])

    if stage_limit < 3:
        return finish()

    wq = BIGW[:, :].rearrange("p (f n) -> p f n", f=16)
    wqb = (None, [wslice[s] for s in range(4)])
    fm_pass(wq, wqb, q_fm_s, B("q_fm_s"), AF.Silu,
            next_load=lambda s: load_wslice(wgT, "w_g", s))
    wg = BIGW[:, :].rearrange("p (f n) -> p f n", f=16)
    wgb = (None, [wslice[s] for s in range(4)])
    fm_pass(wg, wgb, g_s, B("g_s"), AF.Sigmoid)

    wo = BIGW[:, :].rearrange("p (f n) -> p f n", f=16)
    for s in range(4):
        load_wslice(woT, "w_o", s)
    wob = Buf("w_o_all", [wslice[s] for s in range(4)])
    for f in range(16):
        p.comp("dve", (lambda _f=f: lambda e: e.tensor_scalar_mul(
            wo[:, _f, :], wo[:, _f, :], nwc[:, _f:_f + 1]))(),
            reads=[wob, B("nwc")], writes=[wob])
    wslice_into_flex(owner=wob)
    tap("gfm", g_s[0, :, :], [128, T], BF16, reads=[B("g_s")])

    if stage_limit < 4:
        return finish()

    KVST, KVSTb = fX(8192, 8192, F32, "kvst")
    KVG, KVGb = fX(16384, 4096, F32, "kvg")
    sg4 = S_gth.rearrange("(r c p) e -> p r c e", r=8, c=16)
    for h in range(NH):
        p.dma("sp", KVG.rearrange("p (r e) -> p r e", r=8), sg4[:, :, h, :],
              reads=[B("S_gth")], writes=[KVGb])
        p.comp("dve", (lambda _h=h: lambda e: e.tensor_scalar_mul(
            KVST[:, _h * 128:(_h + 1) * 128], KVG[:, 0:128],
            wkv[:, _h:_h + 1]))(),
            reads=[KVGb, B("wkv")], writes=[KVSTb])
        for i in range(1, 8):
            p.comp("dve", (lambda _h=h, _i=i: lambda e: e.scalar_tensor_tensor(
                KVST[:, _h * 128:(_h + 1) * 128], KVG[:, _i * 128:(_i + 1) * 128],
                wkv[:, _i * NH + _h:_i * NH + _h + 1],
                KVST[:, _h * 128:(_h + 1) * 128], OP.mult, OP.add))(),
                reads=[KVGb, B("wkv"), KVSTb], writes=[KVSTb])
    tap("kv0", KVST, [128, 2048], F32, reads=[KVSTb])

    ROWS, ROWSb = fX(24576, 8192, F32, "rows")
    ssrow = ROWS[0:1, :]
    rsrow = ROWS[32:33, :]
    RSBF, RSBFb = fX(32768, 4096, BF16, "rsbf")
    rsbf = RSBF[0:1, :]
    QBs = [fA(0, 8192, BF16, "qb0"), fA(8192, 8192, BF16, "qb1")]
    KBs = [fA(16384, 8192, BF16, "kb0"), fA(24576, 8192, BF16, "kb1")]
    KTBs = [fA(32768, 8192, BF16, "ktb0"), fA(40960, 8192, BF16, "ktb1")]
    VBs = [fA(49152, 8192, BF16, "vb0"), fA(57344, 8192, BF16, "vb1")]
    ATTS, ATTSb = fX(0, 8192, BF16, "atts")
    STBs = [fX(20480, 1024, BF16, "stb0"), fX(21504, 1024, BF16, "stb1")]
    QQs = [fX(22528, 512, BF16, "qq0"), fX(23040, 512, BF16, "qq1")]
    SQG, SQGb = fX(23552, 512, BF16, "sqg")
    KVBFs = [fX(16384, 256, BF16, "kvbf0"), fX(16640, 256, BF16, "kvbf1")]
    VKDPd = [[fX(16896, 256, BF16, "vkdp00"), fX(17152, 256, BF16, "vkdp01")],
             [fX(17408, 256, BF16, "vkdp10"), fX(17664, 256, BF16, "vkdp11")]]
    GBs = [fX(17920, 512, BF16, "gb0"), fX(18432, 512, BF16, "gb1")]

    def loadblk(i, par):
        p.dma("sp", QBs[par][0], q_fm_s[i, :, :],
              reads=[B("q_fm_s")], writes=[QBs[par][1]])
        p.dma("sp", KBs[par][0], k_fm_s[i, :, :],
              reads=[B("k_fm_s")], writes=[KBs[par][1]])
        p.dma("sp", KTBs[par][0].rearrange("p (c n) -> p c n", c=2),
              k_tm_s.rearrange("m p n -> p m n")[:, 2 * i:2 * i + 2, :],
              reads=[B("k_tm_s")], writes=[KTBs[par][1]])
        p.dma("sp", VBs[par][0].rearrange("p (c n) -> p c n", c=2),
              v_tm_s.rearrange("m p n -> p m n")[:, 2 * i:2 * i + 2, :],
              reads=[B("v_tm_s")], writes=[VBs[par][1]])

    loadblk(0, 0)
    for i in range(NB):
        par = i % 2
        if i + 1 < NB:
            loadblk(i + 1, (i + 1) % 2)
        qb, qbb = QBs[par]
        kb, kbb = KBs[par]
        ktb, ktbb = KTBs[par]
        vb, vbb = VBs[par]
        ssbank = 2
        for h in range(NH):
            pbP = 5 if h % 2 == 0 else 3
            pbO = 6 if h % 2 == 0 else 0
            pbC = 4 if h % 2 == 0 else 1
            KVBF, KVBFb = KVBFs[h % 2]
            QQ, QQb = QQs[h % 2]
            STB, STBb = STBs[h % 2]
            VKDP = VKDPd[h % 2]
            p.comp("act", (lambda _h=h, _K=KVBF: lambda e: e.activation(
                _K, KVST[:, _h * 128:(_h + 1) * 128], AF.Copy, bias=0.0))(),
                reads=[KVSTb], writes=[KVBFb])
            p.comp("dve", (lambda _h=h, _q=qb, _Q=QQ: lambda e: e.tensor_mul(
                _Q, _q[:, _h * 256:(_h + 1) * 256],
                qdb[:, _h * 256:(_h + 1) * 256]))(),
                reads=[qbb, qdbb], writes=[QQb])
            p.mm_group(pbb[pbP], [
                ((lambda _h=h, _c=c, _k=kb, _q=qb, _p=pbP: lambda e: e.matmul(
                    PB[_p][:, _c * 256:(_c + 1) * 256],
                    _k[:, _h * 256 + _c * 128:_h * 256 + (_c + 1) * 128],
                    _q[:, _h * 256:(_h + 1) * 256],
                    start=True, stop=True))(), [kbb, qbb])
                for c in range(2)])
            p.comp("dve", (lambda _h=h, _S=STB, _p=pbP: lambda e: e.tensor_tensor(
                _S, PB[_p][:, 0:512], dtt[:, _h * 512:(_h + 1) * 512], OP.mult))(),
                reads=[pbb[pbP], dttb], writes=[STBb])
            p.mm_group(pbb[pbO], [
                ((lambda _h=h, _v=vb, _S=STB, _p=pbO: lambda e: e.matmul(
                    PB[_p][:, 0:256], _v[:, _h * 128:(_h + 1) * 128],
                    _S[:, 0:256], start=True, stop=False))(), [vbb, STBb]),
                ((lambda _h=h, _v=vb, _S=STB, _p=pbO: lambda e: e.matmul(
                    PB[_p][:, 0:256], _v[:, 2048 + _h * 128:2048 + (_h + 1) * 128],
                    _S[:, 256:512], start=False, stop=False))(), [vbb, STBb]),
                ((lambda _K=KVBF, _Q=QQ, _p=pbO: lambda e: e.matmul(
                    PB[_p][:, 0:256], _K, _Q, start=False, stop=True))(),
                 [KVBFb, QQb]),
            ])
            p.comp("act", (lambda _h=h, _p=pbO: lambda e: e.activation(
                ATTS[:, _h * 256:(_h + 1) * 256], PB[_p][:, 0:256], AF.Copy,
                bias=0.0))(),
                reads=[pbb[pbO]], writes=[ATTSb])
            p.comp("act", (lambda _p=pbO: lambda e: e.activation(
                SQG[:, 0:256], PB[_p][:, 0:256], AF.Square))(),
                reads=[pbb[pbO]], writes=[SQGb])
            gbuf, gbufb = GBs[h % 2]
            p.dma("act", gbuf, g_s[h, :, 256 * i:256 * (i + 1)],
                  reads=[B("g_s")], writes=[gbufb])
            p.comp("pool", (lambda _h=h, _g=gbuf: lambda e: e.tensor_mul(
                ATTS[:, _h * 256:(_h + 1) * 256],
                ATTS[:, _h * 256:(_h + 1) * 256], _g))(),
                reads=[ATTSb, gbufb], writes=[ATTSb])
            p.mm_group(pbb[ssbank], [
                ((lambda _h=h, _b=ssbank: lambda e: e.matmul(
                    PB[_b][:, 0:256], ones_bf[:, :], SQG[:, 0:256],
                    start=(_h == 0), stop=(_h == NH - 1)))(), [B("ones"), SQGb])])
            for cc_ in range(2):
                p.comp("dve", (lambda _h=h, _c=cc_, _v=vb, _V=VKDP[cc_][0]:
                               lambda e: e.tensor_scalar_mul(
                    _V, _v[:, _c * 2048 + _h * 128:_c * 2048 + (_h + 1) * 128],
                    kdc[:, 2 * _h + _c:2 * _h + _c + 1]))(),
                    reads=[vbb, B("kdc")], writes=[VKDP[cc_][1]])
            p.mm_group(pbb[pbC], [
                ((lambda _h=h, _c=c, _k=ktb, _V=VKDP[c][0], _p=pbC: lambda e: e.matmul(
                    PB[_p][:, 0:128],
                    _k[:, _c * 2048 + _h * 128:_c * 2048 + (_h + 1) * 128],
                    _V, start=(_c == 0), stop=(_c == 1)))(),
                 [ktbb, VKDP[c][1]])
                for c in range(2)])
            p.comp("dve", (lambda _h=h, _p=pbC: lambda e: e.scalar_tensor_tensor(
                KVST[:, _h * 128:(_h + 1) * 128], KVST[:, _h * 128:(_h + 1) * 128],
                lam[:, _h:_h + 1], PB[_p][:, 0:128], OP.mult, OP.add))(),
                reads=[KVSTb, B("lam"), pbb[pbC]], writes=[KVSTb])
        p.comp("dve", (lambda _i=i, _b=ssbank: lambda e: e.tensor_copy(
            ssrow[:, 256 * _i:256 * (_i + 1)], PB[_b][0:1, 0:256]))(),
            reads=[pbb[ssbank]], writes=[ROWSb])
        for cc_ in range(2):
            p.dma("sp", attn_s[2 * i + cc_, :, :].rearrange("p (h t) -> p h t", h=NH),
                  ATTS.rearrange("p (h c t) -> p c h t", h=NH, c=2)[:, cc_, :, :],
                  reads=[ATTSb], writes=[B("attn_s")])
    tap("attnc0", attn_s[0, :, :], [128, NH * 128], BF16, reads=[B("attn_s")])

    p.comp("dve", lambda e: e.tensor_scalar(rsrow[:, :], ssrow[:, :],
                                            float(EPS_LN / D),
                                            float(EPS_RMS * EPS_LN),
                                            OP.mult, OP.add),
           reads=[ROWSb], writes=[ROWSb])
    p.comp("dve", lambda e: e.tensor_copy(rsbf[:, :], rsrow[:, :]),
           reads=[ROWSb], writes=[RSBFb])
    p.mm_group(pbb[4], [
        ((lambda _m=m: lambda e: e.matmul(
            PB[4][:, _m:_m + 1], rsbf[:, _m * 128:(_m + 1) * 128],
            ones_bf[0:1, 0:1], start=True, stop=True))(), [RSBFb, B("ones")])
        for m in range(NT)])
    p.comp("dve", lambda e: e.tensor_copy(rstdc[:, :], PB[4][:, 0:16]),
           reads=[pbb[4]], writes=[B("rstdc")])
    tap("rstdc", rstdc[:, :], [128, 16], F32, reads=[B("rstdc")])

    if stage_limit < 5:
        return finish()

    OSC, OSCb = fX(0, 8192, F32, "osc")
    XADJ, XADJb = fX(8192, 8192, F32, "xadj")
    XP, XPb = fX(16384, 8192, F32, "xp")
    XT4, XT4b = fX(24576, 8192, F32, "xt4")
    ATC0, ATC0b = fX(40960, 4096, BF16, "atc0")
    ATC1, ATC1b = fX(45056, 4096, BF16, "atc1")
    ATC = [(ATC0, ATC0b), (ATC1, ATC1b)]
    GTC, GTCb = fX(49152, 4096, BF16, "gtc")
    LN2T, LN2Tb = fX(57344, 4096, BF16, "ln2t")

    ln2A, ln2Ab = fA(0, 65536, BF16, "ln2fm")
    lnfm2 = ln2A.rearrange("p (f t) -> p f t", f=16)

    ATC2, ATC2b = fX(49152, 4096, BF16, "atc2")
    ATC3, ATC3b = fX(53248, 4096, BF16, "atc3")
    ATC = [ATC[0], ATC[1], (ATC2, ATC2b), (ATC3, ATC3b)]
    LN2T1, LN2T1b = fX(32768, 4096, BF16, "ln2t1")
    LN2Ts = [(LN2T, LN2Tb), (LN2T1, LN2T1b)]

    def s4_loads(m):
        atc, atcb = ATC[m % 4]
        p.dma("sp", atc, attn_s[m, :, :], reads=[B("attn_s")], writes=[atcb])

    def s4_main(m):
        atc, atcb = ATC[m % 4]
        ln2t, ln2tb = LN2Ts[m % 2]
        p.dma("sp", XT4, x_in[m * 128:(m + 1) * 128, :], writes=[XT4b])
        for s in range(4):
            mms = []
            for f in range(16):
                mms.append((
                    (lambda _f=f, _s=s, _a=atc: lambda e: e.matmul(
                        PB[_s][:, 0:512], _a[:, _f * 128:(_f + 1) * 128],
                        wo[:, _f, _s * 512:(_s + 1) * 512],
                        start=(_f == 0), stop=(_f == 15)))(),
                    [atcb, wob]))
            p.mm_group(pbb[s], mms)
        for s in range(4):
            p.comp("dve", (lambda _s=s: lambda e: e.tensor_copy(
                OSC[:, _s * 512:(_s + 1) * 512], PB[_s][:, 0:512]))(),
                reads=[pbb[s]], writes=[OSCb])
        cm, cr, bm, br = ln_stats(OSC, OSCb, XP, XPb, EPS_LN,
                                  eps_extra=(rstdc[:, m:m + 1], B("rstdc")))
        p.comp("dve", (lambda _m=cm, _r=cr: lambda e: e.scalar_tensor_tensor(
            colE[:, 0:1], _m, -1.0, _r, OP.mult, OP.mult))(),
            reads=[bm, br], writes=[B("cE0")])
        p.comp("dve", lambda e: e.tensor_scalar_add(XADJ, XT4, colE[:, 0:1]),
               reads=[XT4b, B("cE0")], writes=[XADJb])
        p.comp("dve", (lambda _r=cr: lambda e: e.scalar_tensor_tensor(
            XP, OSC, _r, XADJ, OP.mult, OP.add))(),
            reads=[OSCb, br, XADJb], writes=[XPb])
        p.dma("sp", x1_s[m * 128:(m + 1) * 128, :], XP, reads=[XPb],
              writes=[B("x1_s")])
        cm, cr, bm, br = ln_stats(XP, XPb, XADJ, XADJb, EPS_LN)
        p.comp("dve", (lambda _m=cm, _r=cr, _t=ln2t: lambda e: e.tensor_scalar(
            _t, XP, _m, _r, OP.subtract, OP.mult))(),
            reads=[XPb, bm, br], writes=[ln2tb])

    def s4_tail(m):
        ln2t, ln2tb = LN2Ts[m % 2]
        transpose_tile(ln2t, ln2tb, lnfm2, ln2Ab, m)
        mms = []
        for f in range(16):
            mms.append((
                (lambda _f=f, _m=m: lambda e: e.matmul(
                    PB[4][:, _m * 32:(_m + 1) * 32],
                    lnfm2[:, _f, _m * 128:(_m + 1) * 128],
                    gw_sb[:, _f * 32:(_f + 1) * 32],
                    start=(_f == 0), stop=(_f == 15)))(),
                [ln2Ab, B("gw_sb")]))
        p.mm_group(pbb[4], mms)

    s4_loads(0)
    s4_loads(1)
    s4_loads(2)
    for m in range(NT):
        if m + 3 < NT:
            s4_loads(m + 3)
        s4_main(m)
        if m >= 1:
            s4_tail(m - 1)
    s4_tail(NT - 1)
    tap("ln2fm", ln2A[:, 0:2048], [128, 2048], BF16, reads=[ln2Ab])
    tap("x1", x1_s[0:128, :], [128, D], F32, reads=[B("x1_s")])

    if stage_limit < 6:
        return finish()

    SC, SCb = fX(0, 2048, F32, "sc5")
    SB5, SB5b = fX(2048, 2048, F32, "sb5")
    WK1, WK1b = fX(4096, 2048, F32, "wk1")
    MK, MKb = fX(6144, 2048, F32, "mk5")
    CUR, CURb = fX(8192, 2048, F32, "cur5")
    T15, T15b = fX(10240, 2048, F32, "t15")
    M1, M1b = fX(12288, 256, F32, "m15")
    M2, M2b = fX(12544, 256, F32, "m25")
    GS, GSb = fX(12800, 256, F32, "gs5")
    G1, G1b = fX(13056, 64, F32, "g15")
    G2, G2b = fX(13120, 64, F32, "g25")
    TH, THb = fX(13184, 64, F32, "th5")
    SSUM, SSUMb = fX(13248, 64, F32, "ssum5")

    def v3(ap, a, b_):
        return ap.rearrange("p (a b) -> p a b", a=a, b=b_)

    def vcol(ap, a):
        return ap.rearrange("p (a o) -> p a o", a=a, o=1)

    p.comp("act", lambda e: e.activation(SC, PB[4][:, 0:512], AF.Sigmoid),
           reads=[pbb[4]], writes=[SCb])
    p.comp("dve", lambda e: e.tensor_tensor(SB5, SC, gb_rep[:, :], OP.add),
           reads=[SCb, B("gb_rep")], writes=[SB5b])
    p.comp("dve", lambda e: e.tensor_reduce(vcol(M1, 64), v3(SB5, 64, 8), AX.X, OP.max),
           reads=[SB5b], writes=[M1b])
    p.comp("dve", lambda e: e.tensor_tensor(
        v3(WK1, 64, 8), v3(SB5, 64, 8), vcol(M1, 64).broadcast_to((128, 64, 8)),
        OP.is_ge), reads=[SB5b, M1b], writes=[WK1b])
    p.comp("dve", lambda e: e.scalar_tensor_tensor(WK1, WK1, -BIGR, SB5, OP.mult, OP.add),
           reads=[WK1b, SB5b], writes=[WK1b])
    p.comp("dve", lambda e: e.tensor_reduce(vcol(M2, 64), v3(WK1, 64, 8), AX.X, OP.max),
           reads=[WK1b], writes=[M2b])
    p.comp("dve", lambda e: e.tensor_tensor(GS[:, 0:64], M1[:, 0:64], M2[:, 0:64], OP.add),
           reads=[M1b, M2b], writes=[GSb])
    p.comp("dve", lambda e: e.tensor_reduce(vcol(G1, 16), v3(GS[:, 0:64], 16, 4), AX.X, OP.max),
           reads=[GSb], writes=[G1b])
    p.comp("dve", lambda e: e.tensor_tensor(
        v3(M1[:, 0:64], 16, 4), v3(GS[:, 0:64], 16, 4),
        vcol(G1, 16).broadcast_to((128, 16, 4)), OP.is_ge),
        reads=[GSb, G1b], writes=[M1b])
    p.comp("dve", lambda e: e.scalar_tensor_tensor(
        M1[:, 0:64], M1[:, 0:64], -BIGR, GS[:, 0:64], OP.mult, OP.add),
        reads=[M1b, GSb], writes=[M1b])
    p.comp("dve", lambda e: e.tensor_reduce(vcol(G2, 16), v3(M1[:, 0:64], 16, 4), AX.X, OP.max),
           reads=[M1b], writes=[G2b])
    p.comp("dve", lambda e: e.tensor_tensor(
        v3(M2[:, 0:64], 16, 4), v3(GS[:, 0:64], 16, 4),
        vcol(G2, 16).broadcast_to((128, 16, 4)), OP.is_ge),
        reads=[GSb, G2b], writes=[M2b])
    p.comp("dve", lambda e: e.tensor_scalar(M2[:, 0:64], M2[:, 0:64], BIGR, -BIGR,
                                            OP.mult, OP.add),
           reads=[M2b], writes=[M2b])
    p.comp("dve", lambda e: e.tensor_tensor(
        v3(MK, 64, 8), v3(SB5, 64, 8), vcol(M2[:, 0:64], 64).broadcast_to((128, 64, 8)),
        OP.add), reads=[SB5b, M2b], writes=[MKb])
    p.comp("dve", lambda e: e.tensor_copy(CUR, MK), reads=[MKb], writes=[CURb])
    for t_ in range(8):
        p.comp("dve", lambda e: e.tensor_reduce(vcol(TH, 16), v3(CUR, 16, 32), AX.X, OP.max),
               reads=[CURb], writes=[THb])
        if t_ < 7:
            p.comp("dve", lambda e: e.tensor_tensor(
                v3(T15, 16, 32), v3(CUR, 16, 32), vcol(TH, 16).broadcast_to((128, 16, 32)),
                OP.is_ge), reads=[CURb, THb], writes=[T15b])
            p.comp("dve", lambda e: e.scalar_tensor_tensor(CUR, T15, -BIGR, CUR,
                                                           OP.mult, OP.add),
                   reads=[T15b, CURb], writes=[CURb])
    p.comp("dve", lambda e: e.tensor_tensor(
        v3(T15, 16, 32), v3(MK, 16, 32), vcol(TH, 16).broadcast_to((128, 16, 32)),
        OP.is_ge), reads=[MKb, THb], writes=[T15b])
    p.comp("dve", lambda e: e.tensor_tensor(T15, T15, SC, OP.mult),
           reads=[T15b, SCb], writes=[T15b])
    p.comp("dve", lambda e: e.tensor_reduce(vcol(SSUM, 16), v3(T15, 16, 32), AX.X, OP.add),
           reads=[T15b], writes=[SSUMb])
    p.comp("dve", lambda e: e.reciprocal(SSUM[:, 0:16], SSUM[:, 0:16]),
           reads=[SSUMb], writes=[SSUMb])
    p.comp("dve", lambda e: e.tensor_scalar_mul(SSUM[:, 0:16], SSUM[:, 0:16], 2.5),
           reads=[SSUMb], writes=[SSUMb])
    p.comp("dve", lambda e: e.tensor_tensor(
        v3(cw[:, :], 16, 32), v3(T15, 16, 32), vcol(SSUM[:, 0:16], 16).broadcast_to((128, 16, 32)),
        OP.mult), reads=[T15b, SSUMb], writes=[B("cw")])
    tap("cw", cw[:, :], [128, 512], F32, reads=[B("cw")])

    if stage_limit < 7:
        return finish()

    FC1ap, FC1b = fW(0, 32768, BF16, "fc1")
    FC1 = FC1ap.rearrange("p (f l) -> p f l", f=16)
    for g in range(2):
        p.dma("pool", FC1[:, 8 * g:8 * (g + 1), :],
              fc1T.rearrange("(f p) l -> p f l", p=128)[:, 8 * g:8 * (g + 1), :],
              writes=[FC1b])
    XLTap, XLTb = fX(0, 32768, BF16, "xlt")
    XLT = XLTap.rearrange("p (l t) -> p l t", l=8)
    for lo in range(8):
        for s in range(4):
            bank = (2 * lo + s) % 4
            mms = []
            for f in range(16):
                mms.append((
                    (lambda _f=f, _lo=lo, _s=s, _bk=bank: lambda e: e.matmul(
                        PB[_bk][:, 0:512], FC1[:, _f, _lo * 128:(_lo + 1) * 128],
                        lnfm2[:, _f, _s * 512:(_s + 1) * 512],
                        start=(_f == 0), stop=(_f == 15)))(),
                    [FC1b, ln2Ab]))
            p.mm_group(pbb[bank], mms)
            p.comp("dve", (lambda _lo=lo, _s=s, _bk=bank: lambda e: e.tensor_copy(
                XLT[:, _lo, _s * 512:(_s + 1) * 512], PB[_bk][:, 0:512]))(),
                reads=[pbb[bank]], writes=[XLTb])
    tap("xlat", XLTap[:, 0:2048], [128, 2048], BF16, reads=[XLTb])

    SUPs = [fW(32768, 16384, BF16, "supb0"), fW(49152, 16384, BF16, "supb1")]
    RSXs = [fX(57344, 2048, F32, "rsx0"), fX(59392, 2048, F32, "rsx1")]
    MSTs = [fX(32768, 4096, BF16, "mst0"), fX(36864, 4096, BF16, "mst1")]
    for j in range(8):
        sj, sjb = SUPs[j % 2]
        sjv = sj.rearrange("p (f n) -> p f n", f=16)
        p.dma("pool", sjv[:, :, :],
              supT[:, 512 * j:512 * (j + 1)].rearrange("(f p) n -> p f n", p=128),
              writes=[sjb])
        for sub in range(4):
            mo = j * 4 + sub
            mstp, mstb = MSTs[mo % 2]
            for s in range(4):
                bank = (2 * sub + s) % 4
                mms = []
                for f in range(16):
                    mms.append((
                        (lambda _f=f, _sub=sub, _s=s, _bk=bank, _sj=sjv: lambda e: e.matmul(
                            PB[_bk][:, 0:512], _sj[:, _f, _sub * 128:(_sub + 1) * 128],
                            lnfm2[:, _f, _s * 512:(_s + 1) * 512],
                            start=(_f == 0), stop=(_f == 15)))(),
                        [sjb, ln2Ab]))
                p.mm_group(pbb[bank], mms)
                rs, rsb = RSXs[s % 2]
                p.comp("act", (lambda _bk=bank, _rs=rs: lambda e: e.activation(
                    _rs[:, 0:512], PB[_bk][:, 0:512], AF.Relu))(),
                    reads=[pbb[bank]], writes=[rsb])
                p.comp("act", (lambda _rs=rs, _mt=mstp, _s=s: lambda e: e.activation(
                    _mt[:, _s * 512:(_s + 1) * 512], _rs[:, 0:512], AF.Square))(),
                    reads=[rsb], writes=[mstb])
            p.dma("sp", mid_s[mo, :, :], mstp, reads=[mstb], writes=[B("mid_s")])
    tap("mid", mid_s[0, :, :], [128, T], BF16, reads=[B("mid_s")])

    if stage_limit < 8:
        return finish()

    W1ap, W1b_ = fW(0, 16384, BF16, "w1b")
    W1B = W1ap.rearrange("p (c i) -> p c i", c=8)
    W2ap, W2b_ = fW(16384, 16384, BF16, "w2b")
    W2B = W2ap.rearrange("p (c l) -> p c l", c=8)
    HSBap, HSBb = fW(32768, 32768, BF16, "hsb")
    HSB = HSBap.rearrange("p (c t) -> p c t", c=8)
    RELs = [fX(57344, 2048, F32, "rels0"), fX(59392, 2048, F32, "rels1")]
    YACCap, YACCb = fA(0, 65536, F32, "yacc")
    for e_ in range(NE):
        p.dma("pool", W1B[:, :, :], w1T[e_].rearrange("(c p) i -> p c i", p=128),
              writes=[W1b_])
        p.dma("pool", W2B[:, :, :], w2T[e_].rearrange("(c p) l -> p c l", p=128),
              writes=[W2b_])
        for ic in range(8):
            for s in range(4):
                bank = (2 * ic + s) % 4
                mms = []
                for lc in range(8):
                    mms.append((
                        (lambda _l=lc, _i=ic, _s=s, _bk=bank: lambda e: e.matmul(
                            PB[_bk][:, 0:512], W1B[:, _l, _i * 128:(_i + 1) * 128],
                            XLT[:, _l, _s * 512:(_s + 1) * 512],
                            start=(_l == 0), stop=(_l == 7)))(),
                        [W1b_, XLTb]))
                p.mm_group(pbb[bank], mms)
                rs, rsb = RELs[s % 2]
                p.comp("act", (lambda _bk=bank, _rs=rs: lambda e: e.activation(
                    _rs[:, 0:512], PB[_bk][:, 0:512], AF.Relu))(),
                    reads=[pbb[bank]], writes=[rsb])
                p.comp("act", (lambda _rs=rs, _i=ic, _s=s: lambda e: e.activation(
                    HSB[:, _i, _s * 512:(_s + 1) * 512], _rs[:, 0:512], AF.Square))(),
                    reads=[rsb], writes=[HSBb])
        for m in range(NT):
            for ls in range(2):
                bank = 4 + ((2 * m + ls) % 2)
                mms = []
                for ic in range(8):
                    mms.append((
                        (lambda _i=ic, _m=m, _ls=ls, _bk=bank: lambda e: e.matmul(
                            PB[_bk][:, 0:512], HSB[:, _i, _m * 128:(_m + 1) * 128],
                            W2B[:, _i, _ls * 512:(_ls + 1) * 512],
                            start=(_i == 0), stop=(_i == 7)))(),
                        [HSBb, W2b_]))
                p.mm_group(pbb[bank], mms)
                if e_ == 0:
                    p.comp("dve", (lambda _m=m, _ls=ls, _e=e_, _bk=bank:
                                   lambda e: e.tensor_scalar_mul(
                        YACCap[:, _m * 1024 + _ls * 512:_m * 1024 + (_ls + 1) * 512],
                        PB[_bk][:, 0:512], cw[:, _m * 32 + _e:_m * 32 + _e + 1]))(),
                        reads=[pbb[bank], B("cw")], writes=[YACCb])
                else:
                    p.comp("dve", (lambda _m=m, _ls=ls, _e=e_, _bk=bank:
                                   lambda e: e.scalar_tensor_tensor(
                        YACCap[:, _m * 1024 + _ls * 512:_m * 1024 + (_ls + 1) * 512],
                        PB[_bk][:, 0:512], cw[:, _m * 32 + _e:_m * 32 + _e + 1],
                        YACCap[:, _m * 1024 + _ls * 512:_m * 1024 + (_ls + 1) * 512],
                        OP.mult, OP.add))(),
                        reads=[pbb[bank], B("cw"), YACCb], writes=[YACCb])
    tap("yacc", YACCap[:, 0:2048], [128, 2048], F32, reads=[YACCb])

    if stage_limit < 9:
        return finish()

    YBFap, YBFb = fX(0, 32768, BF16, "ybf")
    for m in range(NT):
        p.comp("act", (lambda _m=m: lambda e: e.activation(
            YBFap[:, _m * 1024:(_m + 1) * 1024],
            YACCap[:, _m * 1024:(_m + 1) * 1024], AF.Copy, bias=0.0))(),
            reads=[YACCb], writes=[YBFb])
    YFMap, YFMb = fW(32768, 32768, BF16, "yfm")
    YFM = YFMap.rearrange("p (c t) -> p c t", c=8)
    for m in range(NT):
        mms = []
        for lc in range(8):
            mms.append((
                (lambda _m=m, _l=lc: lambda e: e.transpose(
                    PTR[:, _l * 128:(_l + 1) * 128],
                    YBFap[:, _m * 1024 + _l * 128:_m * 1024 + (_l + 1) * 128],
                    ident[:, :]))(),
                [YBFb, B("ident")]))
        p.mm_group(ptrb, mms)
        p.comp("dve", (lambda _m=m: lambda e: e.tensor_copy(
            YFM[:, :, _m * 128:(_m + 1) * 128],
            PTR[:, :].rearrange("p (l t) -> p l t", l=8)))(),
            reads=[ptrb], writes=[YFMb])
    FC2ap, FC2b = fW(0, 32768, BF16, "fc2")
    FC2 = FC2ap.rearrange("p (c o) -> p c o", c=8)
    for g in range(2):
        p.dma("pool", FC2[:, 4 * g:4 * (g + 1), :],
              fc2T.rearrange("(c p) o -> p c o", p=128)[:, 4 * g:4 * (g + 1), :],
              writes=[FC2b])
    RTSs = [fX(32768, 4096, BF16, "rts0"), fX(36864, 4096, BF16, "rts1")]
    for m in range(NT):
        rt, rtb = RTSs[m % 2]
        for s in range(4):
            bank = (2 * m + s) % 4
            mms = []
            for lc in range(8):
                mms.append((
                    (lambda _l=lc, _m=m, _s=s, _bk=bank: lambda e: e.matmul(
                        PB[_bk][:, 0:512], YFM[:, _l, _m * 128:(_m + 1) * 128],
                        FC2[:, _l, _s * 512:(_s + 1) * 512],
                        start=(_l == 0), stop=(_l == 7)))(),
                    [YFMb, FC2b]))
            p.mm_group(pbb[bank], mms)
            p.comp("act", (lambda _bk=bank, _rt=rt, _s=s: lambda e: e.activation(
                _rt[:, _s * 512:(_s + 1) * 512], PB[_bk][:, 0:512], AF.Copy,
                bias=0.0))(),
                reads=[pbb[bank]], writes=[rtb])
        p.dma("sp", rout_s[m, :, :], rt, reads=[rtb], writes=[B("rout_s")])
    tap("routed", rout_s[0, :, :], [128, D], BF16, reads=[B("rout_s")])

    SDAap, SDAb = fW(0, 65536, BF16, "sda")
    SDA = SDAap.rearrange("p (c o) -> p c o", c=16)
    SDBap, SDBb = fA(0, 65536, BF16, "sdb")
    SDB = SDBap.rearrange("p (c o) -> p c o", c=16)
    sdn4 = sdnT.rearrange("(c p) o -> p c o", p=128)
    for g in range(4):
        p.dma("pool", SDA[:, 4 * g:4 * (g + 1), :], sdn4[:, 4 * g:4 * (g + 1), :],
              writes=[SDAb])
    for g in range(4):
        p.dma("pool", SDB[:, 4 * g:4 * (g + 1), :], sdn4[:, 16 + 4 * g:16 + 4 * (g + 1), :],
              writes=[SDBb])

    if stage_limit < 10:
        return finish()

    MIDs = [fX(0, 8192, BF16, "midb0"), fX(8192, 8192, BF16, "midb1")]
    RTBs = [fX(16384, 4096, BF16, "rtb0"), fX(20480, 4096, BF16, "rtb1")]
    X1B, X1Bb = fX(24576, 8192, F32, "x1b")
    F1, F1b = fX(32768, 8192, F32, "f1")
    OUTs = [(lambda a_, b_: (a_, b_))(*fX(40960, 8192, F32, "outb0")),
            (lambda a_, b_: (a_, b_))(*fX(49152, 8192, F32, "outb1"))]

    def s8_loads(m):
        mb, mbb = MIDs[m % 2]
        mbv = mb.rearrange("p (c t) -> p c t", c=32)
        for g in range(4):
            p.dma("sp", mbv[:, 8 * g:8 * (g + 1), :],
                  mid_s.rearrange("c p t -> p c t")[:, 8 * g:8 * (g + 1),
                                                    m * 128:(m + 1) * 128],
                  reads=[B("mid_s")], writes=[mbb])
        rt, rtb = RTBs[m % 2]
        p.dma("sp", rt, rout_s[m, :, :], reads=[B("rout_s")], writes=[rtb])

    s8_loads(0)
    for m in range(NT):
        mb, mbb = MIDs[m % 2]
        mbv = mb.rearrange("p (c t) -> p c t", c=32)
        rt, rtb = RTBs[m % 2]
        if m + 1 < NT:
            s8_loads(m + 1)
        p.dma("sp", X1B, x1_s[m * 128:(m + 1) * 128, :], reads=[B("x1_s")],
              writes=[X1Bb])
        for s in range(4):
            bank = (2 * m + s) % 4
            mms = []
            for mo in range(32):
                sap = SDA if mo < 16 else SDB
                sbu = SDAb if mo < 16 else SDBb
                mms.append((
                    (lambda _mo=mo, _s=s, _bk=bank, _mv=mbv, _sa=sap: lambda e: e.matmul(
                        PB[_bk][:, 0:512], _mv[:, _mo, :],
                        _sa[:, _mo % 16, _s * 512:(_s + 1) * 512],
                        start=(_mo == 0), stop=(_mo == 31)))(),
                    [mbb, sbu]))
            p.mm_group(pbb[bank], mms)
            p.comp("dve", (lambda _s=s, _bk=bank, _rt=rt: lambda e: e.tensor_tensor(
                F1[:, _s * 512:(_s + 1) * 512], PB[_bk][:, 0:512],
                _rt[:, _s * 512:(_s + 1) * 512], OP.add))(),
                reads=[pbb[bank], rtb], writes=[F1b])
        ob, obb = OUTs[m % 2]
        sqo, sqob = OUTs[(m + 1) % 2]
        cm, cr, bm, br = ln_stats(F1, F1b, sqo, sqob, EPS_LN)
        p.comp("dve", (lambda _m=cm, _r=cr: lambda e: e.scalar_tensor_tensor(
            colE[:, 1:2], _m, -1.0, _r, OP.mult, OP.mult))(),
            reads=[bm, br], writes=[B("cE1")])
        p.comp("dve", lambda e: e.tensor_scalar_add(X1B, X1B, colE[:, 1:2]),
               reads=[X1Bb, B("cE1")], writes=[X1Bb])
        p.comp("dve", (lambda _r=cr, _ob=ob: lambda e: e.scalar_tensor_tensor(
            _ob, F1, _r, X1B, OP.mult, OP.add))(),
            reads=[F1b, br, X1Bb], writes=[obb])
        p.dma("sp", y_out[m * 128:(m + 1) * 128, :], ob, reads=[obb])

    return finish()


def _host_constants():
    arb = np.broadcast_to(np.arange(1, 257, dtype=np.float32)[None, :],
                          (128, 256)).copy()
    pp = np.arange(128, dtype=np.float32)
    kdm = np.stack([255.0 - pp, 255.0 - (128.0 + pp)], axis=1).astype(np.float32)
    mtt = np.empty((128, 512), np.float32)
    q = np.arange(256, dtype=np.float32)
    for c in range(2):
        k = c * 128 + np.arange(128, dtype=np.float32)
        d = q[None, :] - k[:, None]
        mtt[:, c * 256:(c + 1) * 256] = np.where(d >= 0, d, BIG)
    ident = np.eye(128, dtype=ml_dtypes.bfloat16)
    ones_bf = np.ones((128, 128), dtype=ml_dtypes.bfloat16)
    onesf = np.ones((1, 128), dtype=np.float32)
    return arb, kdm, mtt, ident, ones_bf, onesf


def make_in_maps(inputs):
    x = np.asarray(inputs["L0_attn_hidden_states"], dtype=np.float32)
    slope = np.asarray(inputs["L0_attn_slope_rate"], dtype=np.float32)
    wqkv = np.asarray(inputs["L0_attn_w_qkv"], dtype=np.float32)
    wg = np.asarray(inputs["L0_attn_w_gate"], dtype=np.float32)
    wo = np.asarray(inputs["L0_attn_w_out"], dtype=np.float32)
    nw = np.asarray(inputs["L0_attn_norm_w"], dtype=np.float32)
    gw = np.asarray(inputs["L0_ffn_gate_w"], dtype=np.float32)
    gb = np.asarray(inputs["L0_ffn_gate_bias"], dtype=np.float32)
    fc1 = np.asarray(inputs["L0_ffn_fc1_w"], dtype=np.float32)
    fc2 = np.asarray(inputs["L0_ffn_fc2_w"], dtype=np.float32)
    w1 = np.asarray(inputs["L0_ffn_w1"], dtype=np.float32)
    w2 = np.asarray(inputs["L0_ffn_w2"], dtype=np.float32)
    sup = np.asarray(inputs["L0_ffn_shared_up_w"], dtype=np.float32)
    sdn = np.asarray(inputs["L0_ffn_shared_down_w"], dtype=np.float32)

    arb, kdm, mtt, ident, ones_bf, onesf = _host_constants()
    common = dict(
        wqkvT=np.ascontiguousarray(wqkv.T),
        wgT=np.ascontiguousarray(wg.T),
        woT=np.ascontiguousarray(wo.T),
        fc1T=np.ascontiguousarray(fc1.T),
        fc2T=np.ascontiguousarray(fc2.T),
        supT=np.ascontiguousarray(sup.T),
        sdnT=np.ascontiguousarray(sdn.T),
        w1T=np.ascontiguousarray(w1.transpose(0, 2, 1)),
        w2T=np.ascontiguousarray(w2.transpose(0, 2, 1)),
        gwT=np.ascontiguousarray(gw.T),
        sl_row=np.ascontiguousarray(slope[None, :]),
        gb_row=np.ascontiguousarray(gb[None, :]),
        nw_pc=np.ascontiguousarray(nw.reshape(16, 128).T),
        arb=arb, kdm=kdm, mtt=mtt, ident=ident, ones_bf=ones_bf, onesf=onesf,
    )
    in_maps = []
    for c in range(8):
        b, pos = c // 4, c % 4
        dist = np.full((1, 8), BIG, np.float32)
        for i in range(8):
            qq = i - 4 * b
            if 0 <= qq < pos:
                dist[0, i] = pos - 1 - qq
        m = dict(common)
        m["x"] = np.ascontiguousarray(x[b, pos * T:(pos + 1) * T, :])
        m["dist"] = dist
        in_maps.append(m)
    return in_maps


def kernel(**inputs):
    nc = build_program()
    in_maps = make_in_maps(inputs)
    res = run_bass_kernel_spmd(nc, in_maps, core_ids=list(range(8)))
    out = np.empty((2, 8192, 2048), np.float32)
    for c in range(8):
        b, pos = c // 4, c % 4
        out[b, pos * T:(pos + 1) * T, :] = np.asarray(res.results[c]["y"])
    return out
